# version 01
# baseline (speedup 1.0000x reference)
import contextlib
import math
import numpy as np
import ml_dtypes
import concourse.bass as bass
import concourse.mybir as mybir
from concourse.bass_utils import run_bass_kernel_spmd

F32 = mybir.dt.float32
BF16 = mybir.dt.bfloat16
I32 = mybir.dt.int32
AF = mybir.ActivationFunctionType
ALU = mybir.AluOpType
AX = mybir.AxisListType

PI = math.pi
INV2PI = 1.0 / (2.0 * math.pi)
RC1 = 6.28125
RC2 = 2.0 * math.pi - RC1
NT = 16
NH = 16
NE = 9
NIT = NH * NT

RIEMANN = (14.134725142, 21.022039639, 25.01085758, 30.424876126, 32.935061588,
        37.586178159, 40.918719012, 43.327073281, 48.005150881, 49.773832478,
        52.970321478, 56.446247697, 59.347044003, 60.831778525, 65.112544048,
        67.079810529, 69.546401711, 72.067157674, 75.704690699, 77.144840069,
        79.33737502, 82.910380854, 84.735492981, 87.425274613, 88.809111208,
        92.491899271, 94.651344041, 95.870634228, 98.831194218, 101.317851006,
        103.72553804, 105.446623052)


class Cnt:
    def __init__(self, h):
        self.h = h
        self.n = 0

    def inc(self, instr, k=1):
        instr.then_inc(self.h, k)
        self.n += k
        return instr

    def dinc(self, instr):
        return self.inc(instr, 16)

    def wait(self, eng, n=None):
        eng.wait_ge(self.h, self.n if n is None else n)


def nkc(i):
    return (i % NT) // 4 + 1


KCUM = [0]
for _i in range(NIT):
    KCUM.append(KCUM[-1] + nkc(_i))


def build_program(dbg=(), stop_after=99):
    nc = bass.Bass("TRN2", target_bir_lowering=False,
                   detect_race_conditions=False)

    x_in = nc.dram_tensor("x", [2048, 1024], F32, kind="ExternalInput").ap()
    wqkv_d = nc.dram_tensor("wqkv", [128, 8 * 3072], F32, kind="ExternalInput").ap()
    wo_d = nc.dram_tensor("wo", [128, 8 * 1024], F32, kind="ExternalInput").ap()
    ls_d = nc.dram_tensor("ls", [1, 16], F32, kind="ExternalInput").ap()
    rwt_d = nc.dram_tensor("rwt", [128, 64], F32, kind="ExternalInput").ap()
    ebias_d = nc.dram_tensor("ebias", [1, 8], F32, kind="ExternalInput").ap()
    w1_d = nc.dram_tensor("w1", [NE, 128, 8 * 2048], F32, kind="ExternalInput").ap()
    w3_d = nc.dram_tensor("w3", [NE, 128, 8 * 2048], F32, kind="ExternalInput").ap()
    w2_d = nc.dram_tensor("w2", [NE, 128, 16 * 1024], F32, kind="ExternalInput").ap()
    c_d = nc.dram_tensor("cvec", [8, 128], F32, kind="ExternalInput").ap()
    wmod_d = nc.dram_tensor("wmod", [128, 8 * 6144], F32, kind="ExternalInput").ap()
    ident_d = nc.dram_tensor("ident", [128, 128], F32, kind="ExternalInput").ap()
    posc_d = nc.dram_tensor("posc", [128, 16], F32, kind="ExternalInput").ap()
    ifr_d = nc.dram_tensor("ifr", [1, 512], F32, kind="ExternalInput").ap()
    masks_d = nc.dram_tensor("masks", [128, 2048], F32, kind="ExternalInput").ap()
    modscr_d = nc.dram_tensor("modscr", [48, 128], F32, kind="Internal").ap()
    condscr_d = nc.dram_tensor("condscr", [1, 6144], F32, kind="Internal").ap()
    y_out = nc.dram_tensor("y", [2048, 1024], F32, kind="ExternalOutput").ap()

    dbg_aps = {}
    for name, shape, dt in [
            ("cond", [1, 6144], F32), ("hT", [128, 16384], BF16),
            ("qT", [128, 16384], BF16), ("kT", [128, 16384], BF16),
            ("v", [128, 16384], BF16), ("otm", [128, 16384], BF16),
            ("oT", [128, 16384], BF16), ("x1", [128, 16384], BF16),
            ("h2T", [128, 16384], BF16), ("cw", [128, 128], F32)]:
        if name in dbg:
            dbg_aps[name] = nc.dram_tensor("dbg_" + name, shape, dt,
                                           kind="ExternalOutput").ap()

    gst = contextlib.ExitStack()
    with gst:
        sb = lambda n, s, d_: gst.enter_context(nc.sbuf_tensor(n, s, d_))
        sem = lambda n: Cnt(gst.enter_context(nc.semaphore(n)))

        identf = sb("identf", [128, 128], F32)
        identb = sb("identb", [128, 128], BF16)
        crow = sb("crow", [1, 512], F32)
        condt = sb("condt", [128, 3 * 1024], BF16)
        ebias_t = sb("ebias_t", [128, 8], F32)
        ifs_tile = sb("ifs_tile", [128, 512], F32)
        posc_sb = sb("posc_sb", [128, 16], F32)
        rwt_f = sb("rwt_f", [128, 64], F32)
        rwt_b = sb("rwt_b", [128, 64], BF16)
        ebias_r = sb("ebias_r", [1, 8], F32)
        cwbuf = sb("cwbuf", [128, 128], F32)
        dump = sb("dump", [128, 1024], BF16)
        scr = sb("scr", [128, 32], F32)
        ones_f = sb("ones_f", [1, 128], F32)
        epscol = sb("epscol", [128, 1], F32)
        junk = sb("junk", [1, 2], F32)
        buf1 = sb("buf1", [128, 16384], BF16)
        buf2 = sb("buf2", [128, 16384], BF16)
        out_t = sb("out_t", [128, 2 * 1024], F32)

        ph = sem("ph")
        dx = sem("dx")
        dwm = [sem("dwm0"), sem("dwm1")]
        fwm = [sem("fwm0"), sem("fwm1")]
        dwq = [sem("dwq0"), sem("dwq1")]
        fwq = [sem("fwq0"), sem("fwq1")]
        dwo = sem("dwo")
        dident = sem("dident")
        dmask = sem("dmask")
        sto2 = sem("sto2")
        dw1 = [sem("dw10"), sem("dw11")]
        fw1 = [sem("fw10"), sem("fw11")]
        dw3 = [sem("dw30"), sem("dw31")]
        fw3 = [sem("fw30"), sem("fw31")]
        dw2 = sem("dw2")
        fw2 = sem("fw2")
        ten = sem("ten")
        vec = sem("vec")
        sca = sem("sca")
        sca2 = sem("sca2")
        xld = sem("xld")
        xfree = sem("xfree")
        sto = sem("sto")
        sTS = sem("sTS")
        sVM = sem("sVM")
        sE = sem("sE")
        sV2S = sem("sV2S")
        sT2 = sem("sT2")
        sSE = sem("sSE")
        sTO = sem("sTO")
        sV2N = sem("sV2N")
        sG = sem("sG")
        sSi = sem("sSi")
        sYp = sem("sYp")
        sF = sem("sF")
        sAC = sem("sAC")

        block = gst.enter_context(nc.Block())
        phase_no = [0]

        def barrier():
            phase_no[0] += 1
            tgt = 5 * phase_no[0]

            @block.sync
            def _(e):
                sto.wait(e)
                sto2.wait(e)
                ph.inc(e.nop())
                e.wait_ge(ph.h, tgt)

            @block.gpsimd
            def _(e):
                ph.inc(e.engine_nop())
                e.wait_ge(ph.h, tgt)

            @block.tensor
            def _(e):
                ph.inc(e.ldweights(identb[:, 0:1]))
                e.wait_ge(ph.h, tgt)

            @block.vector
            def _(e):
                ph.inc(e.engine_nop())
                e.wait_ge(ph.h, tgt)

            @block.scalar
            def _(e):
                ph.inc(e.copy(junk[0:1, 0:1], junk[0:1, 1:2]))
                e.wait_ge(ph.h, tgt)

        ph0 = contextlib.ExitStack()
        with ph0:
            sb0 = lambda n, s, d_: ph0.enter_context(nc.sbuf_tensor(n, s, d_))
            c_sb = sb0("c_sb", [8, 128], F32)
            csil = sb0("csil", [8, 128], BF16)
            wmr = [sb0("wmr0", [128, 6144], BF16), sb0("wmr1", [128, 6144], BF16)]
            mod_sb = sb0("mod_sb", [128, 48], F32)
            modT = sb0("modT", [48, 128], F32)
            ls_sb = sb0("ls_sb", [1, 16], F32)
            els_sb = sb0("els_sb", [1, 16], F32)
            ifr_sb = sb0("ifr_sb", [1, 512], F32)
            sclrow = sb0("sclrow", [1, 512], F32)
            cT_bf = sb0("cT_bf", [128, 8], BF16)
            mod_ps = ph0.enter_context(nc.psum_tensor("mod_ps", [128, 48], F32))
            ct_ps = ph0.enter_context(nc.psum_tensor("ct_ps", [128, 8], BF16))
            mt_ps = ph0.enter_context(nc.psum_tensor("mt_ps", [48, 128], F32))
            bc_ps0 = ph0.enter_context(nc.psum_tensor("bc_ps0", [128, 512], F32))

            @block.sync
            def _(e):
                dx.dinc(e.dma_start(out=identf[:], in_=ident_d))
                dx.dinc(e.dma_start(out=c_sb[:], in_=c_d))
                dx.dinc(e.dma_start(out=ls_sb[:], in_=ls_d))
                dx.dinc(e.dma_start(out=posc_sb[:], in_=posc_d))
                dx.dinc(e.dma_start(out=ifr_sb[:], in_=ifr_d))
                dx.dinc(e.dma_start(out=ebias_r[:], in_=ebias_d))
                dx.dinc(e.dma_start(out=rwt_f[:], in_=rwt_d))

            @block.gpsimd
            def _(e):
                dident.dinc(e.dma_start(out=identb[:], in_=ident_d))
                for k in range(8):
                    s_ = k % 2
                    if k >= 2:
                        fwm[s_].wait(e, (k - 2) // 2 + 1)
                    dwm[s_].dinc(e.dma_start(
                        out=wmr[s_][:], in_=wmod_d[:, 6144 * k:6144 * (k + 1)]))

            @block.vector
            def _(e):
                e.memset(junk[:], 0.0)
                dx.wait(e)
                e.memset(ones_f[:], 1.0)
                e.memset(epscol[:], 1.0e-5)
                vec.inc(e.tensor_copy(rwt_b[:], rwt_f[:]))

            @block.scalar
            def _(e):
                vec.wait(e)
                e.activation(csil[:], c_sb[:], AF.Silu)
                sca.inc(e.activation(els_sb[:], ls_sb[:], AF.Exp))

            @block.tensor
            def _(e):
                sca.wait(e)
                dident.wait(e)
                ten.inc(e.transpose(ct_ps[:], csil[:], identb[0:8, 0:8]))

            @block.scalar
            def _(e):
                ten.wait(e)
                sca.inc(e.copy(cT_bf[:], ct_ps[:]))

            t_mv0, s_mv0, v_mv0 = ten.n, sca.n, vec.n

            @block.tensor
            def _(e):
                sca.wait(e)
                for k in range(8):
                    s_ = k % 2
                    dwm[s_].wait(e, 16 * (k // 2 + 1))
                    if k >= 1:
                        vec.wait(e, v_mv0 + k)
                    for m in range(48):
                        mm = e.matmul(mod_ps[:, m:m + 1],
                                      wmr[s_][:, 128 * m:128 * (m + 1)],
                                      cT_bf[:, k:k + 1],
                                      start=True, stop=True)
                    fwm[s_].inc(mm)

            @block.vector
            def _(e):
                for k in range(8):
                    e.wait_ge(fwm[k % 2].h, k // 2 + 1)
                    if k == 0:
                        vec.inc(e.tensor_copy(mod_sb[:], mod_ps[:]))
                    else:
                        vec.inc(e.tensor_tensor(mod_sb[:], mod_sb[:], mod_ps[:],
                                                op=ALU.add))
                    e.drain()

            @block.scalar
            def _(e):
                vec.wait(e)
                sca.inc(e.nop())

            @block.vector
            def _(e):
                sca.wait(e)
                e.tensor_scalar(mod_sb[:, 8:16], mod_sb[:, 8:16], 1.0, None,
                                op0=ALU.add)
                vec.inc(e.tensor_scalar(mod_sb[:, 32:40], mod_sb[:, 32:40], 1.0,
                                        None, op0=ALU.add))
                e.drain()

            @block.scalar
            def _(e):
                vec.wait(e)
                e.sign(scr[:, 0:8], mod_sb[:, 16:24])
                e.sign(scr[:, 8:16], mod_sb[:, 40:48])
                e.activation(scr[:, 16:24], mod_sb[:, 16:24], AF.Abs)
                sca.inc(e.activation(scr[:, 24:32], mod_sb[:, 40:48], AF.Abs))

            @block.vector
            def _(e):
                sca.wait(e)
                e.tensor_scalar(scr[:, 16:32], scr[:, 16:32], 0.25, None,
                                op0=ALU.max)
                e.drain()
                e.tensor_tensor(mod_sb[:, 16:24], scr[:, 0:8], scr[:, 16:24],
                                op=ALU.mult)
                e.tensor_tensor(mod_sb[:, 40:48], scr[:, 8:16], scr[:, 24:32],
                                op=ALU.mult)
                e.drain()
                vec.inc(e.tensor_tensor(
                    sclrow.ap().rearrange("p (h q) -> p h q", h=16),
                    ifr_sb.ap().rearrange("p (h q) -> p h q", h=16),
                    els_sb.ap().rearrange("p (h o) -> p h o", o=1).broadcast_to([1, 16, 32]),
                    op=ALU.mult))

            @block.tensor
            def _(e):
                vec.wait(e)
                ten.inc(e.transpose(mt_ps[:], mod_sb[:], identf[:]))

            @block.scalar
            def _(e):
                ten.wait(e)
                sca.inc(e.copy(modT[:], mt_ps[:]))

            @block.sync
            def _(e):
                sca.wait(e)
                dx.dinc(e.dma_start(
                    out=condscr_d.rearrange("o (x r) -> (o x) r", r=128),
                    in_=modT[:]))

            rows03 = [1, 0, 2]
            dxb0 = dx.n
            tb0, sb20 = ten.n, sca2.n

            @block.sync
            def _(e):
                dx.wait(e, dxb0)
                for i, row in enumerate(rows03):
                    for n in range(2):
                        j = 2 * i + n
                        if j >= 1:
                            ten.wait(e, tb0 + j)
                        dx.dinc(e.dma_start(
                            out=crow[:],
                            in_=condscr_d[:, 1024 * row + 512 * n:
                                          1024 * row + 512 * (n + 1)]))

            @block.tensor
            def _(e):
                for i, row in enumerate(rows03):
                    for n in range(2):
                        j = 2 * i + n
                        dx.wait(e, dxb0 + 16 * (j + 1))
                        if j >= 1:
                            sca2.wait(e, sb20 + j)
                        ten.inc(e.matmul(bc_ps0[:, 0:512], ones_f[:],
                                         crow[0:1, 0:512],
                                         start=True, stop=True))
                sca2.wait(e, sb20 + 6)
                ten.inc(e.matmul(bc_ps0[:, 0:8], ones_f[:], ebias_r[:],
                                 start=True, stop=True))
                sca2.wait(e, sb20 + 7)
                ten.inc(e.matmul(bc_ps0[:, 0:512], ones_f[:], sclrow[:],
                                 start=True, stop=True))

            @block.scalar
            def _(e):
                for i, row in enumerate(rows03):
                    slot = [0, 1, 2][i]
                    for n in range(2):
                        j = 2 * i + n
                        ten.wait(e, tb0 + j + 1)
                        sca2.inc(e.copy(condt[:, 1024 * slot + 512 * n:
                                              1024 * slot + 512 * (n + 1)],
                                        bc_ps0[:, 0:512]))
                ten.wait(e, tb0 + 7)
                sca2.inc(e.copy(ebias_t[:], bc_ps0[:, 0:8]))
                ten.wait(e, tb0 + 8)
                sca2.inc(e.copy(ifs_tile[:], bc_ps0[:, 0:512]))

            if "cond" in dbg:
                @block.sync
                def _(e):
                    vec.wait(e)
                    sca2.wait(e)
                    sto.dinc(e.dma_start(out=dbg_aps["cond"], in_=condscr_d))

        barrier()
        if stop_after < 1:
            @block.sync
            def _(e):
                sto.wait(e)
            return nc, dbg_aps

        def emit_ln(get_x, scale_off, shift_off, dstT, tr_ps, hpre, htile,
                    use_xfree):
            v0, s0, t0, s20 = vec.n, sca.n, ten.n, sca2.n

            @block.vector
            def _(v):
                for t in range(NT):
                    xs, wt = get_x(t)
                    wt(v)
                    if t >= 2:
                        sca.wait(v, s0 + 3 * (t - 2) + 3)
                    o = 8 * (t % 2)
                    v.tensor_reduce(scr[:, o + 0:o + 1], xs, axis=AX.X, op=ALU.add)
                    v.drain()
                    vec.inc(v.tensor_scalar(scr[:, o + 1:o + 2], scr[:, o + 0:o + 1],
                                            -1.0 / 1024.0, None, op0=ALU.mult))
                    sca.wait(v, s0 + 3 * t + 2)
                    vec.inc(v.scalar_tensor_tensor(
                        scr[:, o + 4:o + 5], scr[:, o + 2:o + 3], 1.0 / 1024.0,
                        scr[:, o + 3:o + 4], op0=ALU.mult, op1=ALU.subtract))
                    sca.wait(v, s0 + 3 * t + 3)
                    v.reciprocal(scr[:, o + 6:o + 7], scr[:, o + 5:o + 6])
                    v.drain()
                    hp = v.tensor_scalar(hpre[:], xs, scr[:, o + 1:o + 2],
                                         scr[:, o + 6:o + 7], op0=ALU.add,
                                         op1=ALU.mult)
                    if use_xfree:
                        xfree.inc(hp)
                    if t >= 2:
                        ten.wait(v, t0 + (t - 2) + 1)
                    hsl = htile[:, 1024 * (t % 2):1024 * (t % 2) + 1024]
                    v.tensor_tensor(hpre[:], hpre[:],
                                    condt[:, scale_off:scale_off + 1024], op=ALU.mult)
                    vec.inc(v.tensor_tensor(hsl, hpre[:],
                                            condt[:, shift_off:shift_off + 1024],
                                            op=ALU.add))

            def evac(s, t):
                ten.wait(s, t0 + t + 1)
                dst = dstT.ap().rearrange("p (j c) -> p j c", c=2048)[
                    :, :, 128 * t:128 * (t + 1)]
                srcp = tr_ps.ap().rearrange("p (j c) -> p j c", c=128)
                sca2.inc(s.copy(dst, srcp))

            @block.scalar
            def _(s):
                for t in range(NT):
                    xs, wt = get_x(t)
                    wt(s)
                    o = 8 * (t % 2)
                    sca.inc(s.activation(dump[:], xs, AF.Square,
                                         accum_out=scr[:, o + 2:o + 3]))
                    vec.wait(s, v0 + 3 * t + 1)
                    sca.inc(s.activation(scr[:, o + 3:o + 4], scr[:, o + 1:o + 2],
                                         AF.Square))
                    vec.wait(s, v0 + 3 * t + 2)
                    sca.inc(s.activation(scr[:, o + 5:o + 6], scr[:, o + 4:o + 5],
                                         AF.Sqrt, bias=epscol[:, 0:1]))
                    if t >= 1:
                        evac(s, t - 1)
                evac(s, NT - 1)

            @block.tensor
            def _(t_):
                for t in range(NT):
                    vec.wait(t_, v0 + 3 * t + 3)
                    if t >= 1:
                        sca2.wait(t_, s20 + t)
                    hsl = htile[:, 1024 * (t % 2):1024 * (t % 2) + 1024]
                    for j in range(8):
                        mm = t_.transpose(tr_ps[:, 128 * j:128 * (j + 1)],
                                          hsl[:, 128 * j:128 * (j + 1)], identb[:])
                    ten.inc(mm)

        phB = contextlib.ExitStack()
        with phB:
            sbB = lambda n, s, d_: phB.enter_context(nc.sbuf_tensor(n, s, d_))
            qT = sbB("qT", [128, 16384], BF16)
            kT = sbB("kT", [128, 16384], BF16)
            v_sb = sbB("v_sb", [128, 16384], BF16)

            ph1 = contextlib.ExitStack()
            with ph1:
                hpre = ph1.enter_context(nc.sbuf_tensor("hpre", [128, 1024], F32))
                htile = ph1.enter_context(nc.sbuf_tensor("htile", [128, 2048], BF16))
                xr = ph1.enter_context(nc.sbuf_tensor("xr", [128, 2048], F32))
                tr_ps1 = ph1.enter_context(nc.psum_tensor("tr_ps1", [128, 1024], BF16))
                xb = xld.n
                xfb = xfree.n

                @block.sync
                def _(e):
                    for t in range(NT):
                        if t >= 2:
                            xfree.wait(e, xfb + t - 1)
                        xld.dinc(e.dma_start(
                            out=xr[:, 1024 * (t % 2):1024 * (t % 2) + 1024],
                            in_=x_in[128 * t:128 * (t + 1), :]))

                def get_x1(t):
                    def wt(eng):
                        eng.wait_ge(xld.h, xb + 16 * (t + 1))
                    return xr[:, 1024 * (t % 2):1024 * (t % 2) + 1024], wt

                emit_ln(get_x1, 0, 1024, buf1, tr_ps1, hpre, htile, True)

                if "hT" in dbg:
                    @block.sync
                    def _(e):
                        sca2.wait(e)
                        sto.dinc(e.dma_start(out=dbg_aps["hT"], in_=buf1[:]))

            barrier()
            if stop_after < 2:
                @block.sync
                def _(e):
                    sto.wait(e)
                return nc, dbg_aps

            ph2 = contextlib.ExitStack()
            with ph2:
                sb2 = lambda n, s, d_: ph2.enter_context(nc.sbuf_tensor(n, s, d_))
                wqr = [sb2("wqr0", [128, 4096], BF16), sb2("wqr1", [128, 4096], BF16)]
                tha = sb2("tha", [128, 1024], F32)
                trig = tha
                thf = out_t.ap()[:, 0:1024]
                ta = out_t.ap()[:, 1024:1536]
                tb = out_t.ap()[:, 1536:2048]
                qrot = sb2("qrot", [128, 2 * 1024], BF16)
                qkv_ps = ph2.enter_context(nc.psum_tensor("qkv_ps", [128, 2048], F32))
                tr_ps2 = ph2.enter_context(nc.psum_tensor("tr_ps2", [128, 1024], BF16))

                t0, t20, v0, v20, s0, s20 = (ten.n, sTS.n, vec.n, sVM.n, sca.n,
                                             sca2.n)
                tE0 = sE.n

                @block.gpsimd
                def _(e):
                    for tau in range(3):
                        for p in range(2):
                            s_ = p
                            if tau >= 1:
                                fwq[s_].wait(e, tau)
                            piece = 2 * tau + p
                            src = wqkv_d.rearrange("p (k c) -> p k c", c=3072)[
                                :, :, 512 * piece:512 * (piece + 1)]
                            dwq[s_].dinc(e.dma_start(out=wqr[s_][:, 0:2048],
                                                     in_=src[:, 0:4, :]))
                            dwq[s_].dinc(e.dma_start(out=wqr[s_][:, 2048:4096],
                                                     in_=src[:, 4:8, :]))

                mmwait = {}

                @block.tensor
                def _(e):
                    def transposes(tau, t):
                        sTS.wait(e, t20 + 32 * tau + 2 * t + 2)
                        if 16 * tau + t >= 1:
                            sca2.wait(e, s20 + 16 * tau + t)
                        for j in range(8):
                            mm = e.transpose(
                                tr_ps2[:, 128 * j:128 * (j + 1)],
                                qrot[:, 1024 * (t % 2) + 128 * j:
                                     1024 * (t % 2) + 128 * (j + 1)],
                                identb[:])
                        sE.inc(mm)

                    for tau in range(3):
                        dwq[0].wait(e, 32 * (tau + 1))
                        dwq[1].wait(e, 32 * (tau + 1))
                        for t in range(NT):
                            for p in range(2):
                                it = 32 * tau + 2 * t + p
                                if it >= 4:
                                    pit = it - 4
                                    if pit < 64:
                                        sTS.wait(e, t20 + pit + 1)
                                    else:
                                        sca2.wait(e, s20 + 32 + (pit - 64) + 1)
                                sl = it % 4
                                for k in range(8):
                                    mm = e.matmul(
                                        qkv_ps[:, 512 * sl:512 * (sl + 1)],
                                        buf1[:, 2048 * k + 128 * t:
                                             2048 * k + 128 * (t + 1)],
                                        wqr[p][:, 512 * k:512 * (k + 1)],
                                        start=(k == 0), stop=(k == 7))
                                if t == NT - 1:
                                    fwq[p].inc(mm)
                                    mmwait[it] = (fwq[p].h, fwq[p].n)
                                else:
                                    ten.inc(mm)
                                    mmwait[it] = (ten.h, ten.n)
                            if tau < 2 and t >= 2:
                                transposes(tau, t - 2)
                        if tau < 2:
                            transposes(tau, NT - 2)
                            transposes(tau, NT - 1)

                @block.vector
                def _(e):
                    for tau in range(2):
                        for t in range(NT):
                            e.tensor_scalar(tha[:, 0:512], ifs_tile[:],
                                            posc_sb[:, t:t + 1], None, op0=ALU.mult)
                            e.tensor_scalar(tha[:, 512:1024], tha[:, 0:512],
                                            PI / 2.0, None, op0=ALU.add)
                            e.tensor_scalar(thf, tha[:], INV2PI, 12582912.0,
                                            op0=ALU.mult, op1=ALU.add)
                            e.tensor_scalar(thf, thf, -12582912.0, None,
                                            op0=ALU.add)
                            e.scalar_tensor_tensor(tha[:], thf, -RC1, tha[:],
                                                   op0=ALU.mult, op1=ALU.add)
                            e.scalar_tensor_tensor(tha[:], thf, -RC2,
                                                   tha[:], op0=ALU.mult,
                                                   op1=ALU.add)
                            vVM = e.tensor_scalar(tha[:], tha[:], 3.1415925,
                                                  -3.1415925, op0=ALU.min,
                                                  op1=ALU.max)
                            sVM.inc(vVM)
                            sca.wait(e, s0 + 16 * tau + t + 1)
                            if 16 * tau + t >= 2:
                                sE.wait(e, tE0 + 16 * tau + t - 2 + 1)
                            it0 = 32 * tau + 2 * t
                            e.wait_ge(*mmwait[it0])
                            e.wait_ge(*mmwait[it0 + 1])
                            a0 = (it0 % 4) // 2
                            qp = qkv_ps[:, 1024 * a0:1024 * (a0 + 1)]
                            x1v = qp[:, 0:1024:2]
                            x2v = qp[:, 1:1024:2]
                            cosh = trig[:, 512:1024]
                            sinh = trig[:, 0:512]
                            qsl = qrot.ap().rearrange(
                                "q (tt hh c) -> q tt hh c", tt=2, c=64)[
                                :, t % 2, :, :]
                            ta3 = ta.rearrange("q (hh c) -> q hh c", c=32)
                            tb3 = tb.rearrange("q (hh c) -> q hh c", c=32)
                            e.tensor_tensor(ta, x1v, cosh, op=ALU.mult)
                            e.tensor_tensor(tb, x2v, sinh, op=ALU.mult)
                            e.tensor_tensor(qsl[:, :, 0:32], ta3, tb3,
                                            op=ALU.subtract)
                            e.tensor_tensor(ta, x1v, sinh, op=ALU.mult)
                            e.tensor_tensor(tb, x2v, cosh, op=ALU.mult)
                            rx2 = e.tensor_tensor(qsl[:, :, 32:64], ta3, tb3,
                                                  op=ALU.add)
                            sTS.inc(rx2)
                            sTS.inc(e.engine_nop())

                @block.scalar
                def _(e):
                    def evac2(tau, t):
                        g = 16 * tau + t
                        sE.wait(e, tE0 + g + 1)
                        dT = qT if tau == 0 else kT
                        dst = dT.ap().rearrange("p (j c) -> p j c", c=2048)[
                            :, :, 128 * t:128 * (t + 1)]
                        srcp = tr_ps2.ap().rearrange("p (j c) -> p j c", c=128)
                        sca2.inc(e.copy(dst, srcp))

                    for tau in range(2):
                        for t in range(NT):
                            sVM.wait(e, v20 + 16 * tau + t + 1)
                            sca.inc(e.activation(trig[:], tha[:], AF.Sin))
                            if t >= 2:
                                evac2(tau, t - 2)
                        evac2(tau, NT - 2)
                        evac2(tau, NT - 1)
                    for t in range(NT):
                        for p in range(2):
                            it = 64 + 2 * t + p
                            e.wait_ge(*mmwait[it])
                            sl = it % 4
                            sca2.inc(e.copy(
                                v_sb[:, 1024 * t + 512 * p:1024 * t + 512 * (p + 1)],
                                qkv_ps[:, 512 * sl:512 * (sl + 1)]))

                for nm_ in ("qT", "kT", "v"):
                    if nm_ in dbg:
                        src_t = {"qT": qT, "kT": kT, "v": v_sb}[nm_]

                        @block.sync
                        def _(e, src_t=src_t, nm_=nm_):
                            sca2.wait(e)
                            sE.wait(e)
                            sto.dinc(e.dma_start(out=dbg_aps[nm_], in_=src_t[:]))

            barrier()
            if stop_after < 3:
                @block.sync
                def _(e):
                    sto.wait(e)
                return nc, dbg_aps

            ph3 = contextlib.ExitStack()
            with ph3:
                sb3 = lambda n, s, d_: ph3.enter_context(nc.sbuf_tensor(n, s, d_))
                masks_sb = sb3("masks_sb", [128, 2048], BF16)
                P_sb = sb3("P_sb", [128, 2048], BF16)
                PT_sb = sb3("PT_sb", [128, 2048], BF16)
                sums = sb3("sums", [128, 8], F32)
                wo_sb = sb3("wo_sb", [128, 4096], BF16)
                xr = sb3("xr3", [128, 512], F32)
                s_ps = ph3.enter_context(nc.psum_tensor("s_ps", [128, 2048], F32))
                pt_ps2 = [ph3.enter_context(nc.psum_tensor("pt_ps0", [128, 1024], BF16)),
                          ph3.enter_context(nc.psum_tensor("pt_ps1", [128, 1024], BF16))]
                o_ps2 = [ph3.enter_context(nc.psum_tensor("o_ps0", [128, 512], F32)),
                         ph3.enter_context(nc.psum_tensor("o_ps1", [128, 512], F32))]

                b = dict(TS=sTS.n, VM=sVM.n, E=sE.n, V2S=sV2S.n, T2=sT2.n,
                         SE=sSE.n, TO=sTO.n, V2N=sV2N.n)

                @block.gpsimd
                def _(e):
                    dmask.dinc(e.dma_start(out=masks_sb[:], in_=masks_d))

                def hq(i):
                    return i // NT, i % NT

                @block.tensor
                def _(e):
                    dmask.wait(e)
                    for j in range(NIT + 2):
                        if j < NIT:
                            h, qt = hq(j)
                            qg = qt // 4
                            if j >= 1:
                                sE.wait(e, b["E"] + j)
                            r0 = 64 * (h % 2)
                            cbase = 2048 * (h // 2)
                            for kc in range(qg + 1):
                                mm = e.matmul(
                                    s_ps[:, 512 * kc:512 * (kc + 1)],
                                    qT[r0:r0 + 64,
                                       cbase + 128 * qt:cbase + 128 * (qt + 1)],
                                    kT[r0:r0 + 64,
                                       cbase + 512 * kc:cbase + 512 * (kc + 1)],
                                    start=True, stop=True)
                            sTS.inc(mm)
                        if 0 <= j - 2 < NIT:
                            i = j - 2
                            h, qt = hq(i)
                            sSE.wait(e, b["SE"] + KCUM[i + 1])
                            if i >= 2:
                                sV2N.wait(e, b["V2N"] + i - 1)
                            osl = o_ps2[i % 2][:, 0:64]
                            nk4 = 4 * nkc(i)
                            ptb = PT_sb
                            for kap in range(nk4):
                                mm = e.matmul(
                                    osl, ptb[:, 128 * kap:128 * (kap + 1)],
                                    v_sb[:, 1024 * kap + 64 * h:
                                         1024 * kap + 64 * (h + 1)],
                                    start=(kap == 0), stop=(kap == nk4 - 1))
                            sTO.inc(mm)
                        if 0 <= j - 1 < NIT:
                            i = j - 1
                            h, qt = hq(i)
                            sE.wait(e, b["E"] + i + 1)
                            for kc in range(nkc(i)):
                                kcum = KCUM[i] + kc
                                if kcum >= 2:
                                    sSE.wait(e, b["SE"] + kcum - 1)
                                psl = P_sb
                                for jj in range(4):
                                    mm = e.transpose(
                                        pt_ps2[kcum % 2][:, 128 * jj:
                                                         128 * (jj + 1)],
                                        psl[:, 512 * kc + 128 * jj:
                                            512 * kc + 128 * (jj + 1)],
                                        identb[:])
                                sT2.inc(mm)

                @block.vector
                def _(e):
                    dmask.wait(e)
                    for j in range(NIT + 2):
                        if j < NIT:
                            h, qt = hq(j)
                            qg = qt // 4
                            sTS.wait(e, b["TS"] + j + 1)
                            e.tensor_tensor(s_ps[:, 512 * qg:512 * (qg + 1)],
                                            s_ps[:, 512 * qg:512 * (qg + 1)],
                                            masks_sb[:, 512 * (qt % 4):
                                                     512 * (qt % 4 + 1)],
                                            op=ALU.add)
                            o = 16 + 4 * (j % 2)
                            e.tensor_reduce(scr[:, o:o + 1],
                                            s_ps[:, 0:512 * (qg + 1)],
                                            axis=AX.X, op=ALU.max, negate=True)
                            e.drain()
                            vm = e.tensor_scalar(scr[:, 24 + (j % 2):25 + (j % 2)],
                                                 scr[:, o:o + 1], 0.125, None,
                                                 op0=ALU.mult)
                            sVM.inc(vm)
                        if 0 <= j - 1 < NIT:
                            i = j - 1
                            sE.wait(e, b["E"] + i + 1)
                            so = 26 + 3 * (i % 2)
                            e.tensor_reduce(scr[:, so + 1:so + 2],
                                            sums[:, 4 * (i % 2):
                                                 4 * (i % 2) + nkc(i)],
                                            axis=AX.X, op=ALU.add)
                            e.drain()
                            vs = e.reciprocal(scr[:, so:so + 1],
                                              scr[:, so + 1:so + 2])
                            sV2S.inc(vs)
                        if 0 <= j - 2 < NIT:
                            i = j - 2
                            h, qt = hq(i)
                            sTO.wait(e, b["TO"] + i + 1)
                            so = 26 + 3 * (i % 2)
                            osl = o_ps2[i % 2][:, 0:64]
                            vn = e.tensor_scalar(
                                buf2[:, 1024 * qt + 64 * h:1024 * qt + 64 * (h + 1)],
                                osl, scr[:, so:so + 1], None, op0=ALU.mult)
                            sV2N.inc(vn)

                @block.scalar
                def _(e):
                    for j in range(NIT + 1):
                        if 0 <= j - 1 < NIT:
                            i = j - 1
                            if i >= 1:
                                sTO.wait(e, b["TO"] + i)
                            for kc in range(nkc(i)):
                                kcum = KCUM[i] + kc
                                sT2.wait(e, b["T2"] + kcum + 1)
                                se2 = e.copy(
                                    PT_sb[:, 512 * kc:512 * (kc + 1)],
                                    pt_ps2[kcum % 2][:, 0:512])
                                sSE.inc(se2)
                        if j < NIT:
                            i = j
                            h, qt = hq(i)
                            sVM.wait(e, b["VM"] + i + 1)
                            if i >= 1:
                                sT2.wait(e, b["T2"] + KCUM[i])
                            psl = P_sb
                            for kc in range(nkc(i)):
                                se_ = e.activation(
                                    psl[:, 512 * kc:512 * (kc + 1)],
                                    s_ps[:, 512 * kc:512 * (kc + 1)], AF.Exp,
                                    bias=scr[:, 24 + (i % 2):25 + (i % 2)],
                                    scale=0.125,
                                    accum_out=sums[:, 4 * (i % 2) + kc:
                                                   4 * (i % 2) + kc + 1])
                            sE.inc(se_)

                if "otm" in dbg:
                    @block.sync
                    def _(e):
                        sV2N.wait(e)
                        sto.dinc(e.dma_start(out=dbg_aps["otm"], in_=buf2[:]))

                t0b, s20b = ten.n, sca2.n

                @block.tensor
                def _(e):
                    sV2N.wait(e)
                    for t in range(NT):
                        if t >= 1:
                            sca2.wait(e, s20b + t)
                        for j in range(8):
                            mm = e.transpose(
                                pt_ps2[0][:, 128 * j:128 * (j + 1)],
                                buf2[:, 1024 * t + 128 * j:1024 * t + 128 * (j + 1)],
                                identb[:])
                        ten.inc(mm)

                @block.scalar
                def _(e):
                    for t in range(NT):
                        ten.wait(e, t0b + t + 1)
                        dst = buf1.ap().rearrange("p (j c) -> p j c", c=2048)[
                            :, :, 128 * t:128 * (t + 1)]
                        srcp = pt_ps2[0].ap().rearrange("p (j c) -> p j c", c=128)
                        sca2.inc(e.copy(dst, srcp))

                if "oT" in dbg:
                    @block.sync
                    def _(e):
                        sca2.wait(e)
                        sto.dinc(e.dma_start(out=dbg_aps["oT"], in_=buf1[:]))

                xb2 = xld.n
                xfb2 = xfree.n

                @block.sync
                def _(e):
                    sca2.wait(e)
                    for n in range(2):
                        for t in range(NT):
                            m = 16 * n + t
                            if m >= 1:
                                xfree.wait(e, xfb2 + m)
                            xld.dinc(e.dma_start(
                                out=xr[:, 0:512],
                                in_=x_in[128 * t:128 * (t + 1),
                                         512 * n:512 * (n + 1)]))

                t0c, v0c = ten.n, vec.n
                wofree = fwm[0]

                @block.gpsimd
                def _(e):
                    srcw = wo_d.rearrange("p (k c) -> p k c", c=1024)
                    wof0 = wofree.n
                    for n in range(2):
                        if n == 1:
                            wofree.wait(e, wof0 + 1)
                        dwo.dinc(e.dma_start(out=wo_sb[:],
                                             in_=srcw[:, :, 512 * n:512 * (n + 1)]))

                wowait = {}

                @block.tensor
                def _(e):
                    sca2.wait(e)
                    for n in range(2):
                        e.wait_ge(dwo.h, 16 + 16 * n)
                        for t in range(NT):
                            m = 16 * n + t
                            if m >= 2:
                                vec.wait(e, v0c + (m - 2) + 1)
                            for k in range(8):
                                mm = e.matmul(
                                    s_ps[:, 512 * (m % 2):512 * (m % 2) + 512],
                                    buf1[:, 2048 * k + 128 * t:
                                         2048 * k + 128 * (t + 1)],
                                    wo_sb[:, 512 * k:512 * (k + 1)],
                                    start=(k == 0), stop=(k == 7))
                            if n == 0 and t == NT - 1:
                                wofree.inc(mm)
                                wowait[m] = (wofree.h, wofree.n)
                            else:
                                ten.inc(mm)
                                wowait[m] = (ten.h, ten.n)

                @block.vector
                def _(e):
                    for n in range(2):
                        for t in range(NT):
                            m = 16 * n + t
                            xld.wait(e, xb2 + 16 * (m + 1))
                            e.wait_ge(*wowait[m])
                            asl = s_ps[:, 512 * (m % 2):512 * (m % 2) + 512]
                            vec.inc(e.tensor_tensor(
                                out_t[:, 512 * (m % 2):512 * (m % 2) + 512],
                                asl, condt[:, 2048 + 512 * n:
                                           2048 + 512 * (n + 1)],
                                op=ALU.mult))
                            vv = e.tensor_tensor(
                                buf2[:, 1024 * t + 512 * n:1024 * t + 512 * (n + 1)],
                                out_t[:, 512 * (m % 2):512 * (m % 2) + 512],
                                xr[:, 0:512],
                                op=ALU.add)
                            xfree.inc(vv)

                if "x1" in dbg:
                    @block.sync
                    def _(e):
                        vec.wait(e)
                        sto.dinc(e.dma_start(out=dbg_aps["x1"], in_=buf2[:]))

            barrier()

        ph4 = contextlib.ExitStack()
        with ph4:
            sb4 = lambda n, s, d_: ph4.enter_context(nc.sbuf_tensor(n, s, d_))
            hpre4 = sb4("hpre4", [128, 1024], F32)
            htile4 = sb4("htile4", [128, 2 * 1024], BF16)
            lg = sb4("lg", [128, 8], F32)
            tr_ps4 = ph4.enter_context(nc.psum_tensor("tr_ps4", [128, 1024], BF16))
            lg_ps = ph4.enter_context(nc.psum_tensor("lg_ps", [128, 512], F32))
            bc_ps4 = ph4.enter_context(nc.psum_tensor("bc_ps4", [128, 512], F32))

            rows4 = [4, 3, 5]
            dxb4 = dx.n
            tb4, sb24 = ten.n, sca2.n

            @block.sync
            def _(e):
                for i, row in enumerate(rows4):
                    for n in range(2):
                        j = 2 * i + n
                        if j >= 1:
                            ten.wait(e, tb4 + j)
                        dx.dinc(e.dma_start(
                            out=crow[:],
                            in_=condscr_d[:, 1024 * row + 512 * n:
                                          1024 * row + 512 * (n + 1)]))

            @block.tensor
            def _(e):
                for i, row in enumerate(rows4):
                    for n in range(2):
                        j = 2 * i + n
                        dx.wait(e, dxb4 + 16 * (j + 1))
                        if j >= 1:
                            sca2.wait(e, sb24 + j)
                        ten.inc(e.matmul(bc_ps4[:, 0:512], ones_f[:],
                                         crow[0:1, 0:512],
                                         start=True, stop=True))

            @block.scalar
            def _(e):
                for i, row in enumerate(rows4):
                    slot = [0, 1, 2][i]
                    for n in range(2):
                        j = 2 * i + n
                        ten.wait(e, tb4 + j + 1)
                        sca2.inc(e.copy(condt[:, 1024 * slot + 512 * n:
                                              1024 * slot + 512 * (n + 1)],
                                        bc_ps4[:, 0:512]))

            cb_done_t, cb_done_s = ten.n, sca2.n

            def get_x2(t):
                def wt(eng):
                    eng.wait_ge(sca2.h, cb_done_s)
                return buf2[:, 1024 * t:1024 * (t + 1)], wt

            emit_ln(get_x2, 0, 1024, buf1, tr_ps4, hpre4, htile4, False)

            h2_done_s = sca2.n

            if "h2T" in dbg:
                @block.sync
                def _(e):
                    sca2.wait(e)
                    sto.dinc(e.dma_start(out=dbg_aps["h2T"], in_=buf1[:]))

            t0r, v0r = ten.n, vec.n

            @block.tensor
            def _(e):
                sca2.wait(e, h2_done_s)
                for t in range(NT):
                    if t >= 1:
                        vec.wait(e, v0r + 2 * (t - 1) + 1)
                    for k in range(8):
                        mm = e.matmul(lg_ps[:, 0:8], buf1[:, 2048 * k + 128 * t:
                                                     2048 * k + 128 * (t + 1)],
                                      rwt_b[:, 8 * k:8 * (k + 1)],
                                      start=(k == 0), stop=(k == 7))
                    ten.inc(mm)

            s0r = sca.n

            @block.vector
            def _(e):
                for t in range(NT):
                    ten.wait(e, t0r + t + 1)
                    e.tensor_tensor(lg[:], lg_ps[:, 0:8], ebias_t[:], op=ALU.add)
                    e.drain()
                    e.tensor_reduce(scr[:, 0:1], lg[:], axis=AX.X, op=ALU.max,
                                    negate=True)
                    e.drain()
                    e.tensor_scalar(scr[:, 1:2], scr[:, 0:1], -1.0, None,
                                    op0=ALU.mult)
                    e.drain()
                    e.tensor_scalar(scr[:, 8:16], lg[:], scr[:, 1:2], None,
                                    op0=ALU.is_equal)
                    e.drain()
                    e.scalar_tensor_tensor(scr[:, 16:24], scr[:, 8:16], -1.0e30,
                                           lg[:], op0=ALU.mult, op1=ALU.add)
                    e.drain()
                    e.tensor_reduce(scr[:, 2:3], scr[:, 16:24], axis=AX.X,
                                    op=ALU.max, negate=True)
                    e.drain()
                    vv = e.tensor_tensor(scr[:, 3:4], scr[:, 0:1], scr[:, 2:3],
                                         op=ALU.subtract)
                    vec.inc(vv)
                    sca.wait(e, s0r + t + 1)
                    e.tensor_scalar(scr[:, 5:6], scr[:, 4:5], 1.0, None, op0=ALU.add)
                    e.drain()
                    e.reciprocal(scr[:, 6:7], scr[:, 5:6])
                    e.drain()
                    e.tensor_tensor(scr[:, 7:8], scr[:, 4:5], scr[:, 6:7],
                                    op=ALU.mult)
                    e.tensor_scalar(scr[:, 3:4], scr[:, 2:3], -1.0, None,
                                    op0=ALU.mult)
                    e.drain()
                    e.tensor_scalar(scr[:, 24:32], scr[:, 16:24], scr[:, 3:4], None,
                                    op0=ALU.is_equal)
                    e.tensor_scalar(scr[:, 8:16], scr[:, 8:16], scr[:, 6:7], None,
                                    op0=ALU.mult)
                    e.drain()
                    e.tensor_scalar(scr[:, 24:32], scr[:, 24:32], scr[:, 7:8], None,
                                    op0=ALU.mult)
                    e.drain()
                    vv2 = e.tensor_tensor(cwbuf[:, 8 * t:8 * (t + 1)], scr[:, 8:16],
                                          scr[:, 24:32], op=ALU.add)
                    vec.inc(vv2)

            @block.scalar
            def _(e):
                for t in range(NT):
                    vec.wait(e, v0r + 2 * t + 1)
                    sca.inc(e.activation(scr[:, 4:5], scr[:, 3:4], AF.Exp))

            if "cw" in dbg:
                @block.sync
                def _(e):
                    vec.wait(e)
                    sto.dinc(e.dma_start(out=dbg_aps["cw"], in_=cwbuf[:]))

        barrier()
        if stop_after < 5:
            @block.sync
            def _(e):
                sto.wait(e)
            return nc, dbg_aps

        ph5 = contextlib.ExitStack()
        with ph5:
            sb5 = lambda n, s, d_: ph5.enter_context(nc.sbuf_tensor(n, s, d_))
            w1r = [sb5("w1r0", [128, 4096], BF16), sb5("w1r1", [128, 4096], BF16)]
            w3r = [sb5("w3r0", [128, 4096], BF16), sb5("w3r1", [128, 4096], BF16)]
            w2b = sb5("w2b", [128, 16384], BF16)
            ypb = sb5("ypb", [128, 16384], BF16)
            accb = sb5("accb", [128, 8192], BF16)
            sg = sb5("sg", [128, 1024], F32)
            g_ps = ph5.enter_context(nc.psum_tensor("g_ps", [128, 1024], F32))
            u_ps = ph5.enter_context(nc.psum_tensor("u_ps", [128, 1024], F32))
            f_ps = ph5.enter_context(nc.psum_tensor("f_ps", [128, 1024], F32))

            nq = 0

            @block.gpsimd
            def _(e):
                for C in range(2):
                    for ei in range(NE):
                        ce = C * NE + ei
                        if ce >= 1:
                            fw2.wait(e, ce)
                        src2 = w2_d[ei].rearrange("p (k c) -> p k c", c=1024)
                        dw2.dinc(e.dma_start(out=w2b[:, 0:8192], in_=src2[:, 0:8, :]))
                        dw2.dinc(e.dma_start(out=w2b[:, 8192:16384],
                                             in_=src2[:, 8:16, :]))
                        for q in range(4):
                            qi = ce * 4 + q
                            s_ = qi % 2
                            if qi >= 2:
                                fw1[s_].wait(e, (qi - 2) // 2 + 1)
                                fw3[s_].wait(e, (qi - 2) // 2 + 1)
                            src1 = w1_d[ei].rearrange("p (k c) -> p k c", c=2048)[
                                :, :, 512 * q:512 * (q + 1)]
                            src3 = w3_d[ei].rearrange("p (k c) -> p k c", c=2048)[
                                :, :, 512 * q:512 * (q + 1)]
                            dw1[s_].dinc(e.dma_start(out=w1r[s_][:], in_=src1))
                            dw3[s_].dinc(e.dma_start(out=w3r[s_][:], in_=src3))

            bG, bSi, bYp, bF, bAC = sG.n, sSi.n, sYp.n, sF.n, sAC.n

            guwait = {}
            fwait = {}

            @block.tensor
            def _(e):
                for C in range(2):
                    for ei in range(NE):
                        ce = C * NE + ei
                        for q in range(4):
                            qi = ce * 4 + q
                            s_ = qi % 2
                            dw1[s_].wait(e, 16 * (qi // 2 + 1))
                            dw3[s_].wait(e, 16 * (qi // 2 + 1))
                            for hc4 in range(4):
                                Hc = 4 * q + hc4
                                for n in range(2):
                                    gi = (ce * 16 + Hc) * 2 + n
                                    if gi >= 2:
                                        sYp.wait(e, bYp + gi - 1)
                                    sl = gi % 2
                                    last_q = hc4 == 3 and n == 1
                                    for k in range(8):
                                        gmm = e.matmul(
                                            g_ps[:, 512 * sl:512 * (sl + 1)],
                                            w1r[s_][:, 512 * k + 128 * hc4:
                                                    512 * k + 128 * (hc4 + 1)],
                                            buf1[:, 2048 * k + 1024 * C + 512 * n:
                                                 2048 * k + 1024 * C + 512 * (n + 1)],
                                            start=(k == 0), stop=(k == 7))
                                        mm = e.matmul(
                                            u_ps[:, 512 * sl:512 * (sl + 1)],
                                            w3r[s_][:, 512 * k + 128 * hc4:
                                                    512 * k + 128 * (hc4 + 1)],
                                            buf1[:, 2048 * k + 1024 * C + 512 * n:
                                                 2048 * k + 1024 * C + 512 * (n + 1)],
                                            start=(k == 0), stop=(k == 7))
                                    if last_q:
                                        fw1[s_].inc(gmm)
                                        fw3[s_].inc(mm)
                                        guwait[gi] = (fw3[s_].h, fw3[s_].n)
                                    else:
                                        sG.inc(mm)
                                        guwait[gi] = (sG.h, sG.n)
                        dw2.wait(e, 32 * (ce + 1))
                        sYp.wait(e, bYp + (ce + 1) * 32)
                        for dout in range(2):
                            for tt in range(8):
                                fi = ce * 16 + dout * 8 + tt
                                if fi >= 2:
                                    sAC.wait(e, bAC + fi - 1)
                                sl = fi % 2
                                for Hc in range(16):
                                    mm = e.matmul(
                                        f_ps[:, 512 * sl:512 * (sl + 1)],
                                        ypb[:, 1024 * Hc + 128 * tt:
                                            1024 * Hc + 128 * (tt + 1)],
                                        w2b[:, 1024 * Hc + 512 * dout:
                                            1024 * Hc + 512 * (dout + 1)],
                                        start=(Hc == 0), stop=(Hc == 15))
                                if dout == 1 and tt == 7:
                                    fw2.inc(mm)
                                    fwait[fi] = (fw2.h, fw2.n)
                                else:
                                    sF.inc(mm)
                                    fwait[fi] = (sF.h, sF.n)

            @block.scalar
            def _(e):
                for C in range(2):
                    for ei in range(NE):
                        ce = C * NE + ei
                        for Hc in range(16):
                            for n in range(2):
                                gi = (ce * 16 + Hc) * 2 + n
                                e.wait_ge(*guwait[gi])
                                if gi >= 2:
                                    sYp.wait(e, bYp + gi - 1)
                                sl = gi % 2
                                ss = e.activation(sg[:, 512 * sl:512 * (sl + 1)],
                                                  g_ps[:, 512 * sl:512 * (sl + 1)],
                                                  AF.Silu)
                                sSi.inc(ss)
                        if ei == 0:
                            for dout in range(2):
                                for tt in range(8):
                                    fi = ce * 16 + dout * 8 + tt
                                    e.wait_ge(*fwait[fi])
                                    sl = fi % 2
                                    sc = e.copy(
                                        accb[:, 1024 * tt + 512 * dout:
                                             1024 * tt + 512 * (dout + 1)],
                                        f_ps[:, 512 * sl:512 * (sl + 1)])
                                    sAC.inc(sc)

            bOUT = sV2S.n
            sto_out_base = sto.n

            @block.vector
            def _(e):
                for C in range(2):
                    for ei in range(NE):
                        ce = C * NE + ei
                        for Hc in range(16):
                            for n in range(2):
                                gi = (ce * 16 + Hc) * 2 + n
                                if Hc == 0 and n == 0 and ce >= 1:
                                    e.wait_ge(*fwait[ce * 16 - 1])
                                sSi.wait(e, bSi + gi + 1)
                                sl = gi % 2
                                yy = e.tensor_tensor(
                                    ypb[:, 1024 * Hc + 512 * n:
                                        1024 * Hc + 512 * (n + 1)],
                                    sg[:, 512 * sl:512 * (sl + 1)],
                                    u_ps[:, 512 * sl:512 * (sl + 1)],
                                    op=ALU.mult)
                                sYp.inc(yy)
                        if ei > 0:
                            for dout in range(2):
                                for tt in range(8):
                                    fi = ce * 16 + dout * 8 + tt
                                    e.wait_ge(*fwait[fi])
                                    if ei == 1 and dout == 0 and tt == 0:
                                        sAC.wait(e, bAC + 144 * C + 16)
                                    sl = fi % 2
                                    cwcol = cwbuf[:, 8 * (8 * C + tt) + (ei - 1):
                                                  8 * (8 * C + tt) + ei]
                                    acsl = accb[:, 1024 * tt + 512 * dout:
                                                1024 * tt + 512 * (dout + 1)]
                                    va = e.scalar_tensor_tensor(
                                        acsl, f_ps[:, 512 * sl:512 * (sl + 1)],
                                        cwcol, acsl, op0=ALU.mult, op1=ALU.add)
                                    sAC.inc(va)
                    for tt in range(8):
                        t = 8 * C + tt
                        oidx = t
                        if oidx >= 2:
                            (sto if oidx % 2 == 0 else sto2).wait(
                                e, sto_out_base + 16 * ((oidx - 2) // 2 + 1)
                                if oidx % 2 == 0 else 16 * ((oidx - 2) // 2 + 1))
                        e.tensor_tensor(out_t[:, 1024 * (tt % 2):
                                              1024 * (tt % 2) + 1024],
                                        accb[:, 1024 * tt:1024 * (tt + 1)],
                                        condt[:, 2048:3072], op=ALU.mult)
                        vo = e.tensor_tensor(
                            out_t[:, 1024 * (tt % 2):1024 * (tt % 2) + 1024],
                            out_t[:, 1024 * (tt % 2):1024 * (tt % 2) + 1024],
                            buf2[:, 1024 * t:1024 * (t + 1)], op=ALU.add)
                        sV2S.inc(vo)

            @block.sync
            def _(e):
                for t in range(NT):
                    sV2S.wait(e, bOUT + t + 1)
                    sm = sto if t % 2 == 0 else sto2
                    sm.dinc(e.dma_start(
                        out=y_out[128 * t:128 * (t + 1), :],
                        in_=out_t[:, 1024 * (t % 2):1024 * (t % 2) + 1024]))
                sto.wait(e)
                sto2.wait(e)


    return nc, dbg_aps


def _blk(w):
    K, C = w.shape
    return np.ascontiguousarray(
        w.reshape(K // 128, 128, C).transpose(1, 0, 2).reshape(128, (K // 128) * C))


def _host_consts():
    ident = np.eye(128, dtype=np.float32)
    posc = (np.arange(16)[None, :] * 128 + np.arange(128)[:, None]).astype(np.float32)
    gam = np.asarray(RIEMANN, dtype=np.float32)
    invf = (1.0 / (gam / gam[0])).astype(np.float32)
    ifr = np.tile(invf, 16)[None, :].astype(np.float32)
    masks = np.zeros((128, 2048), dtype=np.float32)
    r = np.arange(128)[:, None]
    jj = np.arange(512)[None, :]
    for d in range(4):
        masks[:, 512 * d:512 * (d + 1)] = np.where(jj <= 128 * d + r, 0.0, -1.0e30)
    return ident, posc, ifr, masks


_CACHE = {}


def _get_built(dbg=(), stop_after=99):
    key = (tuple(sorted(dbg)), stop_after)
    if key not in _CACHE:
        _CACHE[key] = build_program(dbg, stop_after)
    return _CACHE[key]


def kernel(L0_attn_x, L0_attn_w_qkv, L0_attn_w_o, L0_attn_log_scale,
           L0_ffn_router_w, L0_ffn_experts_bias, L0_ffn_w1, L0_ffn_w3, L0_ffn_w2,
           L0_ffn_sw1, L0_ffn_sw3, L0_ffn_sw2, L0_c, L0_wmod, _dbg=(),
           _stop_after=99, _ncores=8):
    f32 = lambda a: np.ascontiguousarray(np.asarray(a, dtype=np.float32))
    x = f32(L0_attn_x)
    ident, posc, ifr, masks = _host_consts()
    wqkv = _blk(f32(L0_attn_w_qkv))
    wo = _blk(f32(L0_attn_w_o))
    wmod = _blk(f32(L0_wmod))
    rwt = _blk(f32(L0_ffn_router_w).T)
    w1 = np.stack([_blk(w) for w in
                   np.concatenate([f32(L0_ffn_sw1), f32(L0_ffn_w1)], axis=0)])
    w3 = np.stack([_blk(w) for w in
                   np.concatenate([f32(L0_ffn_sw3), f32(L0_ffn_w3)], axis=0)])
    w2 = np.stack([_blk(w) for w in
                   np.concatenate([f32(L0_ffn_sw2), f32(L0_ffn_w2)], axis=0)])
    ls = f32(L0_attn_log_scale).reshape(1, 16)
    ebias = f32(L0_ffn_experts_bias).reshape(1, 8)
    cvec = f32(L0_c).reshape(8, 8, 128)

    nc, dbg_aps = _get_built(_dbg, _stop_after)
    in_maps = []
    for b in range(_ncores):
        in_maps.append({
            "x": x[b], "wqkv": wqkv, "wo": wo, "ls": ls, "rwt": rwt,
            "ebias": ebias, "w1": w1, "w3": w3, "w2": w2, "cvec": cvec[b],
            "wmod": wmod, "ident": ident, "posc": posc, "ifr": ifr,
            "masks": masks,
        })
    res = run_bass_kernel_spmd(nc, in_maps, core_ids=list(range(_ncores)))
    out = np.stack([np.asarray(res.results[b]["y"]) for b in range(_ncores)]
                   + [np.zeros((2048, 1024), np.float32)] * (8 - _ncores), axis=0)
    if _dbg:
        dbgres = [{k: np.asarray(res.results[b]["dbg_" + k]) for k in _dbg}
                  for b in range(_ncores)]
        return out.astype(np.float32), dbgres
    return out.astype(np.float32)


# version 02
# speedup vs baseline: 1.1867x; 1.1867x over previous
import contextlib
import math
import numpy as np
import ml_dtypes
import concourse.bass as bass
import concourse.mybir as mybir
from concourse.bass_utils import run_bass_kernel_spmd

F32 = mybir.dt.float32
BF16 = mybir.dt.bfloat16
I32 = mybir.dt.int32
AF = mybir.ActivationFunctionType
ALU = mybir.AluOpType
AX = mybir.AxisListType

PI = math.pi
INV2PI = 1.0 / (2.0 * math.pi)
RC1 = 6.28125
RC2 = 2.0 * math.pi - RC1
NT = 16
NH = 16
NE = 9
NIT = NH * NT

RIEMANN = (14.134725142, 21.022039639, 25.01085758, 30.424876126, 32.935061588,
        37.586178159, 40.918719012, 43.327073281, 48.005150881, 49.773832478,
        52.970321478, 56.446247697, 59.347044003, 60.831778525, 65.112544048,
        67.079810529, 69.546401711, 72.067157674, 75.704690699, 77.144840069,
        79.33737502, 82.910380854, 84.735492981, 87.425274613, 88.809111208,
        92.491899271, 94.651344041, 95.870634228, 98.831194218, 101.317851006,
        103.72553804, 105.446623052)


class Cnt:
    def __init__(self, h):
        self.h = h
        self.n = 0

    def inc(self, instr, k=1):
        instr.then_inc(self.h, k)
        self.n += k
        return instr

    def dinc(self, instr):
        return self.inc(instr, 16)

    def wait(self, eng, n=None):
        eng.wait_ge(self.h, self.n if n is None else n)


def nkc(i):
    return (i % NT) // 4 + 1


KCUM = [0]
for _i in range(NIT):
    KCUM.append(KCUM[-1] + nkc(_i))


def build_program(dbg=(), stop_after=99):
    nc = bass.Bass("TRN2", target_bir_lowering=False,
                   detect_race_conditions=False)

    x_in = nc.dram_tensor("x", [2048, 1024], F32, kind="ExternalInput").ap()
    wqkv_d = nc.dram_tensor("wqkv", [128, 8 * 3072], F32, kind="ExternalInput").ap()
    wo_d = nc.dram_tensor("wo", [128, 8 * 1024], F32, kind="ExternalInput").ap()
    ls_d = nc.dram_tensor("ls", [1, 16], F32, kind="ExternalInput").ap()
    rwt_d = nc.dram_tensor("rwt", [128, 64], F32, kind="ExternalInput").ap()
    ebias_d = nc.dram_tensor("ebias", [1, 8], F32, kind="ExternalInput").ap()
    w1_d = nc.dram_tensor("w1", [NE, 128, 8 * 2048], F32, kind="ExternalInput").ap()
    w3_d = nc.dram_tensor("w3", [NE, 128, 8 * 2048], F32, kind="ExternalInput").ap()
    w2_d = nc.dram_tensor("w2", [NE, 128, 16 * 1024], F32, kind="ExternalInput").ap()
    c_d = nc.dram_tensor("cvec", [8, 128], F32, kind="ExternalInput").ap()
    wmod_d = nc.dram_tensor("wmod", [128, 8 * 6144], F32, kind="ExternalInput").ap()
    ident_d = nc.dram_tensor("ident", [128, 128], F32, kind="ExternalInput").ap()
    posc_d = nc.dram_tensor("posc", [128, 16], F32, kind="ExternalInput").ap()
    ifr_d = nc.dram_tensor("ifr", [1, 512], F32, kind="ExternalInput").ap()
    masks_d = nc.dram_tensor("masks", [128, 2048], F32, kind="ExternalInput").ap()
    modscr_d = nc.dram_tensor("modscr", [48, 128], F32, kind="Internal").ap()
    condscr_d = nc.dram_tensor("condscr", [1, 6144], F32, kind="Internal").ap()
    y_out = nc.dram_tensor("y", [2048, 1024], F32, kind="ExternalOutput").ap()

    dbg_aps = {}
    for name, shape, dt in [
            ("cond", [1, 6144], F32), ("hT", [128, 16384], BF16),
            ("qT", [128, 16384], BF16), ("kT", [128, 16384], BF16),
            ("v", [128, 16384], BF16), ("otm", [128, 16384], BF16),
            ("oT", [128, 16384], BF16), ("x1", [128, 16384], BF16),
            ("h2T", [128, 16384], BF16), ("cw", [128, 128], F32)]:
        if name in dbg:
            dbg_aps[name] = nc.dram_tensor("dbg_" + name, shape, dt,
                                           kind="ExternalOutput").ap()

    gst = contextlib.ExitStack()
    with gst:
        sb = lambda n, s, d_: gst.enter_context(nc.sbuf_tensor(n, s, d_))
        sem = lambda n: Cnt(gst.enter_context(nc.semaphore(n)))

        identf = sb("identf", [128, 128], F32)
        identb = sb("identb", [128, 128], BF16)
        crow = sb("crow", [1, 512], F32)
        condt = sb("condt", [128, 3 * 1024], BF16)
        ebias_t = sb("ebias_t", [128, 8], F32)
        ifs_tile = sb("ifs_tile", [128, 512], F32)
        posc_sb = sb("posc_sb", [128, 16], F32)
        rwt_f = sb("rwt_f", [128, 64], F32)
        rwt_b = sb("rwt_b", [128, 64], BF16)
        ebias_r = sb("ebias_r", [1, 8], F32)
        cwbuf = sb("cwbuf", [128, 128], F32)
        dump = sb("dump", [128, 1024], BF16)
        scr = sb("scr", [128, 32], F32)
        ones_f = sb("ones_f", [1, 128], F32)
        epscol = sb("epscol", [128, 1], F32)
        junk = sb("junk", [1, 2], F32)
        buf1 = sb("buf1", [128, 16384], BF16)
        buf2 = sb("buf2", [128, 16384], BF16)
        out_t = sb("out_t", [128, 2 * 1024], F32)

        ph = sem("ph")
        dx = sem("dx")
        dwm = [sem("dwm0"), sem("dwm1")]
        fwm = [sem("fwm0"), sem("fwm1")]
        dwq = [sem("dwq0"), sem("dwq1")]
        fwq = [sem("fwq0"), sem("fwq1")]
        dwo = sem("dwo")
        dident = sem("dident")
        dmask = sem("dmask")
        sto2 = sem("sto2")
        dw1 = [sem("dw10"), sem("dw11")]
        fw1 = [sem("fw10"), sem("fw11")]
        dw3 = [sem("dw30"), sem("dw31")]
        fw3 = [sem("fw30"), sem("fw31")]
        dw2 = sem("dw2")
        fw2 = sem("fw2")
        ten = sem("ten")
        vec = sem("vec")
        sca = sem("sca")
        sca2 = sem("sca2")
        xld = sem("xld")
        xfree = sem("xfree")
        sto = sem("sto")
        sTS = sem("sTS")
        sVM = sem("sVM")
        sE = sem("sE")
        sV2S = sem("sV2S")
        sT2 = sem("sT2")
        sSE = sem("sSE")
        sTO = sem("sTO")
        sV2N = sem("sV2N")
        sG = sem("sG")
        sSi = sem("sSi")
        sYp = sem("sYp")
        sF = sem("sF")
        sAC = sem("sAC")

        block = gst.enter_context(nc.Block())
        phase_no = [0]

        def barrier():
            phase_no[0] += 1
            tgt = 5 * phase_no[0]

            @block.sync
            def _(e):
                sto.wait(e)
                sto2.wait(e)
                ph.inc(e.nop())
                e.wait_ge(ph.h, tgt)

            @block.gpsimd
            def _(e):
                ph.inc(e.engine_nop())
                e.wait_ge(ph.h, tgt)

            @block.tensor
            def _(e):
                ph.inc(e.ldweights(identb[:, 0:1]))
                e.wait_ge(ph.h, tgt)

            @block.vector
            def _(e):
                ph.inc(e.engine_nop())
                e.wait_ge(ph.h, tgt)

            @block.scalar
            def _(e):
                ph.inc(e.copy(junk[0:1, 0:1], junk[0:1, 1:2]))
                e.wait_ge(ph.h, tgt)

        ph0 = contextlib.ExitStack()
        with ph0:
            sb0 = lambda n, s, d_: ph0.enter_context(nc.sbuf_tensor(n, s, d_))
            c_sb = sb0("c_sb", [8, 128], F32)
            csil = sb0("csil", [8, 128], BF16)
            wmr = [sb0("wmr0", [128, 6144], BF16), sb0("wmr1", [128, 6144], BF16)]
            mod_sb = sb0("mod_sb", [128, 48], F32)
            modT = sb0("modT", [48, 128], F32)
            ls_sb = sb0("ls_sb", [1, 16], F32)
            els_sb = sb0("els_sb", [1, 16], F32)
            ifr_sb = sb0("ifr_sb", [1, 512], F32)
            sclrow = sb0("sclrow", [1, 512], F32)
            cT_bf = sb0("cT_bf", [128, 8], BF16)
            mod_ps = ph0.enter_context(nc.psum_tensor("mod_ps", [128, 48], F32))
            ct_ps = ph0.enter_context(nc.psum_tensor("ct_ps", [128, 8], BF16))
            mt_ps = ph0.enter_context(nc.psum_tensor("mt_ps", [48, 128], F32))
            bc_ps0 = ph0.enter_context(nc.psum_tensor("bc_ps0", [128, 512], F32))

            @block.sync
            def _(e):
                dx.dinc(e.dma_start(out=identf[:], in_=ident_d))
                dx.dinc(e.dma_start(out=c_sb[:], in_=c_d))
                dx.dinc(e.dma_start(out=ls_sb[:], in_=ls_d))
                dx.dinc(e.dma_start(out=posc_sb[:], in_=posc_d))
                dx.dinc(e.dma_start(out=ifr_sb[:], in_=ifr_d))
                dx.dinc(e.dma_start(out=ebias_r[:], in_=ebias_d))
                dx.dinc(e.dma_start(out=rwt_f[:], in_=rwt_d))

            @block.gpsimd
            def _(e):
                dident.dinc(e.dma_start(out=identb[:], in_=ident_d))
                for k in range(8):
                    s_ = k % 2
                    if k >= 2:
                        fwm[s_].wait(e, (k - 2) // 2 + 1)
                    dwm[s_].dinc(e.dma_start(
                        out=wmr[s_][:], in_=wmod_d[:, 6144 * k:6144 * (k + 1)]))

            @block.vector
            def _(e):
                e.memset(junk[:], 0.0)
                dx.wait(e)
                e.memset(ones_f[:], 1.0)
                e.memset(epscol[:], 1.0e-5)
                vec.inc(e.tensor_copy(rwt_b[:], rwt_f[:]))

            @block.scalar
            def _(e):
                vec.wait(e)
                e.activation(csil[:], c_sb[:], AF.Silu)
                sca.inc(e.activation(els_sb[:], ls_sb[:], AF.Exp))

            @block.tensor
            def _(e):
                sca.wait(e)
                dident.wait(e)
                ten.inc(e.transpose(ct_ps[:], csil[:], identb[0:8, 0:8]))

            @block.scalar
            def _(e):
                ten.wait(e)
                sca.inc(e.copy(cT_bf[:], ct_ps[:]))

            t_mv0, s_mv0, v_mv0 = ten.n, sca.n, vec.n

            @block.tensor
            def _(e):
                sca.wait(e)
                for k in range(8):
                    s_ = k % 2
                    dwm[s_].wait(e, 16 * (k // 2 + 1))
                    if k >= 1:
                        vec.wait(e, v_mv0 + k)
                    for m in range(48):
                        mm = e.matmul(mod_ps[:, m:m + 1],
                                      wmr[s_][:, 128 * m:128 * (m + 1)],
                                      cT_bf[:, k:k + 1],
                                      start=True, stop=True)
                    fwm[s_].inc(mm)

            @block.vector
            def _(e):
                for k in range(8):
                    e.wait_ge(fwm[k % 2].h, k // 2 + 1)
                    if k == 0:
                        vec.inc(e.tensor_copy(mod_sb[:], mod_ps[:]))
                    else:
                        vec.inc(e.tensor_tensor(mod_sb[:], mod_sb[:], mod_ps[:],
                                                op=ALU.add))
                    e.drain()

            @block.scalar
            def _(e):
                vec.wait(e)
                sca.inc(e.nop())

            @block.vector
            def _(e):
                sca.wait(e)
                e.tensor_scalar(mod_sb[:, 8:16], mod_sb[:, 8:16], 1.0, None,
                                op0=ALU.add)
                vec.inc(e.tensor_scalar(mod_sb[:, 32:40], mod_sb[:, 32:40], 1.0,
                                        None, op0=ALU.add))
                e.drain()

            @block.scalar
            def _(e):
                vec.wait(e)
                e.sign(scr[:, 0:8], mod_sb[:, 16:24])
                e.sign(scr[:, 8:16], mod_sb[:, 40:48])
                e.activation(scr[:, 16:24], mod_sb[:, 16:24], AF.Abs)
                sca.inc(e.activation(scr[:, 24:32], mod_sb[:, 40:48], AF.Abs))

            @block.vector
            def _(e):
                sca.wait(e)
                e.tensor_scalar(scr[:, 16:32], scr[:, 16:32], 0.25, None,
                                op0=ALU.max)
                e.drain()
                e.tensor_tensor(mod_sb[:, 16:24], scr[:, 0:8], scr[:, 16:24],
                                op=ALU.mult)
                e.tensor_tensor(mod_sb[:, 40:48], scr[:, 8:16], scr[:, 24:32],
                                op=ALU.mult)
                e.drain()
                vec.inc(e.tensor_tensor(
                    sclrow.ap().rearrange("p (h q) -> p h q", h=16),
                    ifr_sb.ap().rearrange("p (h q) -> p h q", h=16),
                    els_sb.ap().rearrange("p (h o) -> p h o", o=1).broadcast_to([1, 16, 32]),
                    op=ALU.mult))

            @block.tensor
            def _(e):
                vec.wait(e)
                ten.inc(e.transpose(mt_ps[:], mod_sb[:], identf[:]))

            @block.scalar
            def _(e):
                ten.wait(e)
                sca.inc(e.copy(modT[:], mt_ps[:]))

            @block.sync
            def _(e):
                sca.wait(e)
                dx.dinc(e.dma_start(
                    out=condscr_d.rearrange("o (x r) -> (o x) r", r=128),
                    in_=modT[:]))

            rows03 = [1, 0, 2]
            dxb0 = dx.n
            tb0, sb20 = ten.n, sca2.n

            @block.sync
            def _(e):
                dx.wait(e, dxb0)
                for i, row in enumerate(rows03):
                    for n in range(2):
                        j = 2 * i + n
                        if j >= 1:
                            ten.wait(e, tb0 + j)
                        dx.dinc(e.dma_start(
                            out=crow[:],
                            in_=condscr_d[:, 1024 * row + 512 * n:
                                          1024 * row + 512 * (n + 1)]))

            @block.tensor
            def _(e):
                for i, row in enumerate(rows03):
                    for n in range(2):
                        j = 2 * i + n
                        dx.wait(e, dxb0 + 16 * (j + 1))
                        if j >= 1:
                            sca2.wait(e, sb20 + j)
                        ten.inc(e.matmul(bc_ps0[:, 0:512], ones_f[:],
                                         crow[0:1, 0:512],
                                         start=True, stop=True))
                sca2.wait(e, sb20 + 6)
                ten.inc(e.matmul(bc_ps0[:, 0:8], ones_f[:], ebias_r[:],
                                 start=True, stop=True))
                sca2.wait(e, sb20 + 7)
                ten.inc(e.matmul(bc_ps0[:, 0:512], ones_f[:], sclrow[:],
                                 start=True, stop=True))

            @block.scalar
            def _(e):
                for i, row in enumerate(rows03):
                    slot = [0, 1, 2][i]
                    for n in range(2):
                        j = 2 * i + n
                        ten.wait(e, tb0 + j + 1)
                        sca2.inc(e.copy(condt[:, 1024 * slot + 512 * n:
                                              1024 * slot + 512 * (n + 1)],
                                        bc_ps0[:, 0:512]))
                ten.wait(e, tb0 + 7)
                sca2.inc(e.copy(ebias_t[:], bc_ps0[:, 0:8]))
                ten.wait(e, tb0 + 8)
                sca2.inc(e.copy(ifs_tile[:], bc_ps0[:, 0:512]))

            if "cond" in dbg:
                @block.sync
                def _(e):
                    vec.wait(e)
                    sca2.wait(e)
                    sto.dinc(e.dma_start(out=dbg_aps["cond"], in_=condscr_d))

        barrier()
        if stop_after < 1:
            @block.sync
            def _(e):
                sto.wait(e)
            return nc, dbg_aps

        def emit_ln(get_x, scale_off, shift_off, dstT, tr_ps, hpre, htile,
                    use_xfree):
            v0, s0, t0, s20 = vec.n, sca.n, ten.n, sca2.n

            @block.vector
            def _(v):
                for t in range(NT):
                    xs, wt = get_x(t)
                    wt(v)
                    if t >= 2:
                        sca.wait(v, s0 + 3 * (t - 2) + 3)
                    o = 8 * (t % 2)
                    v.tensor_reduce(scr[:, o + 0:o + 1], xs, axis=AX.X, op=ALU.add)
                    v.drain()
                    vec.inc(v.tensor_scalar(scr[:, o + 1:o + 2], scr[:, o + 0:o + 1],
                                            -1.0 / 1024.0, None, op0=ALU.mult))
                    sca.wait(v, s0 + 3 * t + 2)
                    vec.inc(v.scalar_tensor_tensor(
                        scr[:, o + 4:o + 5], scr[:, o + 2:o + 3], 1.0 / 1024.0,
                        scr[:, o + 3:o + 4], op0=ALU.mult, op1=ALU.subtract))
                    sca.wait(v, s0 + 3 * t + 3)
                    v.reciprocal(scr[:, o + 6:o + 7], scr[:, o + 5:o + 6])
                    v.drain()
                    hp = v.tensor_scalar(hpre[:], xs, scr[:, o + 1:o + 2],
                                         scr[:, o + 6:o + 7], op0=ALU.add,
                                         op1=ALU.mult)
                    if use_xfree:
                        xfree.inc(hp)
                    if t >= 2:
                        ten.wait(v, t0 + (t - 2) + 1)
                    hsl = htile[:, 1024 * (t % 2):1024 * (t % 2) + 1024]
                    v.tensor_tensor(hpre[:], hpre[:],
                                    condt[:, scale_off:scale_off + 1024], op=ALU.mult)
                    vec.inc(v.tensor_tensor(hsl, hpre[:],
                                            condt[:, shift_off:shift_off + 1024],
                                            op=ALU.add))

            def evac(s, t):
                ten.wait(s, t0 + t + 1)
                dst = dstT.ap().rearrange("p (j c) -> p j c", c=2048)[
                    :, :, 128 * t:128 * (t + 1)]
                srcp = tr_ps.ap().rearrange("p (j c) -> p j c", c=128)
                sca2.inc(s.copy(dst, srcp))

            @block.scalar
            def _(s):
                for t in range(NT):
                    xs, wt = get_x(t)
                    wt(s)
                    o = 8 * (t % 2)
                    sca.inc(s.activation(dump[:], xs, AF.Square,
                                         accum_out=scr[:, o + 2:o + 3]))
                    vec.wait(s, v0 + 3 * t + 1)
                    sca.inc(s.activation(scr[:, o + 3:o + 4], scr[:, o + 1:o + 2],
                                         AF.Square))
                    vec.wait(s, v0 + 3 * t + 2)
                    sca.inc(s.activation(scr[:, o + 5:o + 6], scr[:, o + 4:o + 5],
                                         AF.Sqrt, bias=epscol[:, 0:1]))
                    if t >= 1:
                        evac(s, t - 1)
                evac(s, NT - 1)

            @block.tensor
            def _(t_):
                for t in range(NT):
                    vec.wait(t_, v0 + 3 * t + 3)
                    if t >= 1:
                        sca2.wait(t_, s20 + t)
                    hsl = htile[:, 1024 * (t % 2):1024 * (t % 2) + 1024]
                    for j in range(8):
                        mm = t_.transpose(tr_ps[:, 128 * j:128 * (j + 1)],
                                          hsl[:, 128 * j:128 * (j + 1)], identb[:])
                    ten.inc(mm)

        phB = contextlib.ExitStack()
        with phB:
            sbB = lambda n, s, d_: phB.enter_context(nc.sbuf_tensor(n, s, d_))
            qT = sbB("qT", [128, 16384], BF16)
            kT = sbB("kT", [128, 16384], BF16)
            v_sb = sbB("v_sb", [128, 16384], BF16)

            ph1 = contextlib.ExitStack()
            with ph1:
                hpre = ph1.enter_context(nc.sbuf_tensor("hpre", [128, 1024], F32))
                htile = ph1.enter_context(nc.sbuf_tensor("htile", [128, 2048], BF16))
                xr = ph1.enter_context(nc.sbuf_tensor("xr", [128, 2048], F32))
                tr_ps1 = ph1.enter_context(nc.psum_tensor("tr_ps1", [128, 1024], BF16))
                xb = xld.n
                xfb = xfree.n

                @block.sync
                def _(e):
                    for t in range(NT):
                        if t >= 2:
                            xfree.wait(e, xfb + t - 1)
                        xld.dinc(e.dma_start(
                            out=xr[:, 1024 * (t % 2):1024 * (t % 2) + 1024],
                            in_=x_in[128 * t:128 * (t + 1), :]))

                def get_x1(t):
                    def wt(eng):
                        eng.wait_ge(xld.h, xb + 16 * (t + 1))
                    return xr[:, 1024 * (t % 2):1024 * (t % 2) + 1024], wt

                emit_ln(get_x1, 0, 1024, buf1, tr_ps1, hpre, htile, True)

                if "hT" in dbg:
                    @block.sync
                    def _(e):
                        sca2.wait(e)
                        sto.dinc(e.dma_start(out=dbg_aps["hT"], in_=buf1[:]))

            barrier()
            if stop_after < 2:
                @block.sync
                def _(e):
                    sto.wait(e)
                return nc, dbg_aps

            ph2 = contextlib.ExitStack()
            with ph2:
                sb2 = lambda n, s, d_: ph2.enter_context(nc.sbuf_tensor(n, s, d_))
                wqr = [sb2("wqr0", [128, 4096], BF16), sb2("wqr1", [128, 4096], BF16)]
                tha = sb2("tha", [128, 1024], F32)
                trig = tha
                thf = out_t.ap()[:, 0:1024]
                ta = out_t.ap()[:, 1024:1536]
                tb = out_t.ap()[:, 1536:2048]
                qrot = sb2("qrot", [128, 2 * 1024], BF16)
                qkv_ps = ph2.enter_context(nc.psum_tensor("qkv_ps", [128, 2048], F32))
                tr_ps2 = ph2.enter_context(nc.psum_tensor("tr_ps2", [128, 1024], BF16))

                t0, t20, v0, v20, s0, s20 = (ten.n, sTS.n, vec.n, sVM.n, sca.n,
                                             sca2.n)
                tE0 = sE.n

                @block.gpsimd
                def _(e):
                    for tau in range(3):
                        for p in range(2):
                            s_ = p
                            if tau >= 1:
                                fwq[s_].wait(e, tau)
                            piece = 2 * tau + p
                            src = wqkv_d.rearrange("p (k c) -> p k c", c=3072)[
                                :, :, 512 * piece:512 * (piece + 1)]
                            dwq[s_].dinc(e.dma_start(out=wqr[s_][:, 0:2048],
                                                     in_=src[:, 0:4, :]))
                            dwq[s_].dinc(e.dma_start(out=wqr[s_][:, 2048:4096],
                                                     in_=src[:, 4:8, :]))

                mmwait = {}

                @block.tensor
                def _(e):
                    def transposes(tau, t):
                        sTS.wait(e, t20 + 32 * tau + 2 * t + 2)
                        if 16 * tau + t >= 1:
                            sca2.wait(e, s20 + 16 * tau + t)
                        for j in range(8):
                            mm = e.transpose(
                                tr_ps2[:, 128 * j:128 * (j + 1)],
                                qrot[:, 1024 * (t % 2) + 128 * j:
                                     1024 * (t % 2) + 128 * (j + 1)],
                                identb[:])
                        sE.inc(mm)

                    for tau in range(3):
                        dwq[0].wait(e, 32 * (tau + 1))
                        dwq[1].wait(e, 32 * (tau + 1))
                        for t in range(NT):
                            for p in range(2):
                                it = 32 * tau + 2 * t + p
                                if it >= 4:
                                    pit = it - 4
                                    if pit < 64:
                                        sTS.wait(e, t20 + pit + 1)
                                    else:
                                        sca2.wait(e, s20 + 32 + (pit - 64) + 1)
                                sl = it % 4
                                for k in range(8):
                                    mm = e.matmul(
                                        qkv_ps[:, 512 * sl:512 * (sl + 1)],
                                        buf1[:, 2048 * k + 128 * t:
                                             2048 * k + 128 * (t + 1)],
                                        wqr[p][:, 512 * k:512 * (k + 1)],
                                        start=(k == 0), stop=(k == 7))
                                if t == NT - 1:
                                    fwq[p].inc(mm)
                                    mmwait[it] = (fwq[p].h, fwq[p].n)
                                else:
                                    ten.inc(mm)
                                    mmwait[it] = (ten.h, ten.n)
                            if tau < 2 and t >= 2:
                                transposes(tau, t - 2)
                        if tau < 2:
                            transposes(tau, NT - 2)
                            transposes(tau, NT - 1)

                @block.vector
                def _(e):
                    for tau in range(2):
                        for t in range(NT):
                            e.tensor_scalar(tha[:, 0:512], ifs_tile[:],
                                            posc_sb[:, t:t + 1], None, op0=ALU.mult)
                            e.tensor_scalar(tha[:, 512:1024], tha[:, 0:512],
                                            PI / 2.0, None, op0=ALU.add)
                            e.tensor_scalar(thf, tha[:], INV2PI, 12582912.0,
                                            op0=ALU.mult, op1=ALU.add)
                            e.tensor_scalar(thf, thf, -12582912.0, None,
                                            op0=ALU.add)
                            e.scalar_tensor_tensor(tha[:], thf, -RC1, tha[:],
                                                   op0=ALU.mult, op1=ALU.add)
                            e.scalar_tensor_tensor(tha[:], thf, -RC2,
                                                   tha[:], op0=ALU.mult,
                                                   op1=ALU.add)
                            vVM = e.tensor_scalar(tha[:], tha[:], 3.1415925,
                                                  -3.1415925, op0=ALU.min,
                                                  op1=ALU.max)
                            sVM.inc(vVM)
                            sca.wait(e, s0 + 16 * tau + t + 1)
                            if 16 * tau + t >= 2:
                                sE.wait(e, tE0 + 16 * tau + t - 2 + 1)
                            it0 = 32 * tau + 2 * t
                            e.wait_ge(*mmwait[it0])
                            e.wait_ge(*mmwait[it0 + 1])
                            a0 = (it0 % 4) // 2
                            qp = qkv_ps[:, 1024 * a0:1024 * (a0 + 1)]
                            x1v = qp[:, 0:1024:2]
                            x2v = qp[:, 1:1024:2]
                            cosh = trig[:, 512:1024]
                            sinh = trig[:, 0:512]
                            qsl = qrot.ap().rearrange(
                                "q (tt hh c) -> q tt hh c", tt=2, c=64)[
                                :, t % 2, :, :]
                            ta3 = ta.rearrange("q (hh c) -> q hh c", c=32)
                            tb3 = tb.rearrange("q (hh c) -> q hh c", c=32)
                            e.tensor_tensor(ta, x1v, cosh, op=ALU.mult)
                            e.tensor_tensor(tb, x2v, sinh, op=ALU.mult)
                            e.tensor_tensor(qsl[:, :, 0:32], ta3, tb3,
                                            op=ALU.subtract)
                            e.tensor_tensor(ta, x1v, sinh, op=ALU.mult)
                            e.tensor_tensor(tb, x2v, cosh, op=ALU.mult)
                            rx2 = e.tensor_tensor(qsl[:, :, 32:64], ta3, tb3,
                                                  op=ALU.add)
                            sTS.inc(rx2)
                            sTS.inc(e.engine_nop())

                @block.scalar
                def _(e):
                    def evac2(tau, t):
                        g = 16 * tau + t
                        sE.wait(e, tE0 + g + 1)
                        dT = qT if tau == 0 else kT
                        dst = dT.ap().rearrange("p (j c) -> p j c", c=2048)[
                            :, :, 128 * t:128 * (t + 1)]
                        srcp = tr_ps2.ap().rearrange("p (j c) -> p j c", c=128)
                        sca2.inc(e.copy(dst, srcp))

                    for tau in range(2):
                        for t in range(NT):
                            sVM.wait(e, v20 + 16 * tau + t + 1)
                            sca.inc(e.activation(trig[:], tha[:], AF.Sin))
                            if t >= 2:
                                evac2(tau, t - 2)
                        evac2(tau, NT - 2)
                        evac2(tau, NT - 1)
                    for t in range(NT):
                        for p in range(2):
                            it = 64 + 2 * t + p
                            e.wait_ge(*mmwait[it])
                            sl = it % 4
                            sca2.inc(e.copy(
                                v_sb[:, 1024 * t + 512 * p:1024 * t + 512 * (p + 1)],
                                qkv_ps[:, 512 * sl:512 * (sl + 1)]))

                for nm_ in ("qT", "kT", "v"):
                    if nm_ in dbg:
                        src_t = {"qT": qT, "kT": kT, "v": v_sb}[nm_]

                        @block.sync
                        def _(e, src_t=src_t, nm_=nm_):
                            sca2.wait(e)
                            sE.wait(e)
                            sto.dinc(e.dma_start(out=dbg_aps[nm_], in_=src_t[:]))

            barrier()
            if stop_after < 3:
                @block.sync
                def _(e):
                    sto.wait(e)
                return nc, dbg_aps

            ph3 = contextlib.ExitStack()
            with ph3:
                sb3 = lambda n, s, d_: ph3.enter_context(nc.sbuf_tensor(n, s, d_))
                masks_sb = sb3("masks_sb", [128, 2048], BF16)
                P_sb = sb3("P_sb", [128, 2048], BF16)
                PT_sb = sb3("PT_sb", [128, 2048], BF16)
                sums = sb3("sums", [128, 8], F32)
                wo_sb = sb3("wo_sb", [128, 4096], BF16)
                xr = sb3("xr3", [128, 512], F32)
                s_ps = ph3.enter_context(nc.psum_tensor("s_ps", [128, 2048], F32))
                pt_ps2 = [ph3.enter_context(nc.psum_tensor("pt_ps0", [128, 1024], BF16)),
                          ph3.enter_context(nc.psum_tensor("pt_ps1", [128, 1024], BF16))]
                o_ps2 = [ph3.enter_context(nc.psum_tensor("o_ps0", [128, 512], F32)),
                         ph3.enter_context(nc.psum_tensor("o_ps1", [128, 512], F32))]

                b = dict(TS=sTS.n, VM=sVM.n, E=sE.n, V2S=sV2S.n, T2=sT2.n,
                         SE=sSE.n, TO=sTO.n, V2N=sV2N.n)

                @block.gpsimd
                def _(e):
                    dmask.dinc(e.dma_start(out=masks_sb[:], in_=masks_d))

                def hq(i):
                    return i // NT, i % NT

                @block.tensor
                def _(e):
                    dmask.wait(e)
                    for j in range(NIT + 2):
                        if j < NIT:
                            h, qt = hq(j)
                            qg = qt // 4
                            r0 = 64 * (h % 2)
                            cbase = 2048 * (h // 2)
                            for kc in range(qg + 1):
                                if j >= 1:
                                    pn = nkc(j - 1)
                                    sE.wait(e, b["E"] + KCUM[j - 1] +
                                            min(kc + 1, pn))
                                mm = e.matmul(
                                    s_ps[:, 512 * kc:512 * (kc + 1)],
                                    qT[r0:r0 + 64,
                                       cbase + 128 * qt:cbase + 128 * (qt + 1)],
                                    kT[r0:r0 + 64,
                                       cbase + 512 * kc:cbase + 512 * (kc + 1)],
                                    start=True, stop=True)
                            sTS.inc(mm)
                        if 0 <= j - 2 < NIT:
                            i = j - 2
                            h, qt = hq(i)
                            sSE.wait(e, b["SE"] + KCUM[i + 1])
                            if i >= 2:
                                sV2N.wait(e, b["V2N"] + i - 1)
                            osl = o_ps2[i % 2][:, 0:64]
                            nk4 = 4 * nkc(i)
                            ptb = PT_sb
                            for kap in range(nk4):
                                mm = e.matmul(
                                    osl, ptb[:, 128 * kap:128 * (kap + 1)],
                                    v_sb[:, 1024 * kap + 64 * h:
                                         1024 * kap + 64 * (h + 1)],
                                    start=(kap == 0), stop=(kap == nk4 - 1))
                            sTO.inc(mm)
                        if 0 <= j - 1 < NIT:
                            i = j - 1
                            h, qt = hq(i)
                            for kc in range(nkc(i)):
                                kcum = KCUM[i] + kc
                                sE.wait(e, b["E"] + kcum + 1)
                                if kcum >= 2:
                                    sSE.wait(e, b["SE"] + kcum - 1)
                                psl = P_sb
                                for jj in range(4):
                                    mm = e.transpose(
                                        pt_ps2[kcum % 2][:, 128 * jj:
                                                         128 * (jj + 1)],
                                        psl[:, 512 * kc + 128 * jj:
                                            512 * kc + 128 * (jj + 1)],
                                        identb[:])
                                sT2.inc(mm)

                @block.vector
                def _(e):
                    dmask.wait(e)
                    for j in range(NIT + 2):
                        if j < NIT:
                            h, qt = hq(j)
                            qg = qt // 4
                            sTS.wait(e, b["TS"] + j + 1)
                            e.tensor_tensor(s_ps[:, 512 * qg:512 * (qg + 1)],
                                            s_ps[:, 512 * qg:512 * (qg + 1)],
                                            masks_sb[:, 512 * (qt % 4):
                                                     512 * (qt % 4 + 1)],
                                            op=ALU.add)
                            o = 16 + 4 * (j % 2)
                            e.tensor_reduce(scr[:, o:o + 1],
                                            s_ps[:, 0:512 * (qg + 1)],
                                            axis=AX.X, op=ALU.max, negate=True)
                            e.drain()
                            vm = e.tensor_scalar(scr[:, 24 + (j % 2):25 + (j % 2)],
                                                 scr[:, o:o + 1], 0.125, None,
                                                 op0=ALU.mult)
                            sVM.inc(vm)
                        if 0 <= j - 1 < NIT:
                            i = j - 1
                            sE.wait(e, b["E"] + KCUM[i + 1])
                            so = 26 + 3 * (i % 2)
                            e.tensor_reduce(scr[:, so + 1:so + 2],
                                            sums[:, 4 * (i % 2):
                                                 4 * (i % 2) + nkc(i)],
                                            axis=AX.X, op=ALU.add)
                            e.drain()
                            vs = e.reciprocal(scr[:, so:so + 1],
                                              scr[:, so + 1:so + 2])
                            sV2S.inc(vs)
                        if 0 <= j - 2 < NIT:
                            i = j - 2
                            h, qt = hq(i)
                            sTO.wait(e, b["TO"] + i + 1)
                            so = 26 + 3 * (i % 2)
                            osl = o_ps2[i % 2][:, 0:64]
                            vn = e.tensor_scalar(
                                buf2[:, 1024 * qt + 64 * h:1024 * qt + 64 * (h + 1)],
                                osl, scr[:, so:so + 1], None, op0=ALU.mult)
                            sV2N.inc(vn)

                @block.scalar
                def _(e):
                    for j in range(NIT + 1):
                        if 0 <= j - 1 < NIT:
                            i = j - 1
                            if i >= 1:
                                sTO.wait(e, b["TO"] + i)
                            for kc in range(nkc(i)):
                                kcum = KCUM[i] + kc
                                sT2.wait(e, b["T2"] + kcum + 1)
                                se2 = e.copy(
                                    PT_sb[:, 512 * kc:512 * (kc + 1)],
                                    pt_ps2[kcum % 2][:, 0:512])
                                sSE.inc(se2)
                        if j < NIT:
                            i = j
                            h, qt = hq(i)
                            sVM.wait(e, b["VM"] + i + 1)
                            if i >= 1:
                                sT2.wait(e, b["T2"] + KCUM[i])
                            psl = P_sb
                            for kc in range(nkc(i)):
                                se_ = e.activation(
                                    psl[:, 512 * kc:512 * (kc + 1)],
                                    s_ps[:, 512 * kc:512 * (kc + 1)], AF.Exp,
                                    bias=scr[:, 24 + (i % 2):25 + (i % 2)],
                                    scale=0.125,
                                    accum_out=sums[:, 4 * (i % 2) + kc:
                                                   4 * (i % 2) + kc + 1])
                                sE.inc(se_)

                if "otm" in dbg:
                    @block.sync
                    def _(e):
                        sV2N.wait(e)
                        sto.dinc(e.dma_start(out=dbg_aps["otm"], in_=buf2[:]))

                t0b, s20b = ten.n, sca2.n

                @block.tensor
                def _(e):
                    sV2N.wait(e)
                    for t in range(NT):
                        if t >= 1:
                            sca2.wait(e, s20b + t)
                        for j in range(8):
                            mm = e.transpose(
                                pt_ps2[0][:, 128 * j:128 * (j + 1)],
                                buf2[:, 1024 * t + 128 * j:1024 * t + 128 * (j + 1)],
                                identb[:])
                        ten.inc(mm)

                @block.scalar
                def _(e):
                    for t in range(NT):
                        ten.wait(e, t0b + t + 1)
                        dst = buf1.ap().rearrange("p (j c) -> p j c", c=2048)[
                            :, :, 128 * t:128 * (t + 1)]
                        srcp = pt_ps2[0].ap().rearrange("p (j c) -> p j c", c=128)
                        sca2.inc(e.copy(dst, srcp))

                if "oT" in dbg:
                    @block.sync
                    def _(e):
                        sca2.wait(e)
                        sto.dinc(e.dma_start(out=dbg_aps["oT"], in_=buf1[:]))

                xb2 = xld.n
                xfb2 = xfree.n

                @block.sync
                def _(e):
                    sca2.wait(e)
                    for n in range(2):
                        for t in range(NT):
                            m = 16 * n + t
                            if m >= 1:
                                xfree.wait(e, xfb2 + m)
                            xld.dinc(e.dma_start(
                                out=xr[:, 0:512],
                                in_=x_in[128 * t:128 * (t + 1),
                                         512 * n:512 * (n + 1)]))

                t0c, v0c = ten.n, vec.n
                wofree = fwm[0]

                @block.gpsimd
                def _(e):
                    srcw = wo_d.rearrange("p (k c) -> p k c", c=1024)
                    wof0 = wofree.n
                    for n in range(2):
                        if n == 1:
                            wofree.wait(e, wof0 + 1)
                        dwo.dinc(e.dma_start(out=wo_sb[:],
                                             in_=srcw[:, :, 512 * n:512 * (n + 1)]))

                wowait = {}

                @block.tensor
                def _(e):
                    sca2.wait(e)
                    for n in range(2):
                        e.wait_ge(dwo.h, 16 + 16 * n)
                        for t in range(NT):
                            m = 16 * n + t
                            if m >= 2:
                                vec.wait(e, v0c + (m - 2) + 1)
                            for k in range(8):
                                mm = e.matmul(
                                    s_ps[:, 512 * (m % 2):512 * (m % 2) + 512],
                                    buf1[:, 2048 * k + 128 * t:
                                         2048 * k + 128 * (t + 1)],
                                    wo_sb[:, 512 * k:512 * (k + 1)],
                                    start=(k == 0), stop=(k == 7))
                            if n == 0 and t == NT - 1:
                                wofree.inc(mm)
                                wowait[m] = (wofree.h, wofree.n)
                            else:
                                ten.inc(mm)
                                wowait[m] = (ten.h, ten.n)

                @block.vector
                def _(e):
                    for n in range(2):
                        for t in range(NT):
                            m = 16 * n + t
                            xld.wait(e, xb2 + 16 * (m + 1))
                            e.wait_ge(*wowait[m])
                            asl = s_ps[:, 512 * (m % 2):512 * (m % 2) + 512]
                            vec.inc(e.tensor_tensor(
                                out_t[:, 512 * (m % 2):512 * (m % 2) + 512],
                                asl, condt[:, 2048 + 512 * n:
                                           2048 + 512 * (n + 1)],
                                op=ALU.mult))
                            vv = e.tensor_tensor(
                                buf2[:, 1024 * t + 512 * n:1024 * t + 512 * (n + 1)],
                                out_t[:, 512 * (m % 2):512 * (m % 2) + 512],
                                xr[:, 0:512],
                                op=ALU.add)
                            xfree.inc(vv)

                if "x1" in dbg:
                    @block.sync
                    def _(e):
                        vec.wait(e)
                        sto.dinc(e.dma_start(out=dbg_aps["x1"], in_=buf2[:]))

            barrier()

        ph4 = contextlib.ExitStack()
        with ph4:
            sb4 = lambda n, s, d_: ph4.enter_context(nc.sbuf_tensor(n, s, d_))
            hpre4 = sb4("hpre4", [128, 1024], F32)
            htile4 = sb4("htile4", [128, 2 * 1024], BF16)
            lg = sb4("lg", [128, 8], F32)
            tr_ps4 = ph4.enter_context(nc.psum_tensor("tr_ps4", [128, 1024], BF16))
            lg_ps = ph4.enter_context(nc.psum_tensor("lg_ps", [128, 512], F32))
            bc_ps4 = ph4.enter_context(nc.psum_tensor("bc_ps4", [128, 512], F32))

            rows4 = [4, 3, 5]
            dxb4 = dx.n
            tb4, sb24 = ten.n, sca2.n

            @block.sync
            def _(e):
                for i, row in enumerate(rows4):
                    for n in range(2):
                        j = 2 * i + n
                        if j >= 1:
                            ten.wait(e, tb4 + j)
                        dx.dinc(e.dma_start(
                            out=crow[:],
                            in_=condscr_d[:, 1024 * row + 512 * n:
                                          1024 * row + 512 * (n + 1)]))

            @block.tensor
            def _(e):
                for i, row in enumerate(rows4):
                    for n in range(2):
                        j = 2 * i + n
                        dx.wait(e, dxb4 + 16 * (j + 1))
                        if j >= 1:
                            sca2.wait(e, sb24 + j)
                        ten.inc(e.matmul(bc_ps4[:, 0:512], ones_f[:],
                                         crow[0:1, 0:512],
                                         start=True, stop=True))

            @block.scalar
            def _(e):
                for i, row in enumerate(rows4):
                    slot = [0, 1, 2][i]
                    for n in range(2):
                        j = 2 * i + n
                        ten.wait(e, tb4 + j + 1)
                        sca2.inc(e.copy(condt[:, 1024 * slot + 512 * n:
                                              1024 * slot + 512 * (n + 1)],
                                        bc_ps4[:, 0:512]))

            cb_done_t, cb_done_s = ten.n, sca2.n

            def get_x2(t):
                def wt(eng):
                    eng.wait_ge(sca2.h, cb_done_s)
                return buf2[:, 1024 * t:1024 * (t + 1)], wt

            emit_ln(get_x2, 0, 1024, buf1, tr_ps4, hpre4, htile4, False)

            h2_done_s = sca2.n

            if "h2T" in dbg:
                @block.sync
                def _(e):
                    sca2.wait(e)
                    sto.dinc(e.dma_start(out=dbg_aps["h2T"], in_=buf1[:]))

            t0r, v0r = ten.n, vec.n

            @block.tensor
            def _(e):
                sca2.wait(e, h2_done_s)
                for t in range(NT):
                    if t >= 1:
                        vec.wait(e, v0r + 2 * (t - 1) + 1)
                    for k in range(8):
                        mm = e.matmul(lg_ps[:, 0:8], buf1[:, 2048 * k + 128 * t:
                                                     2048 * k + 128 * (t + 1)],
                                      rwt_b[:, 8 * k:8 * (k + 1)],
                                      start=(k == 0), stop=(k == 7))
                    ten.inc(mm)

            s0r = sca.n

            @block.vector
            def _(e):
                for t in range(NT):
                    ten.wait(e, t0r + t + 1)
                    e.tensor_tensor(lg[:], lg_ps[:, 0:8], ebias_t[:], op=ALU.add)
                    e.drain()
                    e.tensor_reduce(scr[:, 0:1], lg[:], axis=AX.X, op=ALU.max,
                                    negate=True)
                    e.drain()
                    e.tensor_scalar(scr[:, 1:2], scr[:, 0:1], -1.0, None,
                                    op0=ALU.mult)
                    e.drain()
                    e.tensor_scalar(scr[:, 8:16], lg[:], scr[:, 1:2], None,
                                    op0=ALU.is_equal)
                    e.drain()
                    e.scalar_tensor_tensor(scr[:, 16:24], scr[:, 8:16], -1.0e30,
                                           lg[:], op0=ALU.mult, op1=ALU.add)
                    e.drain()
                    e.tensor_reduce(scr[:, 2:3], scr[:, 16:24], axis=AX.X,
                                    op=ALU.max, negate=True)
                    e.drain()
                    vv = e.tensor_tensor(scr[:, 3:4], scr[:, 0:1], scr[:, 2:3],
                                         op=ALU.subtract)
                    vec.inc(vv)
                    sca.wait(e, s0r + t + 1)
                    e.tensor_scalar(scr[:, 5:6], scr[:, 4:5], 1.0, None, op0=ALU.add)
                    e.drain()
                    e.reciprocal(scr[:, 6:7], scr[:, 5:6])
                    e.drain()
                    e.tensor_tensor(scr[:, 7:8], scr[:, 4:5], scr[:, 6:7],
                                    op=ALU.mult)
                    e.tensor_scalar(scr[:, 3:4], scr[:, 2:3], -1.0, None,
                                    op0=ALU.mult)
                    e.drain()
                    e.tensor_scalar(scr[:, 24:32], scr[:, 16:24], scr[:, 3:4], None,
                                    op0=ALU.is_equal)
                    e.tensor_scalar(scr[:, 8:16], scr[:, 8:16], scr[:, 6:7], None,
                                    op0=ALU.mult)
                    e.drain()
                    e.tensor_scalar(scr[:, 24:32], scr[:, 24:32], scr[:, 7:8], None,
                                    op0=ALU.mult)
                    e.drain()
                    vv2 = e.tensor_tensor(cwbuf[:, 8 * t:8 * (t + 1)], scr[:, 8:16],
                                          scr[:, 24:32], op=ALU.add)
                    vec.inc(vv2)

            @block.scalar
            def _(e):
                for t in range(NT):
                    vec.wait(e, v0r + 2 * t + 1)
                    sca.inc(e.activation(scr[:, 4:5], scr[:, 3:4], AF.Exp))

            if "cw" in dbg:
                @block.sync
                def _(e):
                    vec.wait(e)
                    sto.dinc(e.dma_start(out=dbg_aps["cw"], in_=cwbuf[:]))

        barrier()
        if stop_after < 5:
            @block.sync
            def _(e):
                sto.wait(e)
            return nc, dbg_aps

        ph5 = contextlib.ExitStack()
        with ph5:
            sb5 = lambda n, s, d_: ph5.enter_context(nc.sbuf_tensor(n, s, d_))
            w1r = [sb5("w1r0", [128, 4096], BF16), sb5("w1r1", [128, 4096], BF16)]
            w3r = [sb5("w3r0", [128, 4096], BF16), sb5("w3r1", [128, 4096], BF16)]
            w2b = sb5("w2b", [128, 16384], BF16)
            ypb = sb5("ypb", [128, 16384], BF16)
            accb = sb5("accb", [128, 8192], BF16)
            sg = sb5("sg", [128, 1024], F32)
            g_ps = ph5.enter_context(nc.psum_tensor("g_ps", [128, 1024], F32))
            u_ps = ph5.enter_context(nc.psum_tensor("u_ps", [128, 1024], F32))
            f_ps = ph5.enter_context(nc.psum_tensor("f_ps", [128, 1024], F32))

            nq = 0

            @block.gpsimd
            def _(e):
                for C in range(2):
                    for ei in range(NE):
                        ce = C * NE + ei
                        for q in range(4):
                            qi = ce * 4 + q
                            s_ = qi % 2
                            if qi >= 2:
                                fw1[s_].wait(e, (qi - 2) // 2 + 1)
                                fw3[s_].wait(e, (qi - 2) // 2 + 1)
                            src1 = w1_d[ei].rearrange("p (k c) -> p k c", c=2048)[
                                :, :, 512 * q:512 * (q + 1)]
                            src3 = w3_d[ei].rearrange("p (k c) -> p k c", c=2048)[
                                :, :, 512 * q:512 * (q + 1)]
                            dw1[s_].dinc(e.dma_start(out=w1r[s_][:], in_=src1))
                            dw3[s_].dinc(e.dma_start(out=w3r[s_][:], in_=src3))
                            if q == 1:
                                if ce >= 1:
                                    fw2.wait(e, ce)
                                src2 = w2_d[ei].rearrange("p (k c) -> p k c",
                                                          c=1024)
                                dw2.dinc(e.dma_start(out=w2b[:, 0:8192],
                                                     in_=src2[:, 0:8, :]))
                                dw2.dinc(e.dma_start(out=w2b[:, 8192:16384],
                                                     in_=src2[:, 8:16, :]))

            bG, bSi, bYp, bF, bAC = sG.n, sSi.n, sYp.n, sF.n, sAC.n

            guwait = {}
            fwait = {}

            @block.tensor
            def _(e):
                for C in range(2):
                    for ei in range(NE):
                        ce = C * NE + ei
                        for q in range(4):
                            qi = ce * 4 + q
                            s_ = qi % 2
                            dw1[s_].wait(e, 16 * (qi // 2 + 1))
                            dw3[s_].wait(e, 16 * (qi // 2 + 1))
                            for hc4 in range(4):
                                Hc = 4 * q + hc4
                                for n in range(2):
                                    gi = (ce * 16 + Hc) * 2 + n
                                    if gi >= 2:
                                        sYp.wait(e, bYp + gi - 1)
                                    sl = gi % 2
                                    last_q = hc4 == 3 and n == 1
                                    for k in range(8):
                                        gmm = e.matmul(
                                            g_ps[:, 512 * sl:512 * (sl + 1)],
                                            w1r[s_][:, 512 * k + 128 * hc4:
                                                    512 * k + 128 * (hc4 + 1)],
                                            buf1[:, 2048 * k + 1024 * C + 512 * n:
                                                 2048 * k + 1024 * C + 512 * (n + 1)],
                                            start=(k == 0), stop=(k == 7))
                                        mm = e.matmul(
                                            u_ps[:, 512 * sl:512 * (sl + 1)],
                                            w3r[s_][:, 512 * k + 128 * hc4:
                                                    512 * k + 128 * (hc4 + 1)],
                                            buf1[:, 2048 * k + 1024 * C + 512 * n:
                                                 2048 * k + 1024 * C + 512 * (n + 1)],
                                            start=(k == 0), stop=(k == 7))
                                    if last_q:
                                        fw1[s_].inc(gmm)
                                        fw3[s_].inc(mm)
                                        guwait[gi] = (fw3[s_].h, fw3[s_].n)
                                    else:
                                        sG.inc(mm)
                                        guwait[gi] = (sG.h, sG.n)
                        dw2.wait(e, 32 * (ce + 1))
                        sYp.wait(e, bYp + (ce + 1) * 32)
                        for dout in range(2):
                            for tt in range(8):
                                fi = ce * 16 + dout * 8 + tt
                                if fi >= 2:
                                    sAC.wait(e, bAC + fi - 1)
                                sl = fi % 2
                                for Hc in range(16):
                                    mm = e.matmul(
                                        f_ps[:, 512 * sl:512 * (sl + 1)],
                                        ypb[:, 1024 * Hc + 128 * tt:
                                            1024 * Hc + 128 * (tt + 1)],
                                        w2b[:, 1024 * Hc + 512 * dout:
                                            1024 * Hc + 512 * (dout + 1)],
                                        start=(Hc == 0), stop=(Hc == 15))
                                if dout == 1 and tt == 7:
                                    fw2.inc(mm)
                                    fwait[fi] = (fw2.h, fw2.n)
                                else:
                                    sF.inc(mm)
                                    fwait[fi] = (sF.h, sF.n)

            @block.scalar
            def _(e):
                for C in range(2):
                    for ei in range(NE):
                        ce = C * NE + ei
                        for Hc in range(16):
                            for n in range(2):
                                gi = (ce * 16 + Hc) * 2 + n
                                e.wait_ge(*guwait[gi])
                                if gi >= 2:
                                    sYp.wait(e, bYp + gi - 1)
                                sl = gi % 2
                                ss = e.activation(sg[:, 512 * sl:512 * (sl + 1)],
                                                  g_ps[:, 512 * sl:512 * (sl + 1)],
                                                  AF.Silu)
                                sSi.inc(ss)
                        if ei == 0:
                            for dout in range(2):
                                for tt in range(8):
                                    fi = ce * 16 + dout * 8 + tt
                                    e.wait_ge(*fwait[fi])
                                    sl = fi % 2
                                    sc = e.copy(
                                        accb[:, 1024 * tt + 512 * dout:
                                             1024 * tt + 512 * (dout + 1)],
                                        f_ps[:, 512 * sl:512 * (sl + 1)])
                                    sAC.inc(sc)

            bOUT = sV2S.n
            sto_out_base = sto.n

            @block.vector
            def _(e):
                for C in range(2):
                    for ei in range(NE):
                        ce = C * NE + ei
                        for Hc in range(16):
                            for n in range(2):
                                gi = (ce * 16 + Hc) * 2 + n
                                if Hc == 0 and n == 0 and ce >= 1:
                                    e.wait_ge(*fwait[ce * 16 - 1])
                                sSi.wait(e, bSi + gi + 1)
                                sl = gi % 2
                                yy = e.tensor_tensor(
                                    ypb[:, 1024 * Hc + 512 * n:
                                        1024 * Hc + 512 * (n + 1)],
                                    sg[:, 512 * sl:512 * (sl + 1)],
                                    u_ps[:, 512 * sl:512 * (sl + 1)],
                                    op=ALU.mult)
                                sYp.inc(yy)
                        if ei > 0:
                            for dout in range(2):
                                for tt in range(8):
                                    fi = ce * 16 + dout * 8 + tt
                                    e.wait_ge(*fwait[fi])
                                    if ei == 1 and dout == 0 and tt == 0:
                                        sAC.wait(e, bAC + 144 * C + 16)
                                    sl = fi % 2
                                    cwcol = cwbuf[:, 8 * (8 * C + tt) + (ei - 1):
                                                  8 * (8 * C + tt) + ei]
                                    acsl = accb[:, 1024 * tt + 512 * dout:
                                                1024 * tt + 512 * (dout + 1)]
                                    va = e.scalar_tensor_tensor(
                                        acsl, f_ps[:, 512 * sl:512 * (sl + 1)],
                                        cwcol, acsl, op0=ALU.mult, op1=ALU.add)
                                    sAC.inc(va)
                    for tt in range(8):
                        t = 8 * C + tt
                        oidx = t
                        if oidx >= 2:
                            (sto if oidx % 2 == 0 else sto2).wait(
                                e, sto_out_base + 16 * ((oidx - 2) // 2 + 1)
                                if oidx % 2 == 0 else 16 * ((oidx - 2) // 2 + 1))
                        e.tensor_tensor(out_t[:, 1024 * (tt % 2):
                                              1024 * (tt % 2) + 1024],
                                        accb[:, 1024 * tt:1024 * (tt + 1)],
                                        condt[:, 2048:3072], op=ALU.mult)
                        vo = e.tensor_tensor(
                            out_t[:, 1024 * (tt % 2):1024 * (tt % 2) + 1024],
                            out_t[:, 1024 * (tt % 2):1024 * (tt % 2) + 1024],
                            buf2[:, 1024 * t:1024 * (t + 1)], op=ALU.add)
                        sV2S.inc(vo)

            @block.sync
            def _(e):
                for t in range(NT):
                    sV2S.wait(e, bOUT + t + 1)
                    sm = sto if t % 2 == 0 else sto2
                    sm.dinc(e.dma_start(
                        out=y_out[128 * t:128 * (t + 1), :],
                        in_=out_t[:, 1024 * (t % 2):1024 * (t % 2) + 1024]))
                sto.wait(e)
                sto2.wait(e)


    return nc, dbg_aps


def _blk(w):
    K, C = w.shape
    return np.ascontiguousarray(
        w.reshape(K // 128, 128, C).transpose(1, 0, 2).reshape(128, (K // 128) * C))


def _host_consts():
    ident = np.eye(128, dtype=np.float32)
    posc = (np.arange(16)[None, :] * 128 + np.arange(128)[:, None]).astype(np.float32)
    gam = np.asarray(RIEMANN, dtype=np.float32)
    invf = (1.0 / (gam / gam[0])).astype(np.float32)
    ifr = np.tile(invf, 16)[None, :].astype(np.float32)
    masks = np.zeros((128, 2048), dtype=np.float32)
    r = np.arange(128)[:, None]
    jj = np.arange(512)[None, :]
    for d in range(4):
        masks[:, 512 * d:512 * (d + 1)] = np.where(jj <= 128 * d + r, 0.0, -1.0e30)
    return ident, posc, ifr, masks


_CACHE = {}


def _get_built(dbg=(), stop_after=99):
    key = (tuple(sorted(dbg)), stop_after)
    if key not in _CACHE:
        _CACHE[key] = build_program(dbg, stop_after)
    return _CACHE[key]


def kernel(L0_attn_x, L0_attn_w_qkv, L0_attn_w_o, L0_attn_log_scale,
           L0_ffn_router_w, L0_ffn_experts_bias, L0_ffn_w1, L0_ffn_w3, L0_ffn_w2,
           L0_ffn_sw1, L0_ffn_sw3, L0_ffn_sw2, L0_c, L0_wmod, _dbg=(),
           _stop_after=99, _ncores=8):
    f32 = lambda a: np.ascontiguousarray(np.asarray(a, dtype=np.float32))
    x = f32(L0_attn_x)
    ident, posc, ifr, masks = _host_consts()
    wqkv = _blk(f32(L0_attn_w_qkv))
    wo = _blk(f32(L0_attn_w_o))
    wmod = _blk(f32(L0_wmod))
    rwt = _blk(f32(L0_ffn_router_w).T)
    w1 = np.stack([_blk(w) for w in
                   np.concatenate([f32(L0_ffn_sw1), f32(L0_ffn_w1)], axis=0)])
    w3 = np.stack([_blk(w) for w in
                   np.concatenate([f32(L0_ffn_sw3), f32(L0_ffn_w3)], axis=0)])
    w2 = np.stack([_blk(w) for w in
                   np.concatenate([f32(L0_ffn_sw2), f32(L0_ffn_w2)], axis=0)])
    ls = f32(L0_attn_log_scale).reshape(1, 16)
    ebias = f32(L0_ffn_experts_bias).reshape(1, 8)
    cvec = f32(L0_c).reshape(8, 8, 128)

    nc, dbg_aps = _get_built(_dbg, _stop_after)
    in_maps = []
    for b in range(_ncores):
        in_maps.append({
            "x": x[b], "wqkv": wqkv, "wo": wo, "ls": ls, "rwt": rwt,
            "ebias": ebias, "w1": w1, "w3": w3, "w2": w2, "cvec": cvec[b],
            "wmod": wmod, "ident": ident, "posc": posc, "ifr": ifr,
            "masks": masks,
        })
    res = run_bass_kernel_spmd(nc, in_maps, core_ids=list(range(_ncores)))
    out = np.stack([np.asarray(res.results[b]["y"]) for b in range(_ncores)]
                   + [np.zeros((2048, 1024), np.float32)] * (8 - _ncores), axis=0)
    if _dbg:
        dbgres = [{k: np.asarray(res.results[b]["dbg_" + k]) for k in _dbg}
                  for b in range(_ncores)]
        return out.astype(np.float32), dbgres
    return out.astype(np.float32)


# version 03
# speedup vs baseline: 1.1910x; 1.0036x over previous
import contextlib
import math
import numpy as np
import ml_dtypes
import concourse.bass as bass
import concourse.mybir as mybir
from concourse.bass_utils import run_bass_kernel_spmd

F32 = mybir.dt.float32
BF16 = mybir.dt.bfloat16
I32 = mybir.dt.int32
AF = mybir.ActivationFunctionType
ALU = mybir.AluOpType
AX = mybir.AxisListType

PI = math.pi
INV2PI = 1.0 / (2.0 * math.pi)
RC1 = 6.28125
RC2 = 2.0 * math.pi - RC1
NT = 16
NH = 16
NE = 9
NIT = NH * NT

RIEMANN = (14.134725142, 21.022039639, 25.01085758, 30.424876126, 32.935061588,
        37.586178159, 40.918719012, 43.327073281, 48.005150881, 49.773832478,
        52.970321478, 56.446247697, 59.347044003, 60.831778525, 65.112544048,
        67.079810529, 69.546401711, 72.067157674, 75.704690699, 77.144840069,
        79.33737502, 82.910380854, 84.735492981, 87.425274613, 88.809111208,
        92.491899271, 94.651344041, 95.870634228, 98.831194218, 101.317851006,
        103.72553804, 105.446623052)


class Cnt:
    def __init__(self, h):
        self.h = h
        self.n = 0

    def inc(self, instr, k=1):
        instr.then_inc(self.h, k)
        self.n += k
        return instr

    def dinc(self, instr):
        return self.inc(instr, 16)

    def wait(self, eng, n=None):
        eng.wait_ge(self.h, self.n if n is None else n)


def nkc(i):
    return (i % NT) // 4 + 1


KCUM = [0]
for _i in range(NIT):
    KCUM.append(KCUM[-1] + nkc(_i))


def build_program(dbg=(), stop_after=99):
    nc = bass.Bass("TRN2", target_bir_lowering=False,
                   detect_race_conditions=False)

    x_in = nc.dram_tensor("x", [2048, 1024], F32, kind="ExternalInput").ap()
    wqkv_d = nc.dram_tensor("wqkv", [128, 8 * 3072], F32, kind="ExternalInput").ap()
    wo_d = nc.dram_tensor("wo", [128, 8 * 1024], F32, kind="ExternalInput").ap()
    ls_d = nc.dram_tensor("ls", [1, 16], F32, kind="ExternalInput").ap()
    rwt_d = nc.dram_tensor("rwt", [128, 64], F32, kind="ExternalInput").ap()
    ebias_d = nc.dram_tensor("ebias", [1, 8], F32, kind="ExternalInput").ap()
    w1_d = nc.dram_tensor("w1", [NE, 128, 8 * 2048], F32, kind="ExternalInput").ap()
    w3_d = nc.dram_tensor("w3", [NE, 128, 8 * 2048], F32, kind="ExternalInput").ap()
    w2_d = nc.dram_tensor("w2", [NE, 128, 16 * 1024], F32, kind="ExternalInput").ap()
    c_d = nc.dram_tensor("cvec", [8, 128], F32, kind="ExternalInput").ap()
    wmod_d = nc.dram_tensor("wmod", [128, 8 * 6144], F32, kind="ExternalInput").ap()
    ident_d = nc.dram_tensor("ident", [128, 128], F32, kind="ExternalInput").ap()
    posc_d = nc.dram_tensor("posc", [128, 16], F32, kind="ExternalInput").ap()
    ifr_d = nc.dram_tensor("ifr", [1, 512], F32, kind="ExternalInput").ap()
    masks_d = nc.dram_tensor("masks", [128, 2048], F32, kind="ExternalInput").ap()
    modscr_d = nc.dram_tensor("modscr", [48, 128], F32, kind="Internal").ap()
    condscr_d = nc.dram_tensor("condscr", [1, 6144], F32, kind="Internal").ap()
    y_out = nc.dram_tensor("y", [2048, 1024], F32, kind="ExternalOutput").ap()

    dbg_aps = {}
    for name, shape, dt in [
            ("cond", [1, 6144], F32), ("hT", [128, 16384], BF16),
            ("qT", [128, 16384], BF16), ("kT", [128, 16384], BF16),
            ("v", [128, 16384], BF16), ("otm", [128, 16384], BF16),
            ("oT", [128, 16384], BF16), ("x1", [128, 16384], BF16),
            ("h2T", [128, 16384], BF16), ("cw", [128, 128], F32)]:
        if name in dbg:
            dbg_aps[name] = nc.dram_tensor("dbg_" + name, shape, dt,
                                           kind="ExternalOutput").ap()

    gst = contextlib.ExitStack()
    with gst:
        sb = lambda n, s, d_: gst.enter_context(nc.sbuf_tensor(n, s, d_))
        sem = lambda n: Cnt(gst.enter_context(nc.semaphore(n)))

        identf = sb("identf", [128, 128], F32)
        identb = sb("identb", [128, 128], BF16)
        crow = sb("crow", [1, 512], F32)
        condt = sb("condt", [128, 3 * 1024], BF16)
        ebias_t = sb("ebias_t", [128, 8], F32)
        ifs_tile = sb("ifs_tile", [128, 512], F32)
        posc_sb = sb("posc_sb", [128, 16], F32)
        rwt_f = sb("rwt_f", [128, 64], F32)
        rwt_b = sb("rwt_b", [128, 64], BF16)
        ebias_r = sb("ebias_r", [1, 8], F32)
        cwbuf = sb("cwbuf", [128, 128], F32)
        dump = sb("dump", [128, 1024], BF16)
        scr = sb("scr", [128, 32], F32)
        ones_f = sb("ones_f", [1, 128], F32)
        epscol = sb("epscol", [128, 1], F32)
        junk = sb("junk", [1, 2], F32)
        buf1 = sb("buf1", [128, 16384], BF16)
        buf2 = sb("buf2", [128, 16384], BF16)
        out_t = sb("out_t", [128, 2 * 1024], F32)

        ph = sem("ph")
        dx = sem("dx")
        dwm = [sem("dwm0"), sem("dwm1")]
        fwm = [sem("fwm0"), sem("fwm1")]
        dwq = [sem("dwq0"), sem("dwq1")]
        fwq = [sem("fwq0"), sem("fwq1")]
        dwo = sem("dwo")
        dident = sem("dident")
        dmask = sem("dmask")
        sto2 = sem("sto2")
        dw1 = [sem("dw10"), sem("dw11")]
        fw1 = [sem("fw10"), sem("fw11")]
        dw3 = [sem("dw30"), sem("dw31")]
        fw3 = [sem("fw30"), sem("fw31")]
        dw2 = sem("dw2")
        fw2 = sem("fw2")
        ten = sem("ten")
        vec = sem("vec")
        sca = sem("sca")
        sca2 = sem("sca2")
        xld = sem("xld")
        xfree = sem("xfree")
        sto = sem("sto")
        sTS = sem("sTS")
        sVM = sem("sVM")
        sE = sem("sE")
        sV2S = sem("sV2S")
        sT2 = sem("sT2")
        sSE = sem("sSE")
        sTO = sem("sTO")
        sV2N = sem("sV2N")
        sG = sem("sG")
        sSi = sem("sSi")
        sYp = sem("sYp")
        sF = sem("sF")
        sAC = sem("sAC")

        block = gst.enter_context(nc.Block())
        phase_no = [0]

        def barrier():
            phase_no[0] += 1
            tgt = 5 * phase_no[0]

            @block.sync
            def _(e):
                sto.wait(e)
                sto2.wait(e)
                ph.inc(e.nop())
                e.wait_ge(ph.h, tgt)

            @block.gpsimd
            def _(e):
                ph.inc(e.engine_nop())
                e.wait_ge(ph.h, tgt)

            @block.tensor
            def _(e):
                ph.inc(e.ldweights(identb[:, 0:1]))
                e.wait_ge(ph.h, tgt)

            @block.vector
            def _(e):
                ph.inc(e.engine_nop())
                e.wait_ge(ph.h, tgt)

            @block.scalar
            def _(e):
                ph.inc(e.copy(junk[0:1, 0:1], junk[0:1, 1:2]))
                e.wait_ge(ph.h, tgt)

        ph0 = contextlib.ExitStack()
        with ph0:
            sb0 = lambda n, s, d_: ph0.enter_context(nc.sbuf_tensor(n, s, d_))
            c_sb = sb0("c_sb", [8, 128], F32)
            csil = sb0("csil", [8, 128], BF16)
            wmr = [sb0("wmr0", [128, 6144], BF16), sb0("wmr1", [128, 6144], BF16)]
            mod_sb = sb0("mod_sb", [128, 48], F32)
            modT = sb0("modT", [48, 128], F32)
            ls_sb = sb0("ls_sb", [1, 16], F32)
            els_sb = sb0("els_sb", [1, 16], F32)
            ifr_sb = sb0("ifr_sb", [1, 512], F32)
            sclrow = sb0("sclrow", [1, 512], F32)
            cT_bf = sb0("cT_bf", [128, 8], BF16)
            mod_ps = ph0.enter_context(nc.psum_tensor("mod_ps", [128, 48], F32))
            ct_ps = ph0.enter_context(nc.psum_tensor("ct_ps", [128, 8], BF16))
            mt_ps = ph0.enter_context(nc.psum_tensor("mt_ps", [48, 128], F32))
            bc_ps0 = ph0.enter_context(nc.psum_tensor("bc_ps0", [128, 512], F32))

            @block.sync
            def _(e):
                dx.dinc(e.dma_start(out=identf[:], in_=ident_d))
                dx.dinc(e.dma_start(out=c_sb[:], in_=c_d))
                dx.dinc(e.dma_start(out=ls_sb[:], in_=ls_d))
                dx.dinc(e.dma_start(out=posc_sb[:], in_=posc_d))
                dx.dinc(e.dma_start(out=ifr_sb[:], in_=ifr_d))
                dx.dinc(e.dma_start(out=ebias_r[:], in_=ebias_d))
                dx.dinc(e.dma_start(out=rwt_f[:], in_=rwt_d))

            @block.gpsimd
            def _(e):
                dident.dinc(e.dma_start(out=identb[:], in_=ident_d))
                for k in range(8):
                    s_ = k % 2
                    if k >= 2:
                        fwm[s_].wait(e, (k - 2) // 2 + 1)
                    dwm[s_].dinc(e.dma_start(
                        out=wmr[s_][:], in_=wmod_d[:, 6144 * k:6144 * (k + 1)]))

            @block.vector
            def _(e):
                e.memset(junk[:], 0.0)
                dx.wait(e)
                e.memset(ones_f[:], 1.0)
                e.memset(epscol[:], 1.0e-5)
                vec.inc(e.tensor_copy(rwt_b[:], rwt_f[:]))

            @block.scalar
            def _(e):
                vec.wait(e)
                e.activation(csil[:], c_sb[:], AF.Silu)
                sca.inc(e.activation(els_sb[:], ls_sb[:], AF.Exp))

            @block.tensor
            def _(e):
                sca.wait(e)
                dident.wait(e)
                ten.inc(e.transpose(ct_ps[:], csil[:], identb[0:8, 0:8]))

            @block.scalar
            def _(e):
                ten.wait(e)
                sca.inc(e.copy(cT_bf[:], ct_ps[:]))

            t_mv0, s_mv0, v_mv0 = ten.n, sca.n, vec.n

            @block.tensor
            def _(e):
                sca.wait(e)
                for k in range(8):
                    s_ = k % 2
                    dwm[s_].wait(e, 16 * (k // 2 + 1))
                    if k >= 1:
                        vec.wait(e, v_mv0 + k)
                    for m in range(48):
                        mm = e.matmul(mod_ps[:, m:m + 1],
                                      wmr[s_][:, 128 * m:128 * (m + 1)],
                                      cT_bf[:, k:k + 1],
                                      start=True, stop=True)
                    fwm[s_].inc(mm)

            @block.vector
            def _(e):
                for k in range(8):
                    e.wait_ge(fwm[k % 2].h, k // 2 + 1)
                    if k == 0:
                        vec.inc(e.tensor_copy(mod_sb[:], mod_ps[:]))
                    else:
                        vec.inc(e.tensor_tensor(mod_sb[:], mod_sb[:], mod_ps[:],
                                                op=ALU.add))
                    e.drain()

            @block.scalar
            def _(e):
                vec.wait(e)
                sca.inc(e.nop())

            @block.vector
            def _(e):
                sca.wait(e)
                e.tensor_scalar(mod_sb[:, 8:16], mod_sb[:, 8:16], 1.0, None,
                                op0=ALU.add)
                vec.inc(e.tensor_scalar(mod_sb[:, 32:40], mod_sb[:, 32:40], 1.0,
                                        None, op0=ALU.add))
                e.drain()

            @block.scalar
            def _(e):
                vec.wait(e)
                e.sign(scr[:, 0:8], mod_sb[:, 16:24])
                e.sign(scr[:, 8:16], mod_sb[:, 40:48])
                e.activation(scr[:, 16:24], mod_sb[:, 16:24], AF.Abs)
                sca.inc(e.activation(scr[:, 24:32], mod_sb[:, 40:48], AF.Abs))

            @block.vector
            def _(e):
                sca.wait(e)
                e.tensor_scalar(scr[:, 16:32], scr[:, 16:32], 0.25, None,
                                op0=ALU.max)
                e.drain()
                e.tensor_tensor(mod_sb[:, 16:24], scr[:, 0:8], scr[:, 16:24],
                                op=ALU.mult)
                e.tensor_tensor(mod_sb[:, 40:48], scr[:, 8:16], scr[:, 24:32],
                                op=ALU.mult)
                e.drain()
                vec.inc(e.tensor_tensor(
                    sclrow.ap().rearrange("p (h q) -> p h q", h=16),
                    ifr_sb.ap().rearrange("p (h q) -> p h q", h=16),
                    els_sb.ap().rearrange("p (h o) -> p h o", o=1).broadcast_to([1, 16, 32]),
                    op=ALU.mult))

            @block.tensor
            def _(e):
                vec.wait(e)
                ten.inc(e.transpose(mt_ps[:], mod_sb[:], identf[:]))

            @block.scalar
            def _(e):
                ten.wait(e)
                sca.inc(e.copy(modT[:], mt_ps[:]))

            @block.sync
            def _(e):
                sca.wait(e)
                dx.dinc(e.dma_start(
                    out=condscr_d.rearrange("o (x r) -> (o x) r", r=128),
                    in_=modT[:]))

            rows03 = [1, 0, 2]
            dxb0 = dx.n
            tb0, sb20 = ten.n, sca2.n

            @block.sync
            def _(e):
                dx.wait(e, dxb0)
                for i, row in enumerate(rows03):
                    for n in range(2):
                        j = 2 * i + n
                        if j >= 1:
                            ten.wait(e, tb0 + j)
                        dx.dinc(e.dma_start(
                            out=crow[:],
                            in_=condscr_d[:, 1024 * row + 512 * n:
                                          1024 * row + 512 * (n + 1)]))

            @block.tensor
            def _(e):
                for i, row in enumerate(rows03):
                    for n in range(2):
                        j = 2 * i + n
                        dx.wait(e, dxb0 + 16 * (j + 1))
                        if j >= 1:
                            sca2.wait(e, sb20 + j)
                        ten.inc(e.matmul(bc_ps0[:, 0:512], ones_f[:],
                                         crow[0:1, 0:512],
                                         start=True, stop=True))
                sca2.wait(e, sb20 + 6)
                ten.inc(e.matmul(bc_ps0[:, 0:8], ones_f[:], ebias_r[:],
                                 start=True, stop=True))
                sca2.wait(e, sb20 + 7)
                ten.inc(e.matmul(bc_ps0[:, 0:512], ones_f[:], sclrow[:],
                                 start=True, stop=True))

            @block.scalar
            def _(e):
                for i, row in enumerate(rows03):
                    slot = [0, 1, 2][i]
                    for n in range(2):
                        j = 2 * i + n
                        ten.wait(e, tb0 + j + 1)
                        sca2.inc(e.copy(condt[:, 1024 * slot + 512 * n:
                                              1024 * slot + 512 * (n + 1)],
                                        bc_ps0[:, 0:512]))
                ten.wait(e, tb0 + 7)
                sca2.inc(e.copy(ebias_t[:], bc_ps0[:, 0:8]))
                ten.wait(e, tb0 + 8)
                sca2.inc(e.copy(ifs_tile[:], bc_ps0[:, 0:512]))

            if "cond" in dbg:
                @block.sync
                def _(e):
                    vec.wait(e)
                    sca2.wait(e)
                    sto.dinc(e.dma_start(out=dbg_aps["cond"], in_=condscr_d))

        barrier()
        if stop_after < 1:
            @block.sync
            def _(e):
                sto.wait(e)
            return nc, dbg_aps

        def emit_ln(get_x, scale_off, shift_off, dstT, tr_ps, hpre, htile,
                    use_xfree):
            v0, s0, t0, s20 = vec.n, sca.n, ten.n, sca2.n

            @block.vector
            def _(v):
                for t in range(NT):
                    xs, wt = get_x(t)
                    wt(v)
                    if t >= 2:
                        sca.wait(v, s0 + 3 * (t - 2) + 3)
                    o = 8 * (t % 2)
                    v.tensor_reduce(scr[:, o + 0:o + 1], xs, axis=AX.X, op=ALU.add)
                    v.drain()
                    vec.inc(v.tensor_scalar(scr[:, o + 1:o + 2], scr[:, o + 0:o + 1],
                                            -1.0 / 1024.0, None, op0=ALU.mult))
                    sca.wait(v, s0 + 3 * t + 2)
                    vec.inc(v.scalar_tensor_tensor(
                        scr[:, o + 4:o + 5], scr[:, o + 2:o + 3], 1.0 / 1024.0,
                        scr[:, o + 3:o + 4], op0=ALU.mult, op1=ALU.subtract))
                    sca.wait(v, s0 + 3 * t + 3)
                    v.reciprocal(scr[:, o + 6:o + 7], scr[:, o + 5:o + 6])
                    v.drain()
                    hp = v.tensor_scalar(hpre[:], xs, scr[:, o + 1:o + 2],
                                         scr[:, o + 6:o + 7], op0=ALU.add,
                                         op1=ALU.mult)
                    if use_xfree:
                        xfree.inc(hp)
                    if t >= 2:
                        ten.wait(v, t0 + (t - 2) + 1)
                    hsl = htile[:, 1024 * (t % 2):1024 * (t % 2) + 1024]
                    v.tensor_tensor(hpre[:], hpre[:],
                                    condt[:, scale_off:scale_off + 1024], op=ALU.mult)
                    vec.inc(v.tensor_tensor(hsl, hpre[:],
                                            condt[:, shift_off:shift_off + 1024],
                                            op=ALU.add))

            def evac(s, t):
                ten.wait(s, t0 + t + 1)
                dst = dstT.ap().rearrange("p (j c) -> p j c", c=2048)[
                    :, :, 128 * t:128 * (t + 1)]
                srcp = tr_ps.ap().rearrange("p (j c) -> p j c", c=128)
                sca2.inc(s.copy(dst, srcp))

            @block.scalar
            def _(s):
                for t in range(NT):
                    xs, wt = get_x(t)
                    wt(s)
                    o = 8 * (t % 2)
                    sca.inc(s.activation(dump[:], xs, AF.Square,
                                         accum_out=scr[:, o + 2:o + 3]))
                    vec.wait(s, v0 + 3 * t + 1)
                    sca.inc(s.activation(scr[:, o + 3:o + 4], scr[:, o + 1:o + 2],
                                         AF.Square))
                    vec.wait(s, v0 + 3 * t + 2)
                    sca.inc(s.activation(scr[:, o + 5:o + 6], scr[:, o + 4:o + 5],
                                         AF.Sqrt, bias=epscol[:, 0:1]))
                    if t >= 1:
                        evac(s, t - 1)
                evac(s, NT - 1)

            @block.tensor
            def _(t_):
                for t in range(NT):
                    vec.wait(t_, v0 + 3 * t + 3)
                    if t >= 1:
                        sca2.wait(t_, s20 + t)
                    hsl = htile[:, 1024 * (t % 2):1024 * (t % 2) + 1024]
                    for j in range(8):
                        mm = t_.transpose(tr_ps[:, 128 * j:128 * (j + 1)],
                                          hsl[:, 128 * j:128 * (j + 1)], identb[:])
                    ten.inc(mm)

        phB = contextlib.ExitStack()
        with phB:
            sbB = lambda n, s, d_: phB.enter_context(nc.sbuf_tensor(n, s, d_))
            qT = sbB("qT", [128, 16384], BF16)
            kT = sbB("kT", [128, 16384], BF16)
            v_sb = sbB("v_sb", [128, 16384], BF16)

            ph1 = contextlib.ExitStack()
            with ph1:
                hpre = ph1.enter_context(nc.sbuf_tensor("hpre", [128, 1024], F32))
                htile = ph1.enter_context(nc.sbuf_tensor("htile", [128, 2048], BF16))
                xr = ph1.enter_context(nc.sbuf_tensor("xr", [128, 2048], F32))
                tr_ps1 = ph1.enter_context(nc.psum_tensor("tr_ps1", [128, 1024], BF16))
                xb = xld.n
                xfb = xfree.n

                @block.sync
                def _(e):
                    for t in range(NT):
                        if t >= 2:
                            xfree.wait(e, xfb + t - 1)
                        xld.dinc(e.dma_start(
                            out=xr[:, 1024 * (t % 2):1024 * (t % 2) + 1024],
                            in_=x_in[128 * t:128 * (t + 1), :]))

                def get_x1(t):
                    def wt(eng):
                        eng.wait_ge(xld.h, xb + 16 * (t + 1))
                    return xr[:, 1024 * (t % 2):1024 * (t % 2) + 1024], wt

                emit_ln(get_x1, 0, 1024, buf1, tr_ps1, hpre, htile, True)

                if "hT" in dbg:
                    @block.sync
                    def _(e):
                        sca2.wait(e)
                        sto.dinc(e.dma_start(out=dbg_aps["hT"], in_=buf1[:]))

            barrier()
            if stop_after < 2:
                @block.sync
                def _(e):
                    sto.wait(e)
                return nc, dbg_aps

            ph2 = contextlib.ExitStack()
            with ph2:
                sb2 = lambda n, s, d_: ph2.enter_context(nc.sbuf_tensor(n, s, d_))
                wqr = [sb2("wqr0", [128, 4096], BF16), sb2("wqr1", [128, 4096], BF16)]
                tha = sb2("tha", [128, 1024], F32)
                trig = tha
                thf = out_t.ap()[:, 0:1024]
                ta = out_t.ap()[:, 1024:1536]
                tb = out_t.ap()[:, 1536:2048]
                qrot = sb2("qrot", [128, 2 * 1024], BF16)
                qkv_ps = ph2.enter_context(nc.psum_tensor("qkv_ps", [128, 2048], F32))
                tr_ps2 = ph2.enter_context(nc.psum_tensor("tr_ps2", [128, 1024], BF16))

                t0, t20, v0, v20, s0, s20 = (ten.n, sTS.n, vec.n, sVM.n, sca.n,
                                             sca2.n)
                tE0 = sE.n

                @block.gpsimd
                def _(e):
                    for tau in range(3):
                        for p in range(2):
                            s_ = p
                            if tau >= 1:
                                fwq[s_].wait(e, tau)
                            piece = 2 * tau + p
                            src = wqkv_d.rearrange("p (k c) -> p k c", c=3072)[
                                :, :, 512 * piece:512 * (piece + 1)]
                            dwq[s_].dinc(e.dma_start(out=wqr[s_][:, 0:2048],
                                                     in_=src[:, 0:4, :]))
                            dwq[s_].dinc(e.dma_start(out=wqr[s_][:, 2048:4096],
                                                     in_=src[:, 4:8, :]))

                mmwait = {}

                @block.tensor
                def _(e):
                    def transposes(tau, t):
                        sTS.wait(e, t20 + 32 * tau + 2 * t + 2)
                        if 16 * tau + t >= 1:
                            sca2.wait(e, s20 + 16 * tau + t)
                        for j in range(8):
                            mm = e.transpose(
                                tr_ps2[:, 128 * j:128 * (j + 1)],
                                qrot[:, 1024 * (t % 2) + 128 * j:
                                     1024 * (t % 2) + 128 * (j + 1)],
                                identb[:])
                        sE.inc(mm)

                    for tau in range(3):
                        dwq[0].wait(e, 32 * (tau + 1))
                        dwq[1].wait(e, 32 * (tau + 1))
                        for t in range(NT):
                            for p in range(2):
                                it = 32 * tau + 2 * t + p
                                if it >= 4:
                                    pit = it - 4
                                    if pit < 64:
                                        sTS.wait(e, t20 + pit + 1)
                                    else:
                                        sca2.wait(e, s20 + 32 + (pit - 64) + 1)
                                sl = it % 4
                                for k in range(8):
                                    mm = e.matmul(
                                        qkv_ps[:, 512 * sl:512 * (sl + 1)],
                                        buf1[:, 2048 * k + 128 * t:
                                             2048 * k + 128 * (t + 1)],
                                        wqr[p][:, 512 * k:512 * (k + 1)],
                                        start=(k == 0), stop=(k == 7))
                                if t == NT - 1:
                                    fwq[p].inc(mm)
                                    mmwait[it] = (fwq[p].h, fwq[p].n)
                                else:
                                    ten.inc(mm)
                                    mmwait[it] = (ten.h, ten.n)
                            if tau < 2 and t >= 2:
                                transposes(tau, t - 2)
                        if tau < 2:
                            transposes(tau, NT - 2)
                            transposes(tau, NT - 1)

                @block.vector
                def _(e):
                    for tau in range(2):
                        for t in range(NT):
                            e.tensor_scalar(tha[:, 0:512], ifs_tile[:],
                                            posc_sb[:, t:t + 1], None, op0=ALU.mult)
                            e.tensor_scalar(tha[:, 512:1024], tha[:, 0:512],
                                            PI / 2.0, None, op0=ALU.add)
                            e.tensor_scalar(thf, tha[:], INV2PI, 12582912.0,
                                            op0=ALU.mult, op1=ALU.add)
                            e.tensor_scalar(thf, thf, -12582912.0, None,
                                            op0=ALU.add)
                            e.scalar_tensor_tensor(tha[:], thf, -RC1, tha[:],
                                                   op0=ALU.mult, op1=ALU.add)
                            e.scalar_tensor_tensor(tha[:], thf, -RC2,
                                                   tha[:], op0=ALU.mult,
                                                   op1=ALU.add)
                            vVM = e.tensor_scalar(tha[:], tha[:], 3.1415925,
                                                  -3.1415925, op0=ALU.min,
                                                  op1=ALU.max)
                            sVM.inc(vVM)
                            sca.wait(e, s0 + 16 * tau + t + 1)
                            if 16 * tau + t >= 2:
                                sE.wait(e, tE0 + 16 * tau + t - 2 + 1)
                            it0 = 32 * tau + 2 * t
                            e.wait_ge(*mmwait[it0])
                            e.wait_ge(*mmwait[it0 + 1])
                            a0 = (it0 % 4) // 2
                            qp = qkv_ps[:, 1024 * a0:1024 * (a0 + 1)]
                            x1v = qp[:, 0:1024:2]
                            x2v = qp[:, 1:1024:2]
                            cosh = trig[:, 512:1024]
                            sinh = trig[:, 0:512]
                            qsl = qrot.ap().rearrange(
                                "q (tt hh c) -> q tt hh c", tt=2, c=64)[
                                :, t % 2, :, :]
                            ta3 = ta.rearrange("q (hh c) -> q hh c", c=32)
                            tb3 = tb.rearrange("q (hh c) -> q hh c", c=32)
                            e.tensor_tensor(ta, x1v, cosh, op=ALU.mult)
                            e.tensor_tensor(tb, x2v, sinh, op=ALU.mult)
                            e.tensor_tensor(qsl[:, :, 0:32], ta3, tb3,
                                            op=ALU.subtract)
                            e.tensor_tensor(ta, x1v, sinh, op=ALU.mult)
                            e.tensor_tensor(tb, x2v, cosh, op=ALU.mult)
                            rx2 = e.tensor_tensor(qsl[:, :, 32:64], ta3, tb3,
                                                  op=ALU.add)
                            sTS.inc(rx2)
                            sTS.inc(e.engine_nop())

                @block.scalar
                def _(e):
                    def evac2(tau, t):
                        g = 16 * tau + t
                        sE.wait(e, tE0 + g + 1)
                        dT = qT if tau == 0 else kT
                        dst = dT.ap().rearrange("p (j c) -> p j c", c=2048)[
                            :, :, 128 * t:128 * (t + 1)]
                        srcp = tr_ps2.ap().rearrange("p (j c) -> p j c", c=128)
                        sca2.inc(e.copy(dst, srcp))

                    for tau in range(2):
                        for t in range(NT):
                            sVM.wait(e, v20 + 16 * tau + t + 1)
                            sca.inc(e.activation(trig[:], tha[:], AF.Sin))
                            if t >= 2:
                                evac2(tau, t - 2)
                        evac2(tau, NT - 2)
                        evac2(tau, NT - 1)
                    for t in range(NT):
                        for p in range(2):
                            it = 64 + 2 * t + p
                            e.wait_ge(*mmwait[it])
                            sl = it % 4
                            sca2.inc(e.copy(
                                v_sb[:, 1024 * t + 512 * p:1024 * t + 512 * (p + 1)],
                                qkv_ps[:, 512 * sl:512 * (sl + 1)]))

                for nm_ in ("qT", "kT", "v"):
                    if nm_ in dbg:
                        src_t = {"qT": qT, "kT": kT, "v": v_sb}[nm_]

                        @block.sync
                        def _(e, src_t=src_t, nm_=nm_):
                            sca2.wait(e)
                            sE.wait(e)
                            sto.dinc(e.dma_start(out=dbg_aps[nm_], in_=src_t[:]))

            barrier()
            if stop_after < 3:
                @block.sync
                def _(e):
                    sto.wait(e)
                return nc, dbg_aps

            ph3 = contextlib.ExitStack()
            with ph3:
                sb3 = lambda n, s, d_: ph3.enter_context(nc.sbuf_tensor(n, s, d_))
                masks_sb = sb3("masks_sb", [128, 2048], BF16)
                P_sb = sb3("P_sb", [128, 2048], BF16)
                PT_sb = sb3("PT_sb", [128, 2048], BF16)
                sums = sb3("sums", [128, 8], F32)
                wo_sb = sb3("wo_sb", [128, 4096], BF16)
                xr = sb3("xr3", [128, 512], F32)
                s_ps = ph3.enter_context(nc.psum_tensor("s_ps", [128, 2048], F32))
                pt_ps2 = [ph3.enter_context(nc.psum_tensor("pt_ps0", [128, 1024], BF16)),
                          ph3.enter_context(nc.psum_tensor("pt_ps1", [128, 1024], BF16))]
                o_ps2 = [ph3.enter_context(nc.psum_tensor("o_ps0", [128, 512], F32)),
                         ph3.enter_context(nc.psum_tensor("o_ps1", [128, 512], F32))]

                b = dict(TS=sTS.n, VM=sVM.n, E=sE.n, V2S=sV2S.n, T2=sT2.n,
                         SE=sSE.n, TO=sTO.n, V2N=sV2N.n)

                @block.gpsimd
                def _(e):
                    dmask.dinc(e.dma_start(out=masks_sb[:], in_=masks_d))

                def hq(i):
                    return i // NT, i % NT

                @block.tensor
                def _(e):
                    dmask.wait(e)
                    for j in range(NIT + 2):
                        if j < NIT:
                            h, qt = hq(j)
                            qg = qt // 4
                            r0 = 64 * (h % 2)
                            cbase = 2048 * (h // 2)
                            for kc in range(qg + 1):
                                if j >= 1:
                                    pn = nkc(j - 1)
                                    sE.wait(e, b["E"] + KCUM[j - 1] +
                                            min(kc + 1, pn))
                                mm = e.matmul(
                                    s_ps[:, 512 * kc:512 * (kc + 1)],
                                    qT[r0:r0 + 64,
                                       cbase + 128 * qt:cbase + 128 * (qt + 1)],
                                    kT[r0:r0 + 64,
                                       cbase + 512 * kc:cbase + 512 * (kc + 1)],
                                    start=True, stop=True)
                            sTS.inc(mm)
                        if 0 <= j - 2 < NIT:
                            i = j - 2
                            h, qt = hq(i)
                            sSE.wait(e, b["SE"] + KCUM[i + 1])
                            if i >= 2:
                                sV2N.wait(e, b["V2N"] + i - 1)
                            osl = o_ps2[i % 2][:, 0:64]
                            nk4 = 4 * nkc(i)
                            ptb = PT_sb
                            for kap in range(nk4):
                                mm = e.matmul(
                                    osl, ptb[:, 128 * kap:128 * (kap + 1)],
                                    v_sb[:, 1024 * kap + 64 * h:
                                         1024 * kap + 64 * (h + 1)],
                                    start=(kap == 0), stop=(kap == nk4 - 1))
                            sTO.inc(mm)
                        if 0 <= j - 1 < NIT:
                            i = j - 1
                            h, qt = hq(i)
                            for kc in range(nkc(i)):
                                kcum = KCUM[i] + kc
                                sE.wait(e, b["E"] + kcum + 1)
                                if kcum >= 2:
                                    sSE.wait(e, b["SE"] + kcum - 1)
                                psl = P_sb
                                for jj in range(4):
                                    mm = e.transpose(
                                        pt_ps2[kcum % 2][:, 128 * jj:
                                                         128 * (jj + 1)],
                                        psl[:, 512 * kc + 128 * jj:
                                            512 * kc + 128 * (jj + 1)],
                                        identb[:])
                                sT2.inc(mm)

                @block.vector
                def _(e):
                    dmask.wait(e)
                    for j in range(NIT + 2):
                        if j < NIT:
                            h, qt = hq(j)
                            qg = qt // 4
                            sTS.wait(e, b["TS"] + j + 1)
                            e.tensor_tensor(s_ps[:, 512 * qg:512 * (qg + 1)],
                                            s_ps[:, 512 * qg:512 * (qg + 1)],
                                            masks_sb[:, 512 * (qt % 4):
                                                     512 * (qt % 4 + 1)],
                                            op=ALU.add)
                            o = 16 + 4 * (j % 2)
                            e.tensor_reduce(scr[:, o:o + 1],
                                            s_ps[:, 0:512 * (qg + 1)],
                                            axis=AX.X, op=ALU.max, negate=True)
                            e.drain()
                            vm = e.tensor_scalar(scr[:, 24 + (j % 2):25 + (j % 2)],
                                                 scr[:, o:o + 1], 0.125, None,
                                                 op0=ALU.mult)
                            sVM.inc(vm)
                        if 0 <= j - 1 < NIT:
                            i = j - 1
                            sE.wait(e, b["E"] + KCUM[i + 1])
                            so = 26 + 3 * (i % 2)
                            e.tensor_reduce(scr[:, so + 1:so + 2],
                                            sums[:, 4 * (i % 2):
                                                 4 * (i % 2) + nkc(i)],
                                            axis=AX.X, op=ALU.add)
                            e.drain()
                            vs = e.reciprocal(scr[:, so:so + 1],
                                              scr[:, so + 1:so + 2])
                            sV2S.inc(vs)
                        if 0 <= j - 2 < NIT:
                            i = j - 2
                            h, qt = hq(i)
                            sTO.wait(e, b["TO"] + i + 1)
                            so = 26 + 3 * (i % 2)
                            osl = o_ps2[i % 2][:, 0:64]
                            vn = e.tensor_scalar(
                                buf2[:, 1024 * qt + 64 * h:1024 * qt + 64 * (h + 1)],
                                osl, scr[:, so:so + 1], None, op0=ALU.mult)
                            sV2N.inc(vn)

                @block.scalar
                def _(e):
                    for j in range(NIT + 1):
                        if 0 <= j - 1 < NIT:
                            i = j - 1
                            if i >= 1:
                                sTO.wait(e, b["TO"] + i)
                            for kc in range(nkc(i)):
                                kcum = KCUM[i] + kc
                                sT2.wait(e, b["T2"] + kcum + 1)
                                se2 = e.copy(
                                    PT_sb[:, 512 * kc:512 * (kc + 1)],
                                    pt_ps2[kcum % 2][:, 0:512])
                                sSE.inc(se2)
                        if j < NIT:
                            i = j
                            h, qt = hq(i)
                            sVM.wait(e, b["VM"] + i + 1)
                            if i >= 1:
                                sT2.wait(e, b["T2"] + KCUM[i])
                            psl = P_sb
                            for kc in range(nkc(i)):
                                se_ = e.activation(
                                    psl[:, 512 * kc:512 * (kc + 1)],
                                    s_ps[:, 512 * kc:512 * (kc + 1)], AF.Exp,
                                    bias=scr[:, 24 + (i % 2):25 + (i % 2)],
                                    scale=0.125,
                                    accum_out=sums[:, 4 * (i % 2) + kc:
                                                   4 * (i % 2) + kc + 1])
                                sE.inc(se_)

                if "otm" in dbg:
                    @block.sync
                    def _(e):
                        sV2N.wait(e)
                        sto.dinc(e.dma_start(out=dbg_aps["otm"], in_=buf2[:]))

                t0b, s20b = ten.n, sca2.n

                @block.tensor
                def _(e):
                    sV2N.wait(e)
                    for t in range(NT):
                        if t >= 1:
                            sca2.wait(e, s20b + t)
                        for j in range(8):
                            mm = e.transpose(
                                pt_ps2[0][:, 128 * j:128 * (j + 1)],
                                buf2[:, 1024 * t + 128 * j:1024 * t + 128 * (j + 1)],
                                identb[:])
                        ten.inc(mm)

                @block.scalar
                def _(e):
                    for t in range(NT):
                        ten.wait(e, t0b + t + 1)
                        dst = buf1.ap().rearrange("p (j c) -> p j c", c=2048)[
                            :, :, 128 * t:128 * (t + 1)]
                        srcp = pt_ps2[0].ap().rearrange("p (j c) -> p j c", c=128)
                        sca2.inc(e.copy(dst, srcp))

                if "oT" in dbg:
                    @block.sync
                    def _(e):
                        sca2.wait(e)
                        sto.dinc(e.dma_start(out=dbg_aps["oT"], in_=buf1[:]))

                xb2 = xld.n
                xfb2 = xfree.n

                @block.sync
                def _(e):
                    sca2.wait(e)
                    for n in range(2):
                        for t in range(NT):
                            m = 16 * n + t
                            if m >= 1:
                                xfree.wait(e, xfb2 + m)
                            xld.dinc(e.dma_start(
                                out=xr[:, 0:512],
                                in_=x_in[128 * t:128 * (t + 1),
                                         512 * n:512 * (n + 1)]))

                t0c, v0c = ten.n, vec.n
                wofree = fwm[0]

                @block.gpsimd
                def _(e):
                    srcw = wo_d.rearrange("p (k c) -> p k c", c=1024)
                    wof0 = wofree.n
                    for n in range(2):
                        if n == 1:
                            wofree.wait(e, wof0 + 1)
                        dwo.dinc(e.dma_start(out=wo_sb[:],
                                             in_=srcw[:, :, 512 * n:512 * (n + 1)]))

                wowait = {}

                @block.tensor
                def _(e):
                    sca2.wait(e)
                    for n in range(2):
                        e.wait_ge(dwo.h, 16 + 16 * n)
                        for t in range(NT):
                            m = 16 * n + t
                            if m >= 2:
                                vec.wait(e, v0c + (m - 2) + 1)
                            for k in range(8):
                                mm = e.matmul(
                                    s_ps[:, 512 * (m % 2):512 * (m % 2) + 512],
                                    buf1[:, 2048 * k + 128 * t:
                                         2048 * k + 128 * (t + 1)],
                                    wo_sb[:, 512 * k:512 * (k + 1)],
                                    start=(k == 0), stop=(k == 7))
                            if n == 0 and t == NT - 1:
                                wofree.inc(mm)
                                wowait[m] = (wofree.h, wofree.n)
                            else:
                                ten.inc(mm)
                                wowait[m] = (ten.h, ten.n)

                @block.vector
                def _(e):
                    for n in range(2):
                        for t in range(NT):
                            m = 16 * n + t
                            xld.wait(e, xb2 + 16 * (m + 1))
                            e.wait_ge(*wowait[m])
                            asl = s_ps[:, 512 * (m % 2):512 * (m % 2) + 512]
                            vec.inc(e.tensor_tensor(
                                out_t[:, 512 * (m % 2):512 * (m % 2) + 512],
                                asl, condt[:, 2048 + 512 * n:
                                           2048 + 512 * (n + 1)],
                                op=ALU.mult))
                            vv = e.tensor_tensor(
                                buf2[:, 1024 * t + 512 * n:1024 * t + 512 * (n + 1)],
                                out_t[:, 512 * (m % 2):512 * (m % 2) + 512],
                                xr[:, 0:512],
                                op=ALU.add)
                            xfree.inc(vv)

                if "x1" in dbg:
                    @block.sync
                    def _(e):
                        vec.wait(e)
                        sto.dinc(e.dma_start(out=dbg_aps["x1"], in_=buf2[:]))

            barrier()

        ph4 = contextlib.ExitStack()
        with ph4:
            sb4 = lambda n, s, d_: ph4.enter_context(nc.sbuf_tensor(n, s, d_))
            hpre4 = out_t.ap()[:, 0:1024]
            htile4 = sb4("htile4", [128, 2 * 1024], BF16)
            lg = sb4("lg", [128, 8], F32)
            w1r = [sb4("w1r0", [128, 4096], BF16), sb4("w1r1", [128, 4096], BF16)]
            w3r = [sb4("w3r0", [128, 4096], BF16), sb4("w3r1", [128, 4096], BF16)]
            w2b = sb4("w2b", [128, 16384], BF16)
            ypb = sb4("ypb", [128, 16384], BF16)
            accb = sb4("accb", [128, 8192], BF16)
            sg = sb4("sg", [128, 1024], F32)
            tr_ps4 = ph4.enter_context(nc.psum_tensor("tr_ps4", [128, 1024], BF16))
            lg_ps = ph4.enter_context(nc.psum_tensor("lg_ps", [128, 512], F32))
            bc_ps4 = lg_ps
            g_ps = ph4.enter_context(nc.psum_tensor("g_ps", [128, 1024], F32))
            u_ps = ph4.enter_context(nc.psum_tensor("u_ps", [128, 1024], F32))
            f_ps = ph4.enter_context(nc.psum_tensor("f_ps", [128, 1024], F32))

            rows4 = [4, 3, 5]
            dxb4 = dx.n
            tb4, sb24 = ten.n, sca2.n

            @block.sync
            def _(e):
                for i, row in enumerate(rows4):
                    for n in range(2):
                        j = 2 * i + n
                        if j >= 1:
                            ten.wait(e, tb4 + j)
                        dx.dinc(e.dma_start(
                            out=crow[:],
                            in_=condscr_d[:, 1024 * row + 512 * n:
                                          1024 * row + 512 * (n + 1)]))

            @block.tensor
            def _(e):
                for i, row in enumerate(rows4):
                    for n in range(2):
                        j = 2 * i + n
                        dx.wait(e, dxb4 + 16 * (j + 1))
                        if j >= 1:
                            sca2.wait(e, sb24 + j)
                        ten.inc(e.matmul(bc_ps4[:, 0:512], ones_f[:],
                                         crow[0:1, 0:512],
                                         start=True, stop=True))

            @block.scalar
            def _(e):
                for i, row in enumerate(rows4):
                    slot = [0, 1, 2][i]
                    for n in range(2):
                        j = 2 * i + n
                        ten.wait(e, tb4 + j + 1)
                        sca2.inc(e.copy(condt[:, 1024 * slot + 512 * n:
                                              1024 * slot + 512 * (n + 1)],
                                        bc_ps4[:, 0:512]))

            cb_done_t, cb_done_s = ten.n, sca2.n

            def get_x2(t):
                def wt(eng):
                    eng.wait_ge(sca2.h, cb_done_s)
                return buf2[:, 1024 * t:1024 * (t + 1)], wt

            emit_ln(get_x2, 0, 1024, buf1, tr_ps4, hpre4, htile4, False)

            h2_done_s = sca2.n

            if "h2T" in dbg:
                @block.sync
                def _(e):
                    sca2.wait(e)
                    sto.dinc(e.dma_start(out=dbg_aps["h2T"], in_=buf1[:]))

            t0r, v0r = ten.n, vec.n

            @block.tensor
            def _(e):
                for t in range(NT):
                    sca2.wait(e, cb_done_s + t + 1)
                    if t >= 1:
                        vec.wait(e, v0r + 2 * (t - 1) + 1)
                    for k in range(8):
                        mm = e.matmul(lg_ps[:, 0:8], buf1[:, 2048 * k + 128 * t:
                                                     2048 * k + 128 * (t + 1)],
                                      rwt_b[:, 8 * k:8 * (k + 1)],
                                      start=(k == 0), stop=(k == 7))
                    ten.inc(mm)

            s0r = sca.n

            @block.vector
            def _(e):
                for t in range(NT):
                    ten.wait(e, t0r + t + 1)
                    e.tensor_tensor(lg[:], lg_ps[:, 0:8], ebias_t[:], op=ALU.add)
                    e.drain()
                    e.tensor_reduce(scr[:, 0:1], lg[:], axis=AX.X, op=ALU.max,
                                    negate=True)
                    e.drain()
                    e.tensor_scalar(scr[:, 1:2], scr[:, 0:1], -1.0, None,
                                    op0=ALU.mult)
                    e.drain()
                    e.tensor_scalar(scr[:, 8:16], lg[:], scr[:, 1:2], None,
                                    op0=ALU.is_equal)
                    e.drain()
                    e.scalar_tensor_tensor(scr[:, 16:24], scr[:, 8:16], -1.0e30,
                                           lg[:], op0=ALU.mult, op1=ALU.add)
                    e.drain()
                    e.tensor_reduce(scr[:, 2:3], scr[:, 16:24], axis=AX.X,
                                    op=ALU.max, negate=True)
                    e.drain()
                    vv = e.tensor_tensor(scr[:, 3:4], scr[:, 0:1], scr[:, 2:3],
                                         op=ALU.subtract)
                    vec.inc(vv)
                    sca.wait(e, s0r + t + 1)
                    e.tensor_scalar(scr[:, 5:6], scr[:, 4:5], 1.0, None, op0=ALU.add)
                    e.drain()
                    e.reciprocal(scr[:, 6:7], scr[:, 5:6])
                    e.drain()
                    e.tensor_tensor(scr[:, 7:8], scr[:, 4:5], scr[:, 6:7],
                                    op=ALU.mult)
                    e.tensor_scalar(scr[:, 3:4], scr[:, 2:3], -1.0, None,
                                    op0=ALU.mult)
                    e.drain()
                    e.tensor_scalar(scr[:, 24:32], scr[:, 16:24], scr[:, 3:4], None,
                                    op0=ALU.is_equal)
                    e.tensor_scalar(scr[:, 8:16], scr[:, 8:16], scr[:, 6:7], None,
                                    op0=ALU.mult)
                    e.drain()
                    e.tensor_scalar(scr[:, 24:32], scr[:, 24:32], scr[:, 7:8], None,
                                    op0=ALU.mult)
                    e.drain()
                    vv2 = e.tensor_tensor(cwbuf[:, 8 * t:8 * (t + 1)], scr[:, 8:16],
                                          scr[:, 24:32], op=ALU.add)
                    vec.inc(vv2)

            @block.scalar
            def _(e):
                for t in range(NT):
                    vec.wait(e, v0r + 2 * t + 1)
                    sca.inc(e.activation(scr[:, 4:5], scr[:, 3:4], AF.Exp))

            if "cw" in dbg:
                @block.sync
                def _(e):
                    vec.wait(e)
                    sto.dinc(e.dma_start(out=dbg_aps["cw"], in_=cwbuf[:]))

            if stop_after < 5:
                @block.sync
                def _(e):
                    sto.wait(e)
                return nc, dbg_aps

            @block.gpsimd
            def _(e):
                for C in range(2):
                    for ei in range(NE):
                        ce = C * NE + ei
                        if ei == 0:
                            pass
                        for q in range(4):
                            qi = ce * 4 + q
                            s_ = qi % 2
                            if qi >= 2:
                                fw1[s_].wait(e, (qi - 2) // 2 + 1)
                                fw3[s_].wait(e, (qi - 2) // 2 + 1)
                            src1 = w1_d[ei].rearrange("p (k c) -> p k c", c=2048)[
                                :, :, 512 * q:512 * (q + 1)]
                            src3 = w3_d[ei].rearrange("p (k c) -> p k c", c=2048)[
                                :, :, 512 * q:512 * (q + 1)]
                            dw1[s_].dinc(e.dma_start(out=w1r[s_][:], in_=src1))
                            dw3[s_].dinc(e.dma_start(out=w3r[s_][:], in_=src3))
                            if q == 1:
                                if ce >= 1:
                                    fw2.wait(e, ce)
                                src2 = w2_d[ei].rearrange("p (k c) -> p k c",
                                                          c=1024)
                                dw2.dinc(e.dma_start(out=w2b[:, 0:8192],
                                                     in_=src2[:, 0:8, :]))
                                dw2.dinc(e.dma_start(out=w2b[:, 8192:16384],
                                                     in_=src2[:, 8:16, :]))

            bG, bSi, bYp, bF, bAC = sG.n, sSi.n, sYp.n, sF.n, sAC.n

            guwait = {}
            fwait = {}

            @block.tensor
            def _(e):
                for C in range(2):
                    for ei in range(NE):
                        ce = C * NE + ei
                        if ei == 0:
                            sca2.wait(e, cb_done_s + 8 * (C + 1))
                        for q in range(4):
                            qi = ce * 4 + q
                            s_ = qi % 2
                            dw1[s_].wait(e, 16 * (qi // 2 + 1))
                            dw3[s_].wait(e, 16 * (qi // 2 + 1))
                            for hc4 in range(4):
                                Hc = 4 * q + hc4
                                for n in range(2):
                                    gi = (ce * 16 + Hc) * 2 + n
                                    if gi >= 2:
                                        sYp.wait(e, bYp + gi - 1)
                                    sl = gi % 2
                                    last_q = hc4 == 3 and n == 1
                                    for k in range(8):
                                        gmm = e.matmul(
                                            g_ps[:, 512 * sl:512 * (sl + 1)],
                                            w1r[s_][:, 512 * k + 128 * hc4:
                                                    512 * k + 128 * (hc4 + 1)],
                                            buf1[:, 2048 * k + 1024 * C + 512 * n:
                                                 2048 * k + 1024 * C + 512 * (n + 1)],
                                            start=(k == 0), stop=(k == 7))
                                        mm = e.matmul(
                                            u_ps[:, 512 * sl:512 * (sl + 1)],
                                            w3r[s_][:, 512 * k + 128 * hc4:
                                                    512 * k + 128 * (hc4 + 1)],
                                            buf1[:, 2048 * k + 1024 * C + 512 * n:
                                                 2048 * k + 1024 * C + 512 * (n + 1)],
                                            start=(k == 0), stop=(k == 7))
                                    if last_q:
                                        fw1[s_].inc(gmm)
                                        fw3[s_].inc(mm)
                                        guwait[gi] = (fw3[s_].h, fw3[s_].n)
                                    else:
                                        sG.inc(mm)
                                        guwait[gi] = (sG.h, sG.n)
                        dw2.wait(e, 32 * (ce + 1))
                        sYp.wait(e, bYp + (ce + 1) * 32)
                        for dout in range(2):
                            for tt in range(8):
                                fi = ce * 16 + dout * 8 + tt
                                if fi >= 2:
                                    sAC.wait(e, bAC + fi - 1)
                                sl = fi % 2
                                for Hc in range(16):
                                    mm = e.matmul(
                                        f_ps[:, 512 * sl:512 * (sl + 1)],
                                        ypb[:, 1024 * Hc + 128 * tt:
                                            1024 * Hc + 128 * (tt + 1)],
                                        w2b[:, 1024 * Hc + 512 * dout:
                                            1024 * Hc + 512 * (dout + 1)],
                                        start=(Hc == 0), stop=(Hc == 15))
                                if dout == 1 and tt == 7:
                                    fw2.inc(mm)
                                    fwait[fi] = (fw2.h, fw2.n)
                                else:
                                    sF.inc(mm)
                                    fwait[fi] = (sF.h, sF.n)

            @block.scalar
            def _(e):
                for C in range(2):
                    for ei in range(NE):
                        ce = C * NE + ei
                        for Hc in range(16):
                            for n in range(2):
                                gi = (ce * 16 + Hc) * 2 + n
                                e.wait_ge(*guwait[gi])
                                if gi >= 2:
                                    sYp.wait(e, bYp + gi - 1)
                                sl = gi % 2
                                ss = e.activation(sg[:, 512 * sl:512 * (sl + 1)],
                                                  g_ps[:, 512 * sl:512 * (sl + 1)],
                                                  AF.Silu)
                                sSi.inc(ss)
                        if ei == 0:
                            for dout in range(2):
                                for tt in range(8):
                                    fi = ce * 16 + dout * 8 + tt
                                    e.wait_ge(*fwait[fi])
                                    sl = fi % 2
                                    sc = e.copy(
                                        accb[:, 1024 * tt + 512 * dout:
                                             1024 * tt + 512 * (dout + 1)],
                                        f_ps[:, 512 * sl:512 * (sl + 1)])
                                    sAC.inc(sc)

            bOUT = sV2S.n
            sto_out_base = sto.n

            @block.vector
            def _(e):
                for C in range(2):
                    for ei in range(NE):
                        ce = C * NE + ei
                        for Hc in range(16):
                            for n in range(2):
                                gi = (ce * 16 + Hc) * 2 + n
                                if Hc == 0 and n == 0 and ce >= 1:
                                    e.wait_ge(*fwait[ce * 16 - 1])
                                sSi.wait(e, bSi + gi + 1)
                                sl = gi % 2
                                yy = e.tensor_tensor(
                                    ypb[:, 1024 * Hc + 512 * n:
                                        1024 * Hc + 512 * (n + 1)],
                                    sg[:, 512 * sl:512 * (sl + 1)],
                                    u_ps[:, 512 * sl:512 * (sl + 1)],
                                    op=ALU.mult)
                                sYp.inc(yy)
                        if ei > 0:
                            for dout in range(2):
                                for tt in range(8):
                                    fi = ce * 16 + dout * 8 + tt
                                    e.wait_ge(*fwait[fi])
                                    if ei == 1 and dout == 0 and tt == 0:
                                        sAC.wait(e, bAC + 144 * C + 16)
                                        vec.wait(e, v0r + 32)
                                    sl = fi % 2
                                    cwcol = cwbuf[:, 8 * (8 * C + tt) + (ei - 1):
                                                  8 * (8 * C + tt) + ei]
                                    acsl = accb[:, 1024 * tt + 512 * dout:
                                                1024 * tt + 512 * (dout + 1)]
                                    va = e.scalar_tensor_tensor(
                                        acsl, f_ps[:, 512 * sl:512 * (sl + 1)],
                                        cwcol, acsl, op0=ALU.mult, op1=ALU.add)
                                    sAC.inc(va)
                    for tt in range(8):
                        t = 8 * C + tt
                        oidx = t
                        if oidx >= 2:
                            (sto if oidx % 2 == 0 else sto2).wait(
                                e, sto_out_base + 16 * ((oidx - 2) // 2 + 1)
                                if oidx % 2 == 0 else 16 * ((oidx - 2) // 2 + 1))
                        e.tensor_tensor(out_t[:, 1024 * (tt % 2):
                                              1024 * (tt % 2) + 1024],
                                        accb[:, 1024 * tt:1024 * (tt + 1)],
                                        condt[:, 2048:3072], op=ALU.mult)
                        vo = e.tensor_tensor(
                            out_t[:, 1024 * (tt % 2):1024 * (tt % 2) + 1024],
                            out_t[:, 1024 * (tt % 2):1024 * (tt % 2) + 1024],
                            buf2[:, 1024 * t:1024 * (t + 1)], op=ALU.add)
                        sV2S.inc(vo)

            @block.sync
            def _(e):
                for t in range(NT):
                    sV2S.wait(e, bOUT + t + 1)
                    sm = sto if t % 2 == 0 else sto2
                    sm.dinc(e.dma_start(
                        out=y_out[128 * t:128 * (t + 1), :],
                        in_=out_t[:, 1024 * (t % 2):1024 * (t % 2) + 1024]))
                sto.wait(e)
                sto2.wait(e)


    return nc, dbg_aps


def _blk(w):
    K, C = w.shape
    return np.ascontiguousarray(
        w.reshape(K // 128, 128, C).transpose(1, 0, 2).reshape(128, (K // 128) * C))


def _host_consts():
    ident = np.eye(128, dtype=np.float32)
    posc = (np.arange(16)[None, :] * 128 + np.arange(128)[:, None]).astype(np.float32)
    gam = np.asarray(RIEMANN, dtype=np.float32)
    invf = (1.0 / (gam / gam[0])).astype(np.float32)
    ifr = np.tile(invf, 16)[None, :].astype(np.float32)
    masks = np.zeros((128, 2048), dtype=np.float32)
    r = np.arange(128)[:, None]
    jj = np.arange(512)[None, :]
    for d in range(4):
        masks[:, 512 * d:512 * (d + 1)] = np.where(jj <= 128 * d + r, 0.0, -1.0e30)
    return ident, posc, ifr, masks


_CACHE = {}


def _get_built(dbg=(), stop_after=99):
    key = (tuple(sorted(dbg)), stop_after)
    if key not in _CACHE:
        _CACHE[key] = build_program(dbg, stop_after)
    return _CACHE[key]


def kernel(L0_attn_x, L0_attn_w_qkv, L0_attn_w_o, L0_attn_log_scale,
           L0_ffn_router_w, L0_ffn_experts_bias, L0_ffn_w1, L0_ffn_w3, L0_ffn_w2,
           L0_ffn_sw1, L0_ffn_sw3, L0_ffn_sw2, L0_c, L0_wmod, _dbg=(),
           _stop_after=99, _ncores=8):
    f32 = lambda a: np.ascontiguousarray(np.asarray(a, dtype=np.float32))
    x = f32(L0_attn_x)
    ident, posc, ifr, masks = _host_consts()
    wqkv = _blk(f32(L0_attn_w_qkv))
    wo = _blk(f32(L0_attn_w_o))
    wmod = _blk(f32(L0_wmod))
    rwt = _blk(f32(L0_ffn_router_w).T)
    w1 = np.stack([_blk(w) for w in
                   np.concatenate([f32(L0_ffn_sw1), f32(L0_ffn_w1)], axis=0)])
    w3 = np.stack([_blk(w) for w in
                   np.concatenate([f32(L0_ffn_sw3), f32(L0_ffn_w3)], axis=0)])
    w2 = np.stack([_blk(w) for w in
                   np.concatenate([f32(L0_ffn_sw2), f32(L0_ffn_w2)], axis=0)])
    ls = f32(L0_attn_log_scale).reshape(1, 16)
    ebias = f32(L0_ffn_experts_bias).reshape(1, 8)
    cvec = f32(L0_c).reshape(8, 8, 128)

    nc, dbg_aps = _get_built(_dbg, _stop_after)
    in_maps = []
    for b in range(_ncores):
        in_maps.append({
            "x": x[b], "wqkv": wqkv, "wo": wo, "ls": ls, "rwt": rwt,
            "ebias": ebias, "w1": w1, "w3": w3, "w2": w2, "cvec": cvec[b],
            "wmod": wmod, "ident": ident, "posc": posc, "ifr": ifr,
            "masks": masks,
        })
    res = run_bass_kernel_spmd(nc, in_maps, core_ids=list(range(_ncores)))
    out = np.stack([np.asarray(res.results[b]["y"]) for b in range(_ncores)]
                   + [np.zeros((2048, 1024), np.float32)] * (8 - _ncores), axis=0)
    if _dbg:
        dbgres = [{k: np.asarray(res.results[b]["dbg_" + k]) for k in _dbg}
                  for b in range(_ncores)]
        return out.astype(np.float32), dbgres
    return out.astype(np.float32)


# version 05
# speedup vs baseline: 1.1950x; 1.0034x over previous
import contextlib
import math
import numpy as np
import ml_dtypes
import concourse.bass as bass
import concourse.mybir as mybir
from concourse.bass_utils import run_bass_kernel_spmd

F32 = mybir.dt.float32
BF16 = mybir.dt.bfloat16
I32 = mybir.dt.int32
AF = mybir.ActivationFunctionType
ALU = mybir.AluOpType
AX = mybir.AxisListType

PI = math.pi
INV2PI = 1.0 / (2.0 * math.pi)
RC1 = 6.28125
RC2 = float(np.float32(2.0 * math.pi - RC1))
RC3 = 2.0 * math.pi - RC1 - RC2
NT = 16
NH = 16
NE = 9
NIT = NH * NT

RIEMANN = (14.134725142, 21.022039639, 25.01085758, 30.424876126, 32.935061588,
        37.586178159, 40.918719012, 43.327073281, 48.005150881, 49.773832478,
        52.970321478, 56.446247697, 59.347044003, 60.831778525, 65.112544048,
        67.079810529, 69.546401711, 72.067157674, 75.704690699, 77.144840069,
        79.33737502, 82.910380854, 84.735492981, 87.425274613, 88.809111208,
        92.491899271, 94.651344041, 95.870634228, 98.831194218, 101.317851006,
        103.72553804, 105.446623052)


class Cnt:
    def __init__(self, h):
        self.h = h
        self.n = 0

    def inc(self, instr, k=1):
        instr.then_inc(self.h, k)
        self.n += k
        return instr

    def dinc(self, instr):
        return self.inc(instr, 16)

    def wait(self, eng, n=None):
        eng.wait_ge(self.h, self.n if n is None else n)


def nkc(i):
    return (i % NT) // 4 + 1


KCUM = [0]
for _i in range(NIT):
    KCUM.append(KCUM[-1] + nkc(_i))


def build_program(dbg=(), stop_after=99):
    nc = bass.Bass("TRN2", target_bir_lowering=False,
                   detect_race_conditions=False)

    x_in = nc.dram_tensor("x", [2048, 1024], F32, kind="ExternalInput").ap()
    wqkv_d = nc.dram_tensor("wqkv", [128, 8 * 3072], F32, kind="ExternalInput").ap()
    wo_d = nc.dram_tensor("wo", [128, 8 * 1024], F32, kind="ExternalInput").ap()
    ls_d = nc.dram_tensor("ls", [1, 16], F32, kind="ExternalInput").ap()
    rwt_d = nc.dram_tensor("rwt", [128, 64], F32, kind="ExternalInput").ap()
    ebias_d = nc.dram_tensor("ebias", [1, 8], F32, kind="ExternalInput").ap()
    w1_d = nc.dram_tensor("w1", [NE, 128, 8 * 2048], F32, kind="ExternalInput").ap()
    w3_d = nc.dram_tensor("w3", [NE, 128, 8 * 2048], F32, kind="ExternalInput").ap()
    w2_d = nc.dram_tensor("w2", [NE, 128, 16 * 1024], F32, kind="ExternalInput").ap()
    c_d = nc.dram_tensor("cvec", [8, 128], F32, kind="ExternalInput").ap()
    wmod_d = nc.dram_tensor("wmod", [128, 8 * 6144], F32, kind="ExternalInput").ap()
    ident_d = nc.dram_tensor("ident", [128, 128], F32, kind="ExternalInput").ap()
    posc_d = nc.dram_tensor("posc", [128, 16], F32, kind="ExternalInput").ap()
    ifr_d = nc.dram_tensor("ifr", [1, 512], F32, kind="ExternalInput").ap()
    masks_d = nc.dram_tensor("masks", [128, 2048], F32, kind="ExternalInput").ap()
    modscr_d = nc.dram_tensor("modscr", [48, 128], F32, kind="Internal").ap()
    condscr_d = nc.dram_tensor("condscr", [1, 6144], F32, kind="Internal").ap()
    y_out = nc.dram_tensor("y", [2048, 1024], F32, kind="ExternalOutput").ap()

    dbg_aps = {}
    for name, shape, dt in [
            ("cond", [1, 6144], F32), ("hT", [128, 16384], BF16),
            ("qT", [128, 16384], BF16), ("kT", [128, 16384], BF16),
            ("v", [128, 16384], BF16), ("otm", [128, 16384], BF16),
            ("oT", [128, 16384], BF16), ("x1", [128, 16384], BF16),
            ("h2T", [128, 16384], BF16), ("cw", [128, 128], F32)]:
        if name in dbg:
            dbg_aps[name] = nc.dram_tensor("dbg_" + name, shape, dt,
                                           kind="ExternalOutput").ap()

    gst = contextlib.ExitStack()
    with gst:
        sb = lambda n, s, d_: gst.enter_context(nc.sbuf_tensor(n, s, d_))
        sem = lambda n: Cnt(gst.enter_context(nc.semaphore(n)))

        identb = sb("identb", [128, 128], BF16)
        crow = sb("crow", [1, 512], F32)
        condt = sb("condt", [128, 3 * 1024], BF16)
        ebias_t = sb("ebias_t", [128, 8], F32)
        ifs_tile = sb("ifs_tile", [128, 512], F32)
        posc_sb = sb("posc_sb", [128, 16], F32)
        rwt_b = sb("rwt_b", [128, 64], BF16)
        ebias_r = sb("ebias_r", [1, 8], F32)
        cwbuf = sb("cwbuf", [128, 128], F32)
        dump = sb("dump", [128, 1024], BF16)
        scr = sb("scr", [128, 32], F32)
        ones_f = sb("ones_f", [1, 128], F32)
        epscol = sb("epscol", [128, 1], F32)
        junk = sb("junk", [1, 2], F32)
        buf1 = sb("buf1", [128, 16384], BF16)
        buf2 = sb("buf2", [128, 16384], BF16)
        out_t = sb("out_t", [128, 2 * 1024], F32)

        ph = sem("ph")
        dx = sem("dx")
        dwm = [sem("dwm0"), sem("dwm1")]
        fwm = [sem("fwm0"), sem("fwm1")]
        dwq = [sem("dwq0"), sem("dwq1")]
        fwq = [sem("fwq0"), sem("fwq1")]
        dwo = sem("dwo")
        dident = sem("dident")
        dmask = sem("dmask")
        sto2 = sem("sto2")
        dw1 = [sem("dw10"), sem("dw11")]
        fw1 = [sem("fw10"), sem("fw11")]
        dw3 = [sem("dw30"), sem("dw31")]
        fw3 = [sem("fw30"), sem("fw31")]
        dw2 = sem("dw2")
        fw2 = sem("fw2")
        ten = sem("ten")
        vec = sem("vec")
        sca = sem("sca")
        sca2 = sem("sca2")
        xld = sem("xld")
        xfree = sem("xfree")
        sto = sem("sto")
        sTS = sem("sTS")
        sVM = sem("sVM")
        sE = sem("sE")
        sV2S = sem("sV2S")
        sT2 = sem("sT2")
        sSE = sem("sSE")
        sTO = sem("sTO")
        sV2N = sem("sV2N")
        sG = sem("sG")
        sSi = sem("sSi")
        sYp = sem("sYp")
        sF = sem("sF")
        sAC = sem("sAC")

        block = gst.enter_context(nc.Block())
        phase_no = [0]

        def barrier():
            phase_no[0] += 1
            tgt = 5 * phase_no[0]

            @block.sync
            def _(e):
                sto.wait(e)
                sto2.wait(e)
                ph.inc(e.nop())
                e.wait_ge(ph.h, tgt)

            @block.gpsimd
            def _(e):
                ph.inc(e.engine_nop())
                e.wait_ge(ph.h, tgt)

            @block.tensor
            def _(e):
                ph.inc(e.ldweights(identb[:, 0:1]))
                e.wait_ge(ph.h, tgt)

            @block.vector
            def _(e):
                ph.inc(e.engine_nop())
                e.wait_ge(ph.h, tgt)

            @block.scalar
            def _(e):
                ph.inc(e.copy(junk[0:1, 0:1], junk[0:1, 1:2]))
                e.wait_ge(ph.h, tgt)

        ph0 = contextlib.ExitStack()
        with ph0:
            sb0 = lambda n, s, d_: ph0.enter_context(nc.sbuf_tensor(n, s, d_))
            identf = sb0("identf", [128, 128], F32)
            rwt_f = sb0("rwt_f", [128, 64], F32)
            c_sb = sb0("c_sb", [8, 128], F32)
            csil = sb0("csil", [8, 128], BF16)
            wmr = [sb0("wmr0", [128, 6144], BF16), sb0("wmr1", [128, 6144], BF16)]
            mod_sb = sb0("mod_sb", [128, 48], F32)
            modT = sb0("modT", [48, 128], F32)
            ls_sb = sb0("ls_sb", [1, 16], F32)
            els_sb = sb0("els_sb", [1, 16], F32)
            ifr_sb = sb0("ifr_sb", [1, 512], F32)
            sclrow = sb0("sclrow", [1, 512], F32)
            cT_bf = sb0("cT_bf", [128, 8], BF16)
            mod_ps = ph0.enter_context(nc.psum_tensor("mod_ps", [128, 48], F32))
            ct_ps = ph0.enter_context(nc.psum_tensor("ct_ps", [128, 8], BF16))
            mt_ps = ph0.enter_context(nc.psum_tensor("mt_ps", [48, 128], F32))
            bc_ps0 = ph0.enter_context(nc.psum_tensor("bc_ps0", [128, 512], F32))

            @block.sync
            def _(e):
                dx.dinc(e.dma_start(out=identf[:], in_=ident_d))
                dx.dinc(e.dma_start(out=c_sb[:], in_=c_d))
                dx.dinc(e.dma_start(out=ls_sb[:], in_=ls_d))
                dx.dinc(e.dma_start(out=posc_sb[:], in_=posc_d))
                dx.dinc(e.dma_start(out=ifr_sb[:], in_=ifr_d))
                dx.dinc(e.dma_start(out=ebias_r[:], in_=ebias_d))
                dx.dinc(e.dma_start(out=rwt_f[:], in_=rwt_d))

            @block.gpsimd
            def _(e):
                dident.dinc(e.dma_start(out=identb[:], in_=ident_d))
                for k in range(8):
                    s_ = k % 2
                    if k >= 2:
                        fwm[s_].wait(e, (k - 2) // 2 + 1)
                    dwm[s_].dinc(e.dma_start(
                        out=wmr[s_][:], in_=wmod_d[:, 6144 * k:6144 * (k + 1)]))

            @block.vector
            def _(e):
                e.memset(junk[:], 0.0)
                dx.wait(e)
                e.memset(ones_f[:], 1.0)
                e.memset(epscol[:], 1.0e-5)
                vec.inc(e.tensor_copy(rwt_b[:], rwt_f[:]))

            @block.scalar
            def _(e):
                vec.wait(e)
                e.activation(csil[:], c_sb[:], AF.Silu)
                sca.inc(e.activation(els_sb[:], ls_sb[:], AF.Exp))

            @block.tensor
            def _(e):
                sca.wait(e)
                dident.wait(e)
                ten.inc(e.transpose(ct_ps[:], csil[:], identb[0:8, 0:8]))

            @block.scalar
            def _(e):
                ten.wait(e)
                sca.inc(e.copy(cT_bf[:], ct_ps[:]))

            t_mv0, s_mv0, v_mv0 = ten.n, sca.n, vec.n

            @block.tensor
            def _(e):
                sca.wait(e)
                for k in range(8):
                    s_ = k % 2
                    dwm[s_].wait(e, 16 * (k // 2 + 1))
                    if k >= 1:
                        vec.wait(e, v_mv0 + k)
                    for m in range(48):
                        mm = e.matmul(mod_ps[:, m:m + 1],
                                      wmr[s_][:, 128 * m:128 * (m + 1)],
                                      cT_bf[:, k:k + 1],
                                      start=True, stop=True)
                    fwm[s_].inc(mm)

            @block.vector
            def _(e):
                for k in range(8):
                    e.wait_ge(fwm[k % 2].h, k // 2 + 1)
                    if k == 0:
                        vec.inc(e.tensor_copy(mod_sb[:], mod_ps[:]))
                    else:
                        vec.inc(e.tensor_tensor(mod_sb[:], mod_sb[:], mod_ps[:],
                                                op=ALU.add))
                    e.drain()

            @block.scalar
            def _(e):
                vec.wait(e)
                sca.inc(e.nop())

            @block.vector
            def _(e):
                sca.wait(e)
                e.tensor_scalar(mod_sb[:, 8:16], mod_sb[:, 8:16], 1.0, None,
                                op0=ALU.add)
                vec.inc(e.tensor_scalar(mod_sb[:, 32:40], mod_sb[:, 32:40], 1.0,
                                        None, op0=ALU.add))
                e.drain()

            @block.scalar
            def _(e):
                vec.wait(e)
                e.sign(scr[:, 0:8], mod_sb[:, 16:24])
                e.sign(scr[:, 8:16], mod_sb[:, 40:48])
                e.activation(scr[:, 16:24], mod_sb[:, 16:24], AF.Abs)
                sca.inc(e.activation(scr[:, 24:32], mod_sb[:, 40:48], AF.Abs))

            @block.vector
            def _(e):
                sca.wait(e)
                e.tensor_scalar(scr[:, 16:32], scr[:, 16:32], 0.25, None,
                                op0=ALU.max)
                e.drain()
                e.tensor_tensor(mod_sb[:, 16:24], scr[:, 0:8], scr[:, 16:24],
                                op=ALU.mult)
                e.tensor_tensor(mod_sb[:, 40:48], scr[:, 8:16], scr[:, 24:32],
                                op=ALU.mult)
                e.drain()
                vec.inc(e.tensor_tensor(
                    sclrow.ap().rearrange("p (h q) -> p h q", h=16),
                    ifr_sb.ap().rearrange("p (h q) -> p h q", h=16),
                    els_sb.ap().rearrange("p (h o) -> p h o", o=1).broadcast_to([1, 16, 32]),
                    op=ALU.mult))

            @block.tensor
            def _(e):
                vec.wait(e)
                ten.inc(e.transpose(mt_ps[:], mod_sb[:], identf[:]))

            @block.scalar
            def _(e):
                ten.wait(e)
                sca.inc(e.copy(modT[:], mt_ps[:]))

            @block.sync
            def _(e):
                sca.wait(e)
                dx.dinc(e.dma_start(
                    out=condscr_d.rearrange("o (x r) -> (o x) r", r=128),
                    in_=modT[:]))

            rows03 = [1, 0, 2]
            dxb0 = dx.n
            tb0, sb20 = ten.n, sca2.n

            @block.sync
            def _(e):
                dx.wait(e, dxb0)
                for i, row in enumerate(rows03):
                    for n in range(2):
                        j = 2 * i + n
                        if j >= 1:
                            ten.wait(e, tb0 + j)
                        dx.dinc(e.dma_start(
                            out=crow[:],
                            in_=condscr_d[:, 1024 * row + 512 * n:
                                          1024 * row + 512 * (n + 1)]))

            @block.tensor
            def _(e):
                for i, row in enumerate(rows03):
                    for n in range(2):
                        j = 2 * i + n
                        dx.wait(e, dxb0 + 16 * (j + 1))
                        if j >= 1:
                            sca2.wait(e, sb20 + j)
                        ten.inc(e.matmul(bc_ps0[:, 0:512], ones_f[:],
                                         crow[0:1, 0:512],
                                         start=True, stop=True))
                sca2.wait(e, sb20 + 6)
                ten.inc(e.matmul(bc_ps0[:, 0:8], ones_f[:], ebias_r[:],
                                 start=True, stop=True))
                sca2.wait(e, sb20 + 7)
                ten.inc(e.matmul(bc_ps0[:, 0:512], ones_f[:], sclrow[:],
                                 start=True, stop=True))

            @block.scalar
            def _(e):
                for i, row in enumerate(rows03):
                    slot = [0, 1, 2][i]
                    for n in range(2):
                        j = 2 * i + n
                        ten.wait(e, tb0 + j + 1)
                        sca2.inc(e.copy(condt[:, 1024 * slot + 512 * n:
                                              1024 * slot + 512 * (n + 1)],
                                        bc_ps0[:, 0:512]))
                ten.wait(e, tb0 + 7)
                sca2.inc(e.copy(ebias_t[:], bc_ps0[:, 0:8]))
                ten.wait(e, tb0 + 8)
                sca2.inc(e.copy(ifs_tile[:], bc_ps0[:, 0:512]))

            if "cond" in dbg:
                @block.sync
                def _(e):
                    vec.wait(e)
                    sca2.wait(e)
                    sto.dinc(e.dma_start(out=dbg_aps["cond"], in_=condscr_d))

        barrier()
        if stop_after < 1:
            @block.sync
            def _(e):
                sto.wait(e)
            return nc, dbg_aps

        def emit_ln(get_x, scale_off, shift_off, dstT, tr_ps, hpre, htile,
                    use_xfree):
            v0, s0, t0, s20 = vec.n, sca.n, ten.n, sca2.n

            @block.vector
            def _(v):
                for t in range(NT):
                    xs, wt = get_x(t)
                    wt(v)
                    if t >= 2:
                        sca.wait(v, s0 + 3 * (t - 2) + 3)
                    o = 8 * (t % 2)
                    v.tensor_reduce(scr[:, o + 0:o + 1], xs, axis=AX.X, op=ALU.add)
                    v.drain()
                    vec.inc(v.tensor_scalar(scr[:, o + 1:o + 2], scr[:, o + 0:o + 1],
                                            -1.0 / 1024.0, None, op0=ALU.mult))
                    sca.wait(v, s0 + 3 * t + 2)
                    vec.inc(v.scalar_tensor_tensor(
                        scr[:, o + 4:o + 5], scr[:, o + 2:o + 3], 1.0 / 1024.0,
                        scr[:, o + 3:o + 4], op0=ALU.mult, op1=ALU.subtract))
                    sca.wait(v, s0 + 3 * t + 3)
                    v.reciprocal(scr[:, o + 6:o + 7], scr[:, o + 5:o + 6])
                    v.drain()
                    hp = v.tensor_scalar(hpre[:], xs, scr[:, o + 1:o + 2],
                                         scr[:, o + 6:o + 7], op0=ALU.add,
                                         op1=ALU.mult)
                    if use_xfree:
                        xfree.inc(hp)
                    if t >= 2:
                        ten.wait(v, t0 + (t - 2) + 1)
                    hsl = htile[:, 1024 * (t % 2):1024 * (t % 2) + 1024]
                    v.tensor_tensor(hpre[:], hpre[:],
                                    condt[:, scale_off:scale_off + 1024], op=ALU.mult)
                    vec.inc(v.tensor_tensor(hsl, hpre[:],
                                            condt[:, shift_off:shift_off + 1024],
                                            op=ALU.add))

            def evac(s, t):
                ten.wait(s, t0 + t + 1)
                dst = dstT.ap().rearrange("p (j c) -> p j c", c=2048)[
                    :, :, 128 * t:128 * (t + 1)]
                srcp = tr_ps.ap().rearrange("p (j c) -> p j c", c=128)
                sca2.inc(s.copy(dst, srcp))

            @block.scalar
            def _(s):
                for t in range(NT):
                    xs, wt = get_x(t)
                    wt(s)
                    o = 8 * (t % 2)
                    sca.inc(s.activation(dump[:], xs, AF.Square,
                                         accum_out=scr[:, o + 2:o + 3]))
                    vec.wait(s, v0 + 3 * t + 1)
                    sca.inc(s.activation(scr[:, o + 3:o + 4], scr[:, o + 1:o + 2],
                                         AF.Square))
                    vec.wait(s, v0 + 3 * t + 2)
                    sca.inc(s.activation(scr[:, o + 5:o + 6], scr[:, o + 4:o + 5],
                                         AF.Sqrt, bias=epscol[:, 0:1]))
                    if t >= 1:
                        evac(s, t - 1)
                evac(s, NT - 1)

            @block.tensor
            def _(t_):
                for t in range(NT):
                    vec.wait(t_, v0 + 3 * t + 3)
                    if t >= 1:
                        sca2.wait(t_, s20 + t)
                    hsl = htile[:, 1024 * (t % 2):1024 * (t % 2) + 1024]
                    for j in range(8):
                        mm = t_.transpose(tr_ps[:, 128 * j:128 * (j + 1)],
                                          hsl[:, 128 * j:128 * (j + 1)], identb[:])
                    ten.inc(mm)

        phB = contextlib.ExitStack()
        with phB:
            sbB = lambda n, s, d_: phB.enter_context(nc.sbuf_tensor(n, s, d_))
            qT = sbB("qT", [128, 16384], BF16)
            kT = sbB("kT", [128, 16384], BF16)
            v_sb = sbB("v_sb", [128, 16384], BF16)
            wqr = [sbB("wqr0", [128, 4096], BF16), sbB("wqr1", [128, 4096], BF16)]
            tr_ps1 = phB.enter_context(nc.psum_tensor("tr_ps1", [128, 1024], BF16))

            ph1 = contextlib.ExitStack()
            with ph1:
                hpre = ph1.enter_context(nc.sbuf_tensor("hpre", [128, 1024], F32))
                htile = ph1.enter_context(nc.sbuf_tensor("htile", [128, 2048], BF16))
                xr = out_t.ap()
                xb = xld.n
                xfb = xfree.n

                @block.sync
                def _(e):
                    for t in range(NT):
                        if t >= 2:
                            xfree.wait(e, xfb + t - 1)
                        xld.dinc(e.dma_start(
                            out=xr[:, 1024 * (t % 2):1024 * (t % 2) + 1024],
                            in_=x_in[128 * t:128 * (t + 1), :]))

                def get_x1(t):
                    def wt(eng):
                        eng.wait_ge(xld.h, xb + 16 * (t + 1))
                    return xr[:, 1024 * (t % 2):1024 * (t % 2) + 1024], wt

                ln1_evac_base = sca2.n
                emit_ln(get_x1, 0, 1024, buf1, tr_ps1, hpre, htile, True)

                if "hT" in dbg:
                    @block.sync
                    def _(e):
                        sca2.wait(e)
                        sto.dinc(e.dma_start(out=dbg_aps["hT"], in_=buf1[:]))

            if stop_after < 2:
                barrier()
                @block.sync
                def _(e):
                    sto.wait(e)
                return nc, dbg_aps

            ph2 = contextlib.ExitStack()
            with ph2:
                sb2 = lambda n, s, d_: ph2.enter_context(nc.sbuf_tensor(n, s, d_))
                tha = sb2("tha", [128, 1024], F32)
                thi = sb2("thi", [128, 512], I32)
                trig = tha
                thf = out_t.ap()[:, 0:1024]
                ta = out_t.ap()[:, 1024:1536]
                tb = out_t.ap()[:, 1536:2048]
                qrot = sb2("qrot", [128, 2 * 1024], BF16)
                qkv_ps = ph2.enter_context(nc.psum_tensor("qkv_ps", [128, 2048], F32))
                tr_ps2 = tr_ps1

                t0, t20, v0, v20, s0, s20 = (ten.n, sTS.n, vec.n, sVM.n, sca.n,
                                             sca2.n)
                tE0 = sE.n

                @block.gpsimd
                def _(e):
                    for tau in range(3):
                        for p in range(2):
                            s_ = p
                            if tau >= 1:
                                fwq[s_].wait(e, tau)
                            piece = 2 * tau + p
                            src = wqkv_d.rearrange("p (k c) -> p k c", c=3072)[
                                :, :, 512 * piece:512 * (piece + 1)]
                            dwq[s_].dinc(e.dma_start(out=wqr[s_][:, 0:2048],
                                                     in_=src[:, 0:4, :]))
                            dwq[s_].dinc(e.dma_start(out=wqr[s_][:, 2048:4096],
                                                     in_=src[:, 4:8, :]))

                mmwait = {}

                @block.tensor
                def _(e):
                    def transposes(tau, t):
                        sTS.wait(e, t20 + 32 * tau + 2 * t + 2)
                        sca2.wait(e, s20 + 16 * tau + t if 16 * tau + t >= 1
                                  else s20)
                        for j in range(8):
                            mm = e.transpose(
                                tr_ps2[:, 128 * j:128 * (j + 1)],
                                qrot[:, 1024 * (t % 2) + 128 * j:
                                     1024 * (t % 2) + 128 * (j + 1)],
                                identb[:])
                        sE.inc(mm)

                    for tau in range(3):
                        dwq[0].wait(e, 32 * (tau + 1))
                        dwq[1].wait(e, 32 * (tau + 1))
                        for t in range(NT):
                            if tau == 0:
                                sca2.wait(e, ln1_evac_base + t + 1)
                            for p in range(2):
                                it = 32 * tau + 2 * t + p
                                if it >= 4:
                                    pit = it - 4
                                    if pit < 64:
                                        sTS.wait(e, t20 + pit + 1)
                                    else:
                                        sca2.wait(e, s20 + 32 + (pit - 64) + 1)
                                sl = it % 4
                                for k in range(8):
                                    mm = e.matmul(
                                        qkv_ps[:, 512 * sl:512 * (sl + 1)],
                                        buf1[:, 2048 * k + 128 * t:
                                             2048 * k + 128 * (t + 1)],
                                        wqr[p][:, 512 * k:512 * (k + 1)],
                                        start=(k == 0), stop=(k == 7))
                                if t == NT - 1:
                                    fwq[p].inc(mm)
                                    mmwait[it] = (fwq[p].h, fwq[p].n)
                                else:
                                    ten.inc(mm)
                                    mmwait[it] = (ten.h, ten.n)
                            if tau < 2 and t >= 2:
                                transposes(tau, t - 2)
                        if tau < 2:
                            transposes(tau, NT - 2)
                            transposes(tau, NT - 1)

                @block.vector
                def _(e):
                    for tau in range(2):
                        for t in range(NT):
                            e.tensor_scalar(tha[:, 0:512], ifs_tile[:],
                                            posc_sb[:, t:t + 1], None, op0=ALU.mult)
                            e.tensor_scalar(tha[:, 512:1024], tha[:, 0:512],
                                            PI / 2.0, None, op0=ALU.add)
                            e.tensor_scalar(thf, tha[:], INV2PI, None,
                                            op0=ALU.mult)
                            e.tensor_copy(thi[:], thf[:, 0:512])
                            e.tensor_copy(thf[:, 0:512], thi[:])
                            e.tensor_copy(thi[:], thf[:, 512:1024])
                            e.tensor_copy(thf[:, 512:1024], thi[:])
                            e.scalar_tensor_tensor(tha[:], thf, -RC1, tha[:],
                                                   op0=ALU.mult, op1=ALU.add)
                            e.scalar_tensor_tensor(tha[:], thf, -RC2,
                                                   tha[:], op0=ALU.mult,
                                                   op1=ALU.add)
                            e.scalar_tensor_tensor(tha[:], thf, -RC3,
                                                   tha[:], op0=ALU.mult,
                                                   op1=ALU.add)
                            vVM = e.tensor_scalar(tha[:], tha[:], 3.1415925,
                                                  -3.1415925, op0=ALU.min,
                                                  op1=ALU.max)
                            sVM.inc(vVM)
                            sca.wait(e, s0 + 16 * tau + t + 1)
                            if 16 * tau + t >= 2:
                                sE.wait(e, tE0 + 16 * tau + t - 2 + 1)
                            it0 = 32 * tau + 2 * t
                            e.wait_ge(*mmwait[it0])
                            e.wait_ge(*mmwait[it0 + 1])
                            a0 = (it0 % 4) // 2
                            qp = qkv_ps[:, 1024 * a0:1024 * (a0 + 1)]
                            x1v = qp[:, 0:1024:2]
                            x2v = qp[:, 1:1024:2]
                            cosh = trig[:, 512:1024]
                            sinh = trig[:, 0:512]
                            qsl = qrot.ap().rearrange(
                                "q (tt hh c) -> q tt hh c", tt=2, c=64)[
                                :, t % 2, :, :]
                            ta3 = ta.rearrange("q (hh c) -> q hh c", c=32)
                            tb3 = tb.rearrange("q (hh c) -> q hh c", c=32)
                            e.tensor_tensor(ta, x1v, cosh, op=ALU.mult)
                            e.tensor_tensor(tb, x2v, sinh, op=ALU.mult)
                            e.tensor_tensor(qsl[:, :, 0:32], ta3, tb3,
                                            op=ALU.subtract)
                            e.tensor_tensor(ta, x1v, sinh, op=ALU.mult)
                            e.tensor_tensor(tb, x2v, cosh, op=ALU.mult)
                            rx2 = e.tensor_tensor(qsl[:, :, 32:64], ta3, tb3,
                                                  op=ALU.add)
                            sTS.inc(rx2)
                            sTS.inc(e.engine_nop())

                @block.scalar
                def _(e):
                    def evac2(tau, t):
                        g = 16 * tau + t
                        sE.wait(e, tE0 + g + 1)
                        dT = qT if tau == 0 else kT
                        dst = dT.ap().rearrange("p (j c) -> p j c", c=2048)[
                            :, :, 128 * t:128 * (t + 1)]
                        srcp = tr_ps2.ap().rearrange("p (j c) -> p j c", c=128)
                        sca2.inc(e.copy(dst, srcp))

                    for tau in range(2):
                        for t in range(NT):
                            sVM.wait(e, v20 + 16 * tau + t + 1)
                            sca.inc(e.activation(trig[:], tha[:], AF.Sin))
                            if t >= 2:
                                evac2(tau, t - 2)
                        evac2(tau, NT - 2)
                        evac2(tau, NT - 1)
                    for t in range(NT):
                        for p in range(2):
                            it = 64 + 2 * t + p
                            e.wait_ge(*mmwait[it])
                            sl = it % 4
                            sca2.inc(e.copy(
                                v_sb[:, 1024 * t + 512 * p:1024 * t + 512 * (p + 1)],
                                qkv_ps[:, 512 * sl:512 * (sl + 1)]))

                for nm_ in ("qT", "kT", "v"):
                    if nm_ in dbg:
                        src_t = {"qT": qT, "kT": kT, "v": v_sb}[nm_]

                        @block.sync
                        def _(e, src_t=src_t, nm_=nm_):
                            sca2.wait(e)
                            sE.wait(e)
                            sto.dinc(e.dma_start(out=dbg_aps[nm_], in_=src_t[:]))

            barrier()
            if stop_after < 3:
                @block.sync
                def _(e):
                    sto.wait(e)
                return nc, dbg_aps

            ph3 = contextlib.ExitStack()
            with ph3:
                sb3 = lambda n, s, d_: ph3.enter_context(nc.sbuf_tensor(n, s, d_))
                masks_sb = sb3("masks_sb", [128, 2048], BF16)
                P_sb = wqr[0].ap()[:, 0:2048]
                PT_sb = wqr[0].ap()[:, 2048:4096]
                sums = sb3("sums", [128, 8], F32)
                wo_sb = wqr[1].ap()
                xr = sb3("xr3", [128, 512], F32)
                s_ps = ph3.enter_context(nc.psum_tensor("s_ps", [128, 2048], F32))
                pt_ps2 = [tr_ps1,
                          ph3.enter_context(nc.psum_tensor("pt_ps1", [128, 1024], BF16))]
                o_ps2 = [ph3.enter_context(nc.psum_tensor("o_ps0", [128, 512], F32)),
                         ph3.enter_context(nc.psum_tensor("o_ps1", [128, 512], F32))]

                b = dict(TS=sTS.n, VM=sVM.n, E=sE.n, V2S=sV2S.n, T2=sT2.n,
                         SE=sSE.n, TO=sTO.n, V2N=sV2N.n)

                @block.gpsimd
                def _(e):
                    dmask.dinc(e.dma_start(out=masks_sb[:], in_=masks_d))

                def hq(i):
                    return i // NT, i % NT

                @block.tensor
                def _(e):
                    dmask.wait(e)
                    for j in range(NIT + 2):
                        if j < NIT:
                            h, qt = hq(j)
                            qg = qt // 4
                            r0 = 64 * (h % 2)
                            cbase = 2048 * (h // 2)
                            for kc in range(qg + 1):
                                if j >= 1:
                                    pn = nkc(j - 1)
                                    sE.wait(e, b["E"] + KCUM[j - 1] +
                                            min(kc + 1, pn))
                                mm = e.matmul(
                                    s_ps[:, 512 * kc:512 * (kc + 1)],
                                    qT[r0:r0 + 64,
                                       cbase + 128 * qt:cbase + 128 * (qt + 1)],
                                    kT[r0:r0 + 64,
                                       cbase + 512 * kc:cbase + 512 * (kc + 1)],
                                    start=True, stop=True)
                            sTS.inc(mm)
                        if 0 <= j - 2 < NIT:
                            i = j - 2
                            h, qt = hq(i)
                            sSE.wait(e, b["SE"] + KCUM[i + 1])
                            if i >= 2:
                                sV2N.wait(e, b["V2N"] + i - 1)
                            osl = o_ps2[i % 2][:, 0:64]
                            nk4 = 4 * nkc(i)
                            ptb = PT_sb
                            for kap in range(nk4):
                                mm = e.matmul(
                                    osl, ptb[:, 128 * kap:128 * (kap + 1)],
                                    v_sb[:, 1024 * kap + 64 * h:
                                         1024 * kap + 64 * (h + 1)],
                                    start=(kap == 0), stop=(kap == nk4 - 1))
                            sTO.inc(mm)
                        if 0 <= j - 1 < NIT:
                            i = j - 1
                            h, qt = hq(i)
                            for kc in range(nkc(i)):
                                kcum = KCUM[i] + kc
                                sE.wait(e, b["E"] + kcum + 1)
                                if kcum >= 2:
                                    sSE.wait(e, b["SE"] + kcum - 1)
                                psl = P_sb
                                for jj in range(4):
                                    mm = e.transpose(
                                        pt_ps2[kcum % 2][:, 128 * jj:
                                                         128 * (jj + 1)],
                                        psl[:, 512 * kc + 128 * jj:
                                            512 * kc + 128 * (jj + 1)],
                                        identb[:])
                                sT2.inc(mm)

                @block.vector
                def _(e):
                    dmask.wait(e)
                    for j in range(NIT + 2):
                        if j < NIT:
                            h, qt = hq(j)
                            qg = qt // 4
                            sTS.wait(e, b["TS"] + j + 1)
                            e.tensor_tensor(s_ps[:, 512 * qg:512 * (qg + 1)],
                                            s_ps[:, 512 * qg:512 * (qg + 1)],
                                            masks_sb[:, 512 * (qt % 4):
                                                     512 * (qt % 4 + 1)],
                                            op=ALU.add)
                            o = 16 + 4 * (j % 2)
                            e.tensor_reduce(scr[:, o:o + 1],
                                            s_ps[:, 0:512 * (qg + 1)],
                                            axis=AX.X, op=ALU.max, negate=True)
                            e.drain()
                            vm = e.tensor_scalar(scr[:, 24 + (j % 2):25 + (j % 2)],
                                                 scr[:, o:o + 1], 0.125, None,
                                                 op0=ALU.mult)
                            sVM.inc(vm)
                        if 0 <= j - 1 < NIT:
                            i = j - 1
                            sE.wait(e, b["E"] + KCUM[i + 1])
                            so = 26 + 3 * (i % 2)
                            e.tensor_reduce(scr[:, so + 1:so + 2],
                                            sums[:, 4 * (i % 2):
                                                 4 * (i % 2) + nkc(i)],
                                            axis=AX.X, op=ALU.add)
                            e.drain()
                            vs = e.reciprocal(scr[:, so:so + 1],
                                              scr[:, so + 1:so + 2])
                            sV2S.inc(vs)
                        if 0 <= j - 2 < NIT:
                            i = j - 2
                            h, qt = hq(i)
                            sTO.wait(e, b["TO"] + i + 1)
                            so = 26 + 3 * (i % 2)
                            osl = o_ps2[i % 2][:, 0:64]
                            vn = e.tensor_scalar(
                                buf2[:, 1024 * qt + 64 * h:1024 * qt + 64 * (h + 1)],
                                osl, scr[:, so:so + 1], None, op0=ALU.mult)
                            sV2N.inc(vn)

                @block.scalar
                def _(e):
                    for j in range(NIT + 1):
                        if 0 <= j - 1 < NIT:
                            i = j - 1
                            if i >= 1:
                                sTO.wait(e, b["TO"] + i)
                            for kc in range(nkc(i)):
                                kcum = KCUM[i] + kc
                                sT2.wait(e, b["T2"] + kcum + 1)
                                se2 = e.copy(
                                    PT_sb[:, 512 * kc:512 * (kc + 1)],
                                    pt_ps2[kcum % 2][:, 0:512])
                                sSE.inc(se2)
                        if j < NIT:
                            i = j
                            h, qt = hq(i)
                            sVM.wait(e, b["VM"] + i + 1)
                            if i >= 1:
                                sT2.wait(e, b["T2"] + KCUM[i])
                            psl = P_sb
                            for kc in range(nkc(i)):
                                se_ = e.activation(
                                    psl[:, 512 * kc:512 * (kc + 1)],
                                    s_ps[:, 512 * kc:512 * (kc + 1)], AF.Exp,
                                    bias=scr[:, 24 + (i % 2):25 + (i % 2)],
                                    scale=0.125,
                                    accum_out=sums[:, 4 * (i % 2) + kc:
                                                   4 * (i % 2) + kc + 1])
                                sE.inc(se_)

                if "otm" in dbg:
                    @block.sync
                    def _(e):
                        sV2N.wait(e)
                        sto.dinc(e.dma_start(out=dbg_aps["otm"], in_=buf2[:]))

                t0b, s20b = ten.n, sca2.n

                @block.tensor
                def _(e):
                    sV2N.wait(e)
                    for t in range(NT):
                        if t >= 1:
                            sca2.wait(e, s20b + t)
                        for j in range(8):
                            mm = e.transpose(
                                pt_ps2[0][:, 128 * j:128 * (j + 1)],
                                buf2[:, 1024 * t + 128 * j:1024 * t + 128 * (j + 1)],
                                identb[:])
                        ten.inc(mm)

                @block.scalar
                def _(e):
                    for t in range(NT):
                        ten.wait(e, t0b + t + 1)
                        dst = buf1.ap().rearrange("p (j c) -> p j c", c=2048)[
                            :, :, 128 * t:128 * (t + 1)]
                        srcp = pt_ps2[0].ap().rearrange("p (j c) -> p j c", c=128)
                        sca2.inc(e.copy(dst, srcp))

                if "oT" in dbg:
                    @block.sync
                    def _(e):
                        sca2.wait(e)
                        sto.dinc(e.dma_start(out=dbg_aps["oT"], in_=buf1[:]))

                xb2 = xld.n
                xfb2 = xfree.n

                @block.sync
                def _(e):
                    sca2.wait(e)
                    for n in range(2):
                        for t in range(NT):
                            m = 16 * n + t
                            if m >= 1:
                                xfree.wait(e, xfb2 + m)
                            xld.dinc(e.dma_start(
                                out=xr[:, 0:512],
                                in_=x_in[128 * t:128 * (t + 1),
                                         512 * n:512 * (n + 1)]))

                t0c, v0c = ten.n, vec.n
                wofree = fwm[0]

                @block.gpsimd
                def _(e):
                    srcw = wo_d.rearrange("p (k c) -> p k c", c=1024)
                    wof0 = wofree.n
                    for n in range(2):
                        if n == 1:
                            wofree.wait(e, wof0 + 1)
                        dwo.dinc(e.dma_start(out=wo_sb[:],
                                             in_=srcw[:, :, 512 * n:512 * (n + 1)]))

                wowait = {}

                @block.tensor
                def _(e):
                    sca2.wait(e)
                    for n in range(2):
                        e.wait_ge(dwo.h, 16 + 16 * n)
                        for t in range(NT):
                            m = 16 * n + t
                            if m >= 2:
                                vec.wait(e, v0c + (m - 2) + 1)
                            for k in range(8):
                                mm = e.matmul(
                                    s_ps[:, 512 * (m % 2):512 * (m % 2) + 512],
                                    buf1[:, 2048 * k + 128 * t:
                                         2048 * k + 128 * (t + 1)],
                                    wo_sb[:, 512 * k:512 * (k + 1)],
                                    start=(k == 0), stop=(k == 7))
                            if n == 0 and t == NT - 1:
                                wofree.inc(mm)
                                wowait[m] = (wofree.h, wofree.n)
                            else:
                                ten.inc(mm)
                                wowait[m] = (ten.h, ten.n)

                @block.vector
                def _(e):
                    for n in range(2):
                        for t in range(NT):
                            m = 16 * n + t
                            xld.wait(e, xb2 + 16 * (m + 1))
                            e.wait_ge(*wowait[m])
                            asl = s_ps[:, 512 * (m % 2):512 * (m % 2) + 512]
                            vec.inc(e.tensor_tensor(
                                out_t[:, 512 * (m % 2):512 * (m % 2) + 512],
                                asl, condt[:, 2048 + 512 * n:
                                           2048 + 512 * (n + 1)],
                                op=ALU.mult))
                            vv = e.tensor_tensor(
                                buf2[:, 1024 * t + 512 * n:1024 * t + 512 * (n + 1)],
                                out_t[:, 512 * (m % 2):512 * (m % 2) + 512],
                                xr[:, 0:512],
                                op=ALU.add)
                            xfree.inc(vv)

                if "x1" in dbg:
                    @block.sync
                    def _(e):
                        vec.wait(e)
                        sto.dinc(e.dma_start(out=dbg_aps["x1"], in_=buf2[:]))

            barrier()

        ph4 = contextlib.ExitStack()
        with ph4:
            sb4 = lambda n, s, d_: ph4.enter_context(nc.sbuf_tensor(n, s, d_))
            hpre4 = out_t.ap()[:, 0:1024]
            htile4 = sb4("htile4", [128, 2 * 1024], BF16)
            lg = sb4("lg", [128, 8], F32)
            w1r = [sb4("w1r0", [128, 4096], BF16), sb4("w1r1", [128, 4096], BF16)]
            w3r = [sb4("w3r0", [128, 4096], BF16), sb4("w3r1", [128, 4096], BF16)]
            w2b = sb4("w2b", [128, 16384], BF16)
            ypb = sb4("ypb", [128, 16384], BF16)
            accb = sb4("accb", [128, 8192], BF16)
            sg = sb4("sg", [128, 1024], F32)
            tr_ps4 = ph4.enter_context(nc.psum_tensor("tr_ps4", [128, 1024], BF16))
            lg_ps = ph4.enter_context(nc.psum_tensor("lg_ps", [128, 512], F32))
            bc_ps4 = lg_ps
            g_ps = ph4.enter_context(nc.psum_tensor("g_ps", [128, 1024], F32))
            u_ps = ph4.enter_context(nc.psum_tensor("u_ps", [128, 1024], F32))
            f_ps = ph4.enter_context(nc.psum_tensor("f_ps", [128, 1024], F32))

            rows4 = [4, 3, 5]
            dxb4 = dx.n
            tb4, sb24 = ten.n, sca2.n

            @block.sync
            def _(e):
                for i, row in enumerate(rows4):
                    for n in range(2):
                        j = 2 * i + n
                        if j >= 1:
                            ten.wait(e, tb4 + j)
                        dx.dinc(e.dma_start(
                            out=crow[:],
                            in_=condscr_d[:, 1024 * row + 512 * n:
                                          1024 * row + 512 * (n + 1)]))

            @block.tensor
            def _(e):
                for i, row in enumerate(rows4):
                    for n in range(2):
                        j = 2 * i + n
                        dx.wait(e, dxb4 + 16 * (j + 1))
                        if j >= 1:
                            sca2.wait(e, sb24 + j)
                        ten.inc(e.matmul(bc_ps4[:, 0:512], ones_f[:],
                                         crow[0:1, 0:512],
                                         start=True, stop=True))

            @block.scalar
            def _(e):
                for i, row in enumerate(rows4):
                    slot = [0, 1, 2][i]
                    for n in range(2):
                        j = 2 * i + n
                        ten.wait(e, tb4 + j + 1)
                        sca2.inc(e.copy(condt[:, 1024 * slot + 512 * n:
                                              1024 * slot + 512 * (n + 1)],
                                        bc_ps4[:, 0:512]))

            cb_done_t, cb_done_s = ten.n, sca2.n

            def get_x2(t):
                def wt(eng):
                    eng.wait_ge(sca2.h, cb_done_s)
                return buf2[:, 1024 * t:1024 * (t + 1)], wt

            emit_ln(get_x2, 0, 1024, buf1, tr_ps4, hpre4, htile4, False)

            h2_done_s = sca2.n

            if "h2T" in dbg:
                @block.sync
                def _(e):
                    sca2.wait(e)
                    sto.dinc(e.dma_start(out=dbg_aps["h2T"], in_=buf1[:]))

            t0r, v0r = ten.n, vec.n

            @block.tensor
            def _(e):
                for t in range(NT):
                    sca2.wait(e, cb_done_s + t + 1)
                    if t >= 1:
                        vec.wait(e, v0r + 2 * (t - 1) + 1)
                    for k in range(8):
                        mm = e.matmul(lg_ps[:, 0:8], buf1[:, 2048 * k + 128 * t:
                                                     2048 * k + 128 * (t + 1)],
                                      rwt_b[:, 8 * k:8 * (k + 1)],
                                      start=(k == 0), stop=(k == 7))
                    ten.inc(mm)

            s0r = sca.n

            @block.vector
            def _(e):
                for t in range(NT):
                    ten.wait(e, t0r + t + 1)
                    e.tensor_tensor(lg[:], lg_ps[:, 0:8], ebias_t[:], op=ALU.add)
                    e.drain()
                    e.tensor_reduce(scr[:, 0:1], lg[:], axis=AX.X, op=ALU.max,
                                    negate=True)
                    e.drain()
                    e.tensor_scalar(scr[:, 1:2], scr[:, 0:1], -1.0, None,
                                    op0=ALU.mult)
                    e.drain()
                    e.tensor_scalar(scr[:, 8:16], lg[:], scr[:, 1:2], None,
                                    op0=ALU.is_equal)
                    e.drain()
                    e.scalar_tensor_tensor(scr[:, 16:24], scr[:, 8:16], -1.0e30,
                                           lg[:], op0=ALU.mult, op1=ALU.add)
                    e.drain()
                    e.tensor_reduce(scr[:, 2:3], scr[:, 16:24], axis=AX.X,
                                    op=ALU.max, negate=True)
                    e.drain()
                    vv = e.tensor_tensor(scr[:, 3:4], scr[:, 0:1], scr[:, 2:3],
                                         op=ALU.subtract)
                    vec.inc(vv)
                    sca.wait(e, s0r + t + 1)
                    e.tensor_scalar(scr[:, 5:6], scr[:, 4:5], 1.0, None, op0=ALU.add)
                    e.drain()
                    e.reciprocal(scr[:, 6:7], scr[:, 5:6])
                    e.drain()
                    e.tensor_tensor(scr[:, 7:8], scr[:, 4:5], scr[:, 6:7],
                                    op=ALU.mult)
                    e.tensor_scalar(scr[:, 3:4], scr[:, 2:3], -1.0, None,
                                    op0=ALU.mult)
                    e.drain()
                    e.tensor_scalar(scr[:, 24:32], scr[:, 16:24], scr[:, 3:4], None,
                                    op0=ALU.is_equal)
                    e.tensor_scalar(scr[:, 8:16], scr[:, 8:16], scr[:, 6:7], None,
                                    op0=ALU.mult)
                    e.drain()
                    e.tensor_scalar(scr[:, 24:32], scr[:, 24:32], scr[:, 7:8], None,
                                    op0=ALU.mult)
                    e.drain()
                    vv2 = e.tensor_tensor(cwbuf[:, 8 * t:8 * (t + 1)], scr[:, 8:16],
                                          scr[:, 24:32], op=ALU.add)
                    vec.inc(vv2)

            @block.scalar
            def _(e):
                for t in range(NT):
                    vec.wait(e, v0r + 2 * t + 1)
                    sca.inc(e.activation(scr[:, 4:5], scr[:, 3:4], AF.Exp))

            if "cw" in dbg:
                @block.sync
                def _(e):
                    vec.wait(e)
                    sto.dinc(e.dma_start(out=dbg_aps["cw"], in_=cwbuf[:]))

            if stop_after < 5:
                @block.sync
                def _(e):
                    sto.wait(e)
                return nc, dbg_aps

            @block.gpsimd
            def _(e):
                for C in range(2):
                    for ei in range(NE):
                        ce = C * NE + ei
                        if ei == 0:
                            pass
                        for q in range(4):
                            qi = ce * 4 + q
                            s_ = qi % 2
                            if qi >= 2:
                                fw1[s_].wait(e, (qi - 2) // 2 + 1)
                                fw3[s_].wait(e, (qi - 2) // 2 + 1)
                            src1 = w1_d[ei].rearrange("p (k c) -> p k c", c=2048)[
                                :, :, 512 * q:512 * (q + 1)]
                            src3 = w3_d[ei].rearrange("p (k c) -> p k c", c=2048)[
                                :, :, 512 * q:512 * (q + 1)]
                            dw1[s_].dinc(e.dma_start(out=w1r[s_][:], in_=src1))
                            dw3[s_].dinc(e.dma_start(out=w3r[s_][:], in_=src3))
                            if q == 1:
                                if ce >= 1:
                                    fw2.wait(e, ce)
                                src2 = w2_d[ei].rearrange("p (k c) -> p k c",
                                                          c=1024)
                                dw2.dinc(e.dma_start(out=w2b[:, 0:8192],
                                                     in_=src2[:, 0:8, :]))
                                dw2.dinc(e.dma_start(out=w2b[:, 8192:16384],
                                                     in_=src2[:, 8:16, :]))

            bG, bSi, bYp, bF, bAC = sG.n, sSi.n, sYp.n, sF.n, sAC.n

            guwait = {}
            fwait = {}

            @block.tensor
            def _(e):
                for C in range(2):
                    for ei in range(NE):
                        ce = C * NE + ei
                        if ei == 0:
                            sca2.wait(e, cb_done_s + 8 * (C + 1))
                        for q in range(4):
                            qi = ce * 4 + q
                            s_ = qi % 2
                            dw1[s_].wait(e, 16 * (qi // 2 + 1))
                            dw3[s_].wait(e, 16 * (qi // 2 + 1))
                            for hc4 in range(4):
                                Hc = 4 * q + hc4
                                for n in range(2):
                                    gi = (ce * 16 + Hc) * 2 + n
                                    if gi >= 2:
                                        sYp.wait(e, bYp + gi - 1)
                                    sl = gi % 2
                                    last_q = hc4 == 3 and n == 1
                                    for k in range(8):
                                        gmm = e.matmul(
                                            g_ps[:, 512 * sl:512 * (sl + 1)],
                                            w1r[s_][:, 512 * k + 128 * hc4:
                                                    512 * k + 128 * (hc4 + 1)],
                                            buf1[:, 2048 * k + 1024 * C + 512 * n:
                                                 2048 * k + 1024 * C + 512 * (n + 1)],
                                            start=(k == 0), stop=(k == 7))
                                        mm = e.matmul(
                                            u_ps[:, 512 * sl:512 * (sl + 1)],
                                            w3r[s_][:, 512 * k + 128 * hc4:
                                                    512 * k + 128 * (hc4 + 1)],
                                            buf1[:, 2048 * k + 1024 * C + 512 * n:
                                                 2048 * k + 1024 * C + 512 * (n + 1)],
                                            start=(k == 0), stop=(k == 7))
                                    if last_q:
                                        fw1[s_].inc(gmm)
                                        fw3[s_].inc(mm)
                                        guwait[gi] = (fw3[s_].h, fw3[s_].n)
                                    else:
                                        sG.inc(mm)
                                        guwait[gi] = (sG.h, sG.n)
                        dw2.wait(e, 32 * (ce + 1))
                        sYp.wait(e, bYp + (ce + 1) * 32)
                        for dout in range(2):
                            for tt in range(8):
                                fi = ce * 16 + dout * 8 + tt
                                if fi >= 2:
                                    sAC.wait(e, bAC + fi - 1)
                                sl = fi % 2
                                for Hc in range(16):
                                    mm = e.matmul(
                                        f_ps[:, 512 * sl:512 * (sl + 1)],
                                        ypb[:, 1024 * Hc + 128 * tt:
                                            1024 * Hc + 128 * (tt + 1)],
                                        w2b[:, 1024 * Hc + 512 * dout:
                                            1024 * Hc + 512 * (dout + 1)],
                                        start=(Hc == 0), stop=(Hc == 15))
                                if dout == 1 and tt == 7:
                                    fw2.inc(mm)
                                    fwait[fi] = (fw2.h, fw2.n)
                                else:
                                    sF.inc(mm)
                                    fwait[fi] = (sF.h, sF.n)

            @block.scalar
            def _(e):
                for C in range(2):
                    for ei in range(NE):
                        ce = C * NE + ei
                        for Hc in range(16):
                            for n in range(2):
                                gi = (ce * 16 + Hc) * 2 + n
                                e.wait_ge(*guwait[gi])
                                if gi >= 2:
                                    sYp.wait(e, bYp + gi - 1)
                                sl = gi % 2
                                ss = e.activation(sg[:, 512 * sl:512 * (sl + 1)],
                                                  g_ps[:, 512 * sl:512 * (sl + 1)],
                                                  AF.Silu)
                                sSi.inc(ss)
                        if ei == 0:
                            for dout in range(2):
                                for tt in range(8):
                                    fi = ce * 16 + dout * 8 + tt
                                    e.wait_ge(*fwait[fi])
                                    sl = fi % 2
                                    sc = e.copy(
                                        accb[:, 1024 * tt + 512 * dout:
                                             1024 * tt + 512 * (dout + 1)],
                                        f_ps[:, 512 * sl:512 * (sl + 1)])
                                    sAC.inc(sc)

            bOUT = sV2S.n
            sto_out_base = sto.n

            @block.vector
            def _(e):
                for C in range(2):
                    for ei in range(NE):
                        ce = C * NE + ei
                        for Hc in range(16):
                            for n in range(2):
                                gi = (ce * 16 + Hc) * 2 + n
                                if Hc == 0 and n == 0 and ce >= 1:
                                    e.wait_ge(*fwait[ce * 16 - 1])
                                sSi.wait(e, bSi + gi + 1)
                                sl = gi % 2
                                yy = e.tensor_tensor(
                                    ypb[:, 1024 * Hc + 512 * n:
                                        1024 * Hc + 512 * (n + 1)],
                                    sg[:, 512 * sl:512 * (sl + 1)],
                                    u_ps[:, 512 * sl:512 * (sl + 1)],
                                    op=ALU.mult)
                                sYp.inc(yy)
                        if ei > 0:
                            for dout in range(2):
                                for tt in range(8):
                                    fi = ce * 16 + dout * 8 + tt
                                    e.wait_ge(*fwait[fi])
                                    if ei == 1 and dout == 0 and tt == 0:
                                        sAC.wait(e, bAC + 144 * C + 16)
                                        vec.wait(e, v0r + 32)
                                    sl = fi % 2
                                    cwcol = cwbuf[:, 8 * (8 * C + tt) + (ei - 1):
                                                  8 * (8 * C + tt) + ei]
                                    acsl = accb[:, 1024 * tt + 512 * dout:
                                                1024 * tt + 512 * (dout + 1)]
                                    va = e.scalar_tensor_tensor(
                                        acsl, f_ps[:, 512 * sl:512 * (sl + 1)],
                                        cwcol, acsl, op0=ALU.mult, op1=ALU.add)
                                    sAC.inc(va)
                    for tt in range(8):
                        t = 8 * C + tt
                        oidx = t
                        if oidx >= 2:
                            (sto if oidx % 2 == 0 else sto2).wait(
                                e, sto_out_base + 16 * ((oidx - 2) // 2 + 1)
                                if oidx % 2 == 0 else 16 * ((oidx - 2) // 2 + 1))
                        e.tensor_tensor(out_t[:, 1024 * (tt % 2):
                                              1024 * (tt % 2) + 1024],
                                        accb[:, 1024 * tt:1024 * (tt + 1)],
                                        condt[:, 2048:3072], op=ALU.mult)
                        vo = e.tensor_tensor(
                            out_t[:, 1024 * (tt % 2):1024 * (tt % 2) + 1024],
                            out_t[:, 1024 * (tt % 2):1024 * (tt % 2) + 1024],
                            buf2[:, 1024 * t:1024 * (t + 1)], op=ALU.add)
                        sV2S.inc(vo)

            @block.sync
            def _(e):
                for t in range(NT):
                    sV2S.wait(e, bOUT + t + 1)
                    sm = sto if t % 2 == 0 else sto2
                    sm.dinc(e.dma_start(
                        out=y_out[128 * t:128 * (t + 1), :],
                        in_=out_t[:, 1024 * (t % 2):1024 * (t % 2) + 1024]))
                sto.wait(e)
                sto2.wait(e)


    return nc, dbg_aps


def _blk(w):
    K, C = w.shape
    return np.ascontiguousarray(
        w.reshape(K // 128, 128, C).transpose(1, 0, 2).reshape(128, (K // 128) * C))


def _host_consts():
    ident = np.eye(128, dtype=np.float32)
    posc = (np.arange(16)[None, :] * 128 + np.arange(128)[:, None]).astype(np.float32)
    gam = np.asarray(RIEMANN, dtype=np.float32)
    invf = (1.0 / (gam / gam[0])).astype(np.float32)
    ifr = np.tile(invf, 16)[None, :].astype(np.float32)
    masks = np.zeros((128, 2048), dtype=np.float32)
    r = np.arange(128)[:, None]
    jj = np.arange(512)[None, :]
    for d in range(4):
        masks[:, 512 * d:512 * (d + 1)] = np.where(jj <= 128 * d + r, 0.0, -1.0e30)
    return ident, posc, ifr, masks


_CACHE = {}


def _get_built(dbg=(), stop_after=99):
    key = (tuple(sorted(dbg)), stop_after)
    if key not in _CACHE:
        _CACHE[key] = build_program(dbg, stop_after)
    return _CACHE[key]


def kernel(L0_attn_x, L0_attn_w_qkv, L0_attn_w_o, L0_attn_log_scale,
           L0_ffn_router_w, L0_ffn_experts_bias, L0_ffn_w1, L0_ffn_w3, L0_ffn_w2,
           L0_ffn_sw1, L0_ffn_sw3, L0_ffn_sw2, L0_c, L0_wmod, _dbg=(),
           _stop_after=99, _ncores=8):
    f32 = lambda a: np.ascontiguousarray(np.asarray(a, dtype=np.float32))
    x = f32(L0_attn_x)
    ident, posc, ifr, masks = _host_consts()
    wqkv = _blk(f32(L0_attn_w_qkv))
    wo = _blk(f32(L0_attn_w_o))
    wmod = _blk(f32(L0_wmod))
    rwt = _blk(f32(L0_ffn_router_w).T)
    w1 = np.stack([_blk(w) for w in
                   np.concatenate([f32(L0_ffn_sw1), f32(L0_ffn_w1)], axis=0)])
    w3 = np.stack([_blk(w) for w in
                   np.concatenate([f32(L0_ffn_sw3), f32(L0_ffn_w3)], axis=0)])
    w2 = np.stack([_blk(w) for w in
                   np.concatenate([f32(L0_ffn_sw2), f32(L0_ffn_w2)], axis=0)])
    ls = f32(L0_attn_log_scale).reshape(1, 16)
    ebias = f32(L0_ffn_experts_bias).reshape(1, 8)
    cvec = f32(L0_c).reshape(8, 8, 128)

    nc, dbg_aps = _get_built(_dbg, _stop_after)
    in_maps = []
    for b in range(_ncores):
        in_maps.append({
            "x": x[b], "wqkv": wqkv, "wo": wo, "ls": ls, "rwt": rwt,
            "ebias": ebias, "w1": w1, "w3": w3, "w2": w2, "cvec": cvec[b],
            "wmod": wmod, "ident": ident, "posc": posc, "ifr": ifr,
            "masks": masks,
        })
    res = run_bass_kernel_spmd(nc, in_maps, core_ids=list(range(_ncores)))
    out = np.stack([np.asarray(res.results[b]["y"]) for b in range(_ncores)]
                   + [np.zeros((2048, 1024), np.float32)] * (8 - _ncores), axis=0)
    if _dbg:
        dbgres = [{k: np.asarray(res.results[b]["dbg_" + k]) for k in _dbg}
                  for b in range(_ncores)]
        return out.astype(np.float32), dbgres
    return out.astype(np.float32)
